# version 27
# baseline (speedup 1.0000x reference)
import numpy as np
import ml_dtypes
from contextlib import ExitStack
import concourse.bass as bass
import concourse.mybir as mybir
from concourse.bass_utils import run_bass_kernel_spmd
from concourse.alu_op_type import AluOpType as OP

F32 = mybir.dt.float32
BF16 = mybir.dt.bfloat16
I32 = mybir.dt.int32
AF = mybir.ActivationFunctionType
AX = mybir.AxisListType

NCORES = 8
N = 4096
D = 1024
TT = 512
NTILES = N // TT
NSUB = TT // 128

BF = ml_dtypes.bfloat16


def vec_pm(v, p=128):
    return np.ascontiguousarray(np.asarray(v).reshape(-1, p).T)


def eye_bf16():
    return np.eye(128, dtype=BF)


def tok_in(hbm_ap, r0, tt=TT):
    return hbm_ap[r0:r0 + tt, :].rearrange("(s p) d -> p s d", p=128)


def tok_sb(sb_ap, nsub, d):
    return sb_ap.rearrange("p (s d) -> p s d", s=nsub)


class Prog:
    def __init__(self):
        self.nc = bass.Bass("TRN2", target_bir_lowering=False)
        self.stack = ExitStack()
        self.cnt = {}
        self.H = {}
        self._scopes = []
        self._uid = 0
        self._streams = {}

    def _nm(self, name):
        self._uid += 1
        return f"{name}__{self._uid}"

    def _tgt(self):
        return self._scopes[-1] if self._scopes else self.stack

    def push_scope(self):
        self._scopes.append(ExitStack())

    def pop_scope(self):
        self._scopes.pop().close()

    def dram(self, name, shape, dt, kind=None):
        if name in self.H:
            return self.H[name]
        if kind:
            return self.nc.dram_tensor(name, shape, dt, kind=kind).ap()
        return self.nc.dram_tensor(name, shape, dt).ap()

    def strip_bufs(self):
        if not hasattr(self, "_strips"):
            self._strips = [
                self.stack.enter_context(
                    self.nc.sbuf_tensor(self._nm(f"gstrip{i}"), [128, 512], BF16))
                for i in range(6)]
        return self._strips

    def stream(self, name, nparity=2):
        if name not in self._streams:
            self._streams[name] = DmaStream(self, name, nparity)
        st = self._streams[name]
        assert st.np == nparity
        return st

    def sb(self, name, shape, dt):
        return self._tgt().enter_context(
            self.nc.sbuf_tensor(self._nm(name), shape, dt))

    def scope(self):
        return ExitStack()

    def sb_in(self, scope, name, shape, dt):
        return scope.enter_context(self.nc.sbuf_tensor(self._nm(name), shape, dt))

    def psum(self, name, shape, dt):
        return self._tgt().enter_context(
            self.nc.psum_tensor(self._nm(name), shape, dt))

    def sem(self, name):
        if name in self.cnt:
            return self.cnt[name][0]
        s = self.stack.enter_context(self.nc.semaphore(name))
        self.cnt[name] = [s, 0]
        return s

    def sems(self, names):
        for n in names:
            self.sem(n)

    def s(self, name):
        return self.cnt[name][0]

    def bump(self, name, n=1):
        self.cnt[name][1] += n
        return self.cnt[name][1]

    def val(self, name):
        return self.cnt[name][1]

    def close(self):
        self.stack.close()
        return self.nc


def ap3(t, offset, dims):
    F = t.shape[1]
    return bass.AP(t, offset, [[F, 128]] + dims)


class DmaStream:

    def __init__(self, p, name, nparity=2):
        self.sems = [p.stack.enter_context(p.nc.semaphore(f"{name}{i}"))
                     for i in range(nparity)]
        self.counts = [0] * nparity
        self.np = nparity
        self.i = 0

    def issue(self, eng, out, in_, **kw):
        par = self.i % self.np
        self.i += 1
        self.counts[par] += 16
        eng.dma_start(out=out, in_=in_, **kw).then_inc(self.sems[par], 16)
        return (par, self.counts[par])

    def tok(self, idx):
        par = idx % self.np
        n16 = (idx // self.np + 1) * 16
        return (par, n16)

    def wait(self, eng, token):
        eng.wait_ge(self.sems[token[0]], token[1])

    def prefix_wait(self, eng, M):
        for pr in range(self.np):
            n = len(range(pr, M, self.np))
            if n:
                eng.wait_ge(self.sems[pr], 16 * n)


def emit_ln(p, block, x_tok, xn_bf, stats, eps_tile,
            nsub=NSUB, fw=D, wait_sem=None, wait_val=0, done_inc=None):
    n = nsub
    nh = fw // 512
    assert nh * 6 <= 12
    mv0 = 12 * n
    rs0 = 14 * n
    lns = p.s("lns")
    b0 = p.val("lns")

    @block.vector
    def _(vector):
        if wait_sem is not None:
            vector.wait_ge(wait_sem, wait_val)
        for s in range(n):
            for h in range(nh):
                vector.bn_stats(stats[:, 12 * s + 6 * h:12 * s + 6 * h + 6],
                                x_tok[:, s * fw + h * 512:s * fw + (h + 1) * 512])
        vector.drain()
        last = None
        for s in range(n):
            last = vector.bn_aggr(stats[:, mv0 + 2 * s:mv0 + 2 * s + 2],
                                  stats[:, 12 * s:12 * s + 6 * nh])
        last.then_inc(lns, 1)
        vector.wait_ge(lns, b0 + 2)
        vector.reciprocal(stats[:, rs0:rs0 + n],
                          ap3(stats, mv0 + 1, [[2, n]]))
        vector.drain()
        last = None
        for s in range(n):
            last = vector.tensor_scalar(xn_bf[:, s * fw:(s + 1) * fw],
                                        x_tok[:, s * fw:(s + 1) * fw],
                                        stats[:, mv0 + 2 * s:mv0 + 2 * s + 1],
                                        stats[:, rs0 + s:rs0 + s + 1],
                                        OP.subtract, OP.mult)
        if done_inc is not None:
            last.then_inc(done_inc[0], done_inc[1])

    @block.scalar
    def _(scalar):
        scalar.wait_ge(lns, b0 + 1)
        scalar.activation(ap3(stats, mv0 + 1, [[2, n]]),
                          ap3(stats, mv0 + 1, [[2, n]]),
                          AF.Sqrt, bias=eps_tile[:, 0:1]).then_inc(lns, 1)

    p.bump("lns", 2)


def emit_tok2fm(p, block, src_tok_bf, fm_out, ident, ps_tr, nchunks,
                nsub=NSUB, affine=None, fm_chunk_w=TT, wait=None):
    F = nchunks * 128
    T = nsub * 128
    spe, sdve = p.s("pe"), p.s("dve")
    pe0, dve0 = p.val("pe"), p.val("dve")

    @block.tensor
    def _(tensor):
        if wait is not None:
            tensor.wait_ge(wait[0], wait[1])
        for c in range(nchunks):
            if c >= 2:
                tensor.wait_ge(sdve, dve0 + c - 1)
            mm = None
            for s in range(nsub):
                mm = tensor.transpose(ps_tr[c % 2][:, s * 128:(s + 1) * 128],
                                      src_tok_bf[:, s * F + c * 128:s * F + (c + 1) * 128],
                                      ident[:])
            mm.then_inc(spe, 1)

    @block.scalar
    def _(scalar):
        for c in range(nchunks):
            scalar.wait_ge(spe, pe0 + c + 1)
            dst = fm_out[:, c * fm_chunk_w:c * fm_chunk_w + T]
            if affine is None:
                scalar.copy(dst, ps_tr[c % 2][:, 0:T]).then_inc(sdve, 1)
            else:
                w, b, col0 = affine
                scalar.activation(dst, ps_tr[c % 2][:, 0:T], AF.Identity,
                                  scale=w[:, col0 + c:col0 + c + 1],
                                  bias=b[:, col0 + c:col0 + c + 1]
                                  ).then_inc(sdve, 1)

    p.bump("pe", nchunks)
    p.bump("dve", nchunks)


def emit_weight_prep(p, block, jobs, scope, cap=2048):
    maxc = min(cap, max(j["src"].shape[1] for j in jobs))
    wtmp = [p.sb_in(scope, f"wtmp{i}", [128, maxc], F32) for i in range(2)]
    wbf = [p.sb_in(scope, f"wbf{i}", [128, maxc], BF16) for i in range(2)]
    sprep = p.s("prep")
    prep0 = p.val("prep")
    lds = p.stream("wprep_ld")
    sts = p.stream("wprep_st")
    lb0, sb0 = lds.i, sts.i

    chunks = []
    for j in jobs:
        R, C = j["src"].shape
        for k in range(R // 128):
            for c0 in range(0, C, maxc):
                chunks.append((j, k, c0, min(maxc, C - c0)))
    n_hbm = 0
    hbm_idx = []
    for (j, k, c0, w) in chunks:
        hbm_idx.append(n_hbm if "dst_hbm" in j else -1)
        if "dst_hbm" in j:
            n_hbm += 1

    @block.sync
    def _(sync):
        for i, (j, k, c0, w) in enumerate(chunks):
            if i >= 2:
                sync.wait_ge(sprep, prep0 + i - 1)
            lds.issue(sync, wtmp[i % 2][:, 0:w],
                      j["src"][k * 128:(k + 1) * 128, c0:c0 + w])

    @block.vector
    def _(vector):
        for i, (j, k, c0, w) in enumerate(chunks):
            C = j["src"].shape[1]
            lds.wait(vector, lds.tok(lb0 + i))
            if "dst_sb" in j:
                vector.tensor_copy(j["dst_sb"][:, k * C + c0:k * C + c0 + w],
                                   wtmp[i % 2][:, 0:w]).then_inc(sprep, 1)
            else:
                if hbm_idx[i] >= 2:
                    sts.wait(vector, sts.tok(sb0 + hbm_idx[i] - 2))
                vector.tensor_copy(wbf[hbm_idx[i] % 2][:, 0:w],
                                   wtmp[i % 2][:, 0:w]).then_inc(sprep, 1)

    @block.gpsimd
    def _(gpsimd):
        for i, (j, k, c0, w) in enumerate(chunks):
            if hbm_idx[i] >= 0:
                gpsimd.wait_ge(sprep, prep0 + i + 1)
                sts.issue(gpsimd, j["dst_hbm"][k * 128:(k + 1) * 128, c0:c0 + w],
                          wbf[hbm_idx[i] % 2][:, 0:w])

    for i in range(len(chunks)):
        p.bump("prep", 1)
    return sts, n_hbm


def emit_param_loads(p, block, pairs):
    sld = p.s("ld")

    @block.sync
    def _(sync):
        for t, h in pairs:
            sync.dma_start(out=t[:], in_=h).then_inc(sld, 16)

    for _ in pairs:
        p.bump("ld", 16)
    total = p.val("ld")

    @block.vector
    def _(vector):
        vector.wait_ge(sld, total)

    return total


def build_lb(p=None, H=None):
    standalone = p is None
    if standalone:
        p = Prog()
        p.strip_bufs()
    p.H = H or {}
    p.push_scope()
    nc = p.nc
    MLP = 4096
    xa_h = p.dram("xa", [N, D], F32, "ExternalInput")
    xo_h = p.dram("xo", [N, D], F32, "ExternalInput")
    fcw_h = p.dram("fcw", [D, MLP], F32, "ExternalInput")
    fcb_h = p.dram("fcb", [128, MLP // 128], F32, "ExternalInput")
    pjw_h = p.dram("pjw", [MLP, D], F32, "ExternalInput")
    pjb_h = p.dram("pjb", [128, D // 128], F32, "ExternalInput")
    lnw_h = p.dram("lnw", [128, D // 128], F32, "ExternalInput")
    lnb_h = p.dram("lnb", [128, D // 128], F32, "ExternalInput")
    id_h = p.dram("ident", [128, 128], BF16, "ExternalInput")
    y_h = p.dram("y", [N, D], F32, "ExternalOutput")
    fcw_s = p.dram("fcw_bf", [D, MLP], BF16)

    stats = p.sb("stats", [128, 16 * NSUB], F32)
    eps = p.sb("eps", [128, 1], F32)
    ident = p.sb("id_sb", [128, 128], BF16)
    fcb = p.sb("fcb_sb", [128, MLP // 128], F32)
    pjb = p.sb("pjb_sb", [128, D // 128], F32)
    lnw = p.sb("lnw_sb", [128, D // 128], F32)
    lnb = p.sb("lnb_sb", [128, D // 128], F32)
    pjw = p.sb("pjw_sb", [128, 32 * D], BF16)

    p.sems(["ld", "wld", "st", "prep", "pe", "dve", "act", "lns", "pefree", "bank", "mx", "vd"])
    sld, swld, sst = p.s("ld"), p.s("wld"), p.s("st")
    spe, sdve, sact, spefree = p.s("pe"), p.s("dve"), p.s("act"), p.s("pefree")

    with nc.Block() as block:
        emit_param_loads(p, block, [(ident, id_h), (fcb, fcb_h), (pjb, pjb_h),
                                    (lnw, lnw_h), (lnb, lnb_h)])

        @block.vector
        def _(vector):
            vector.memset(eps[:], 1e-5)

    with p.scope() as prep_scope:
        with nc.Block() as block:
            sts, _nh = emit_weight_prep(p, block, [
                {"src": pjw_h, "dst_sb": pjw},
                {"src": fcw_h, "dst_hbm": fcw_s},
            ], prep_scope)
        with nc.Block() as block:
            @block.sync
            def _(sync):
                for par in range(sts.np):
                    if sts.counts[par]:
                        sync.wait_ge(sts.sems[par], sts.counts[par])

    xa = p.sb("xa_sb", [128, NSUB * D], F32)
    xo = p.sb("xo_sb", [128, NSUB * D], F32)
    xsum = p.sb("xsum_sb", [128, NSUB * D], F32)
    out = p.sb("out_sb", [128, NSUB * D], F32)
    xn_bf = p.sb("xn_bf", [128, NSUB * D], BF16)
    xn_fm = p.sb("xn_fm", [128, 8 * TT], BF16)
    h_fm = p.sb("h_fm", [128, 32 * TT], BF16)
    y_fm = p.sb("y_fm", [128, 8 * TT], BF16)
    strip = p.strip_bufs()
    strips = p.stream("strips", 12)

    ps = [p.psum(f"ps{i}", [128, 512], F32) for i in range(4)]
    ps_tr = [p.psum(f"ps_tr{i}", [128, 1024], BF16) for i in range(2)]

    for i in range(NTILES):
        r0 = i * TT
        with nc.Block() as block:
            @block.sync
            def _(sync):
                sync.dma_start(out=tok_sb(xa[:], NSUB, D),
                               in_=tok_in(xa_h, r0)).then_inc(sld, 16)
                sync.dma_start(out=tok_sb(xo[:], NSUB, D),
                               in_=tok_in(xo_h, r0)).then_inc(sld, 16)

            p.bump("ld", 32)
            emit_ln(p, block, xa, xn_bf, stats, eps,
                    wait_sem=sld, wait_val=p.val("ld"))

            @block.scalar
            def _(scalar):
                pass

            @block.gpsimd
            def _(gpsimd):
                gpsimd.wait_ge(sld, p.val("ld"))
                gpsimd.tensor_tensor(xsum[:], xa[:], xo[:], OP.add)

        with nc.Block() as block:
            emit_tok2fm(p, block, xn_bf, xn_fm, ident, ps_tr, nchunks=8,
                        affine=(lnw, lnb, 0))

        with nc.Block() as block:
            evacs = []
            for m in range(32):
                def f(eng, pst, m=m):
                    return eng.activation(h_fm[:, m * TT:(m + 1) * TT], pst,
                                          AF.Gelu, bias=fcb[:, m:m + 1])
                evacs.append(f)
            emit_mm_stream(p, block, fcw_s, xn_fm, 8, 32, strip, strips, ps,
                           "scalar", evacs)

        with nc.Block() as block:
            pe0, dve0, st0 = p.val("pe"), p.val("dve"), p.val("st")

            @block.tensor
            def _(tensor):
                for g in range(2):
                    if g == 1:
                        tensor.wait_ge(sdve, dve0 + 4)
                    mm = None
                    for k in range(32):
                        for mi in range(4):
                            m = g * 4 + mi
                            mm = tensor.matmul(
                                ps[mi][:],
                                pjw[:, k * D + m * 128:k * D + (m + 1) * 128],
                                h_fm[:, k * TT:(k + 1) * TT],
                                start=(k == 0), stop=(k == 31))
                    mm.then_inc(spe, 1)
                for s in range(NSUB):
                    tensor.wait_ge(sdve, dve0 + 8 + max(0, s - 1))
                    mm = None
                    for c in range(8):
                        mm = tensor.transpose(
                            ps_tr[s % 2][:, c * 128:(c + 1) * 128],
                            y_fm[:, c * TT + s * 128:c * TT + (s + 1) * 128], ident[:])
                    mm.then_inc(spe, 1)

            @block.vector
            def _(vector):
                for g in range(2):
                    vector.wait_ge(spe, pe0 + g + 1)
                    for mi in range(4):
                        m = g * 4 + mi
                        vector.tensor_scalar(y_fm[:, m * TT:(m + 1) * TT], ps[mi][:],
                                             pjb[:, m:m + 1], None,
                                             OP.add).then_inc(sdve, 1)
                vector.wait_ge(sst, st0)
                for s in range(NSUB):
                    vector.wait_ge(spe, pe0 + 2 + s + 1)
                    vector.tensor_tensor(out[:, s * D:(s + 1) * D],
                                         xsum[:, s * D:(s + 1) * D],
                                         ps_tr[s % 2][:, 0:D],
                                         OP.add).then_inc(sdve, 1)

            @block.sync
            def _(sync):
                sync.wait_ge(sdve, dve0 + 8 + NSUB)
                sync.dma_start(out=tok_in(y_h, r0),
                               in_=tok_sb(out[:], NSUB, D)).then_inc(sst, 16)

            p.bump("pe", 2 + NSUB)
            p.bump("dve", 8 + NSUB)
            p.bump("st", 16)

    with nc.Block() as block:
        @block.sync
        def _(sync):
            sync.wait_ge(sst, p.val("st"))

    p.pop_scope()
    if standalone:
        return p.close()


def lb_inmaps(xa_full, xo_full, ins):
    com = {
        "fcw": np.ascontiguousarray(ins["L0_attn_fc_w"]),
        "fcb": vec_pm(ins["L0_attn_fc_b"]),
        "pjw": np.ascontiguousarray(ins["L0_attn_proj_w"]),
        "pjb": vec_pm(ins["L0_attn_proj_b"]),
        "lnw": vec_pm(ins["L0_attn_ln2_w"]),
        "lnb": vec_pm(ins["L0_attn_ln2_b"]),
        "ident": eye_bf16(),
    }
    return [dict(com, xa=np.ascontiguousarray(xa_full[c]),
                 xo=np.ascontiguousarray(xo_full[c])) for c in range(NCORES)]


def emit_mm_stream(p, block, w_s, rhs_fm, nk, nm, strip, strips, ps, evac_engine,
                   evac_fns, pe_sem="bank", ev_sem="mx", rhs_tt=TT):
    spe, sev = p.s(pe_sem), p.s(ev_sem)
    pe0, ev0 = p.val(pe_sem), p.val(ev_sem)
    st_e = p.stream("strips_e", 6)
    st_o = p.stream("strips_o", 6)
    je0, jo0 = st_e.i, st_o.i
    nmsc = nm // 2
    nbuf = 2 * len(strip)
    if not hasattr(p, "_strip_ledger"):
        p._strip_ledger = []
    led = p._strip_ledger
    gj0 = len(led)
    for j in range(nmsc * nk):
        led.append(pe0 + j + 1)
    if callable(rhs_fm):
        rhs_of = rhs_fm
    else:
        rhs_of = lambda k: rhs_fm[:, k * rhs_tt:(k + 1) * rhs_tt]

    def slot(j):
        b = (gj0 + j) % nbuf
        return strip[b // 2][:, (b % 2) * 256:(b % 2) * 256 + 256]

    def issue_half(eng, par, stm):
        for j in range(nmsc * nk):
            if j % 2 != par:
                continue
            msc, k = divmod(j, nk)
            if gj0 + j >= nbuf:
                eng.wait_ge(spe, led[gj0 + j - nbuf])
            stm.issue(eng, slot(j),
                      w_s[k * 128:(k + 1) * 128, msc * 256:(msc + 1) * 256])

    @block.sync
    def _(sync):
        issue_half(sync, 0, st_e)

    @block.gpsimd
    def _(gpsimd):
        issue_half(gpsimd, 1, st_o)

    @block.tensor
    def _(tensor):
        for msc in range(nmsc):
            b0, b1 = (0, 1) if msc % 2 == 0 else (2, 3)
            if msc >= 2:
                tensor.wait_ge(sev, ev0 + 2 * (msc - 1))
            elif ev0 > 0:
                tensor.wait_ge(sev, ev0)
            for k in range(nk):
                j = msc * nk + k
                if j % 2 == 0:
                    st_e.wait(tensor, st_e.tok(je0 + j // 2))
                else:
                    st_o.wait(tensor, st_o.tok(jo0 + j // 2))
                sb_ = slot(j)
                st_, sp_ = (k == 0), (k == nk - 1)
                tensor.matmul(ps[b1][:, 0:rhs_tt], sb_[:, 128:256],
                              rhs_of(k), start=st_, stop=sp_)
                tensor.matmul(ps[b0][:, 0:rhs_tt], sb_[:, 0:128],
                              rhs_of(k), start=st_, stop=sp_).then_inc(spe, 1)

    def emit_evacs(eng):
        for msc in range(nmsc):
            b0, b1 = (0, 1) if msc % 2 == 0 else (2, 3)
            eng.wait_ge(spe, pe0 + nk * (msc + 1))
            evac_fns[2 * msc](eng, ps[b0][:, 0:rhs_tt])
            evac_fns[2 * msc + 1](eng, ps[b1][:, 0:rhs_tt]).then_inc(sev, 2)

    if evac_engine == "scalar":
        @block.scalar
        def _(scalar):
            emit_evacs(scalar)
    else:
        @block.vector
        def _(vector):
            emit_evacs(vector)

    p.bump(pe_sem, nmsc * nk)
    p.bump(ev_sem, nmsc * 2)


def vfence(p, eng, instr):
    svd = p.s("vd")
    instr.then_inc(svd, 1)
    eng.wait_ge(svd, p.bump("vd"))


MAGIC = 12582912.0


def emit_quant_tok(p, block, vector, x_tok, out_bf, qst, i32buf, nsub=NSUB,
                   fw=D, qmax=127.0, lo=-128.0, hi=127.0):
    n = nsub
    for s in range(n):
        vector.reduce_max(qst[:, s:s + 1], x_tok[:, s * fw:(s + 1) * fw],
                          axis=AX.X, apply_absolute_value=True)
    vector.drain()
    vfence(p, vector, vector.tensor_scalar(qst[:, 0:n], qst[:, 0:n], 1e-5,
                                           None, OP.max))
    vfence(p, vector, vector.reciprocal(qst[:, n:2 * n], qst[:, 0:n]))
    vector.tensor_scalar(qst[:, n:2 * n], qst[:, n:2 * n], qmax, None, OP.mult)
    vfence(p, vector,
           vector.tensor_scalar(qst[:, 2 * n:3 * n], qst[:, 0:n], 1.0 / qmax,
                                None, OP.mult))
    for s in range(n):
        vfence(p, vector,
               vector.tensor_scalar(i32buf[:], x_tok[:, s * fw:(s + 1) * fw],
                                    qst[:, n + s:n + s + 1], MAGIC,
                                    OP.mult, OP.add))
        vfence(p, vector,
               vector.tensor_scalar(out_bf[:, s * fw:(s + 1) * fw], i32buf[:],
                                    MAGIC, qst[:, 2 * n + s:2 * n + s + 1],
                                    OP.subtract, OP.mult))


def build_lc1(p=None, H=None):
    standalone = p is None
    if standalone:
        p = Prog()
        p.strip_bufs()
    p.H = H or {}
    p.push_scope()
    nc = p.nc
    MLP = 4096
    x_h = p.dram("x1", [N, D], F32, "ExternalInput")
    w1t_h = p.dram("w1t", [D, MLP], F32, "ExternalInput")
    id_h = p.dram("ident", [128, 128], BF16, "ExternalInput")
    h_out = p.dram("hs", [MLP, N], BF16, "ExternalOutput")
    w1q_s = p.dram("w1q_bf", [D, MLP], BF16)

    stats = p.sb("stats", [128, 16 * NSUB], F32)
    qst = p.sb("qst", [128, 4 * NSUB], F32)
    eps = p.sb("eps", [128, 1], F32)
    ident = p.sb("id_sb", [128, 128], BF16)
    wsc = p.sb("wsc", [128, 4], F32)

    p.sems(["ld", "st", "prep", "pe", "dve", "act", "lns", "bank", "mx", "vd"])
    sld, sst = p.s("ld"), p.s("st")
    spe, sdve, sact = p.s("pe"), p.s("dve"), p.s("act")

    with nc.Block() as block:
        emit_param_loads(p, block, [(ident, id_h)])

        @block.vector
        def _(vector):
            vector.memset(eps[:], 1e-5)
            vector.memset(wsc[:], 0.0)

    emit_wquant_global(p, w1t_h, w1q_s, 8, MLP, "w1")

    xbuf = p.sb("xbuf", [128, NSUB * D], F32)
    xln = p.sb("xln", [128, NSUB * D], F32)
    xq = p.sb("xq", [128, NSUB * D], BF16)
    xq_fm = p.sb("xq_fm", [128, 8 * TT], BF16)
    i32b = p.sb("i32b", [128, D], F32)
    hbuf = p.sb("hbuf", [128, 32 * TT], BF16)
    strip = p.strip_bufs()
    strips = p.stream("strips", 12)
    hsts = p.stream("aux_st")
    ps = [p.psum(f"ps{i}", [128, 512], F32) for i in range(4)]
    ps_t = [p.psum(f"ps_t{i}", [128, 1024], BF16) for i in range(2)]

    for i in range(NTILES):
        r0 = i * TT
        with nc.Block() as block:
            @block.sync
            def _(sync):
                sync.dma_start(out=tok_sb(xbuf[:], NSUB, D),
                               in_=tok_in(x_h, r0)).then_inc(sld, 16)

            p.bump("ld", 16)
            emit_ln(p, block, xbuf, xln, stats, eps,
                    wait_sem=sld, wait_val=p.val("ld"), done_inc=(sdve, 1))
            qd = p.bump("dve")

            @block.vector
            def _(vector):
                vector.wait_ge(sdve, qd)
                emit_quant_tok(p, block, vector, xln, xq, qst, i32b)

        with nc.Block() as block:
            emit_tok2fm(p, block, xq, xq_fm, ident, ps_t, nchunks=8)

        if i >= 1:
            with nc.Block() as block:
                @block.scalar
                def _(scalar):
                    for pr in range(hsts.np):
                        if hsts.counts[pr]:
                            scalar.wait_ge(hsts.sems[pr], hsts.counts[pr])

        with nc.Block() as block:
            evacs = []
            for m in range(32):
                def f(eng, pst, m=m):
                    return eng.copy(hbuf[:, m * TT:(m + 1) * TT], pst)
                evacs.append(f)
            emit_mm_stream(p, block, w1q_s, xq_fm, 8, 32, strip, strips, ps,
                           "scalar", evacs)
            mx_done = p.val("mx")

            @block.gpsimd
            def _(gpsimd):
                for c in range(32):
                    gpsimd.wait_ge(p.s("mx"), mx_done - 32 + 2 * (c // 2) + 2)
                    hsts.issue(gpsimd,
                               h_out[c * 128:(c + 1) * 128, r0:r0 + TT],
                               hbuf[:, c * TT:(c + 1) * TT])

    with nc.Block() as block:
        @block.sync
        def _(sync):
            for pr in range(hsts.np):
                if hsts.counts[pr]:
                    sync.wait_ge(hsts.sems[pr], hsts.counts[pr])

    p.pop_scope()
    if standalone:
        return p.close()


def emit_wquant_global(p, w_h, w_s, nk, C, name):
    nc = p.nc
    spe, sdve = p.s("pe"), p.s("dve")
    with p.scope() as scope:
        wtmp = [p.sb_in(scope, f"{name}wt{i}", [128, C], F32) for i in range(2)]
        wqb = [p.sb_in(scope, f"{name}wqb{i}", [128, C], BF16) for i in range(2)]
        wi32 = p.sb_in(scope, f"{name}wi32", [128, C], F32)
        acc = p.sb_in(scope, f"{name}acc", [128, 2], F32)
        sca = p.sb_in(scope, f"{name}sca", [128, 4], F32)
        ones1 = p.sb_in(scope, f"{name}o1", [128, 1], F32)
        onesr = p.sb_in(scope, f"{name}or", [1, 128], F32)
        ps1 = scope.enter_context(nc.psum_tensor(p._nm(f"{name}ps1"), [1, 8], F32))
        psb = scope.enter_context(nc.psum_tensor(p._nm(f"{name}psb"), [128, 8], F32))
        lds = p.stream("wqgld")
        sts = p.stream("wqgst")
        lb0, sb0 = lds.i, sts.i

        with nc.Block() as block:
            dve0 = p.val("dve")

            @block.sync
            def _(sync):
                for k in range(nk):
                    if k >= 2:
                        sync.wait_ge(sdve, dve0 + k - 1)
                    lds.issue(sync, wtmp[k % 2][:], w_h[k * 128:(k + 1) * 128, :])

            @block.vector
            def _(vector):
                vector.memset(ones1[:], 1.0)
                vector.memset(onesr[:], 1.0)
                vector.memset(acc[:], 0.0)
                vector.drain()
                for k in range(nk):
                    lds.wait(vector, lds.tok(lb0 + k))
                    vector.reduce_sum(sca[:, 0:1], wtmp[k % 2][:], axis=AX.X,
                                      apply_absolute_value=True)
                    vector.drain()
                    vector.tensor_tensor(acc[:, 0:1], acc[:, 0:1], sca[:, 0:1],
                                         OP.add).then_inc(sdve, 1)
                    vector.drain()
            p.bump("dve", nk)

        with nc.Block() as block:
            pe0, dve0 = p.val("pe"), p.val("dve")

            @block.tensor
            def _(tensor):
                tensor.matmul(ps1[:, 0:1], ones1[:], acc[:, 0:1],
                              start=True, stop=True).then_inc(spe, 1)

            @block.vector
            def _(vector):
                vector.wait_ge(spe, pe0 + 1)
                vector.tensor_scalar(sca[:1, 0:1], ps1[:, 0:1],
                                     1.0 / (nk * 128 * C), 1e-5, OP.mult, OP.max)
                vector.drain()
                vector.reciprocal(sca[:1, 1:2], sca[:1, 0:1])
                vector.drain()
            p.bump("pe", 1)

        with nc.Block() as block:
            pe0 = p.val("pe")

            @block.tensor
            def _(tensor):
                tensor.matmul(psb[:, 0:2], onesr[:], sca[0:1, 0:2],
                              start=True, stop=True).then_inc(spe, 1)

            @block.vector
            def _(vector):
                vector.wait_ge(spe, pe0 + 1)
                vector.tensor_copy(sca[:, 2:4], psb[:, 0:2])
            p.bump("pe", 1)

        with nc.Block() as block:
            dve0 = p.val("dve")

            @block.sync
            def _(sync):
                for k in range(nk):
                    if k >= 2:
                        sync.wait_ge(sdve, dve0 + k - 1)
                    lds.issue(sync, wtmp[k % 2][:], w_h[k * 128:(k + 1) * 128, :])

            @block.vector
            def _(vector):
                for k in range(nk):
                    lds.wait(vector, lds.tok(lb0 + nk + k))
                    if k >= 2:
                        sts.wait(vector, sts.tok(sb0 + k - 2))
                    vector.tensor_scalar(wi32[:], wtmp[k % 2][:], sca[:, 3:4],
                                         MAGIC, OP.mult, OP.add)
                    vector.drain()
                    vector.tensor_scalar(wi32[:], wi32[:], MAGIC, -1.0,
                                         OP.subtract, OP.max)
                    vector.drain()
                    vector.tensor_scalar(wqb[k % 2][:], wi32[:], 1.0, sca[:, 2:3],
                                         OP.min, OP.mult).then_inc(sdve, 1)
                    vector.drain()

            @block.gpsimd
            def _(gpsimd):
                for k in range(nk):
                    gpsimd.wait_ge(sdve, dve0 + k + 1)
                    sts.issue(gpsimd, w_s[k * 128:(k + 1) * 128, :], wqb[k % 2][:])
            p.bump("dve", nk)

        with nc.Block() as block:
            @block.sync
            def _(sync):
                for pr in range(sts.np):
                    if sts.counts[pr]:
                        sync.wait_ge(sts.sems[pr], sts.counts[pr])


def build_lc2(p=None, H=None):
    standalone = p is None
    if standalone:
        p = Prog()
        p.strip_bufs()
    p.H = H or {}
    p.push_scope()
    nc = p.nc
    MLP = 4096
    hs_h = p.dram("hs", [MLP, N], BF16, "ExternalInput")
    x_h = p.dram("x1", [N, D], F32, "ExternalInput")
    w2t_h = p.dram("w2t", [MLP, D], F32, "ExternalInput")
    cw_h = p.dram("convw", [128, 96], F32, "ExternalInput")
    cb_h = p.dram("convb", [128, 32], F32, "ExternalInput")
    id_h = p.dram("ident", [128, 128], BF16, "ExternalInput")
    idf_h = p.dram("identf", [128, 128], F32, "ExternalInput")
    x2_h = p.dram("x2", [N, D], F32, "ExternalOutput")
    w2q_s = p.dram("w2q_bf", [MLP, D], BF16)

    qst = p.sb("qst", [128, 4 * NSUB], F32)
    mgk = p.sb("mgk", [128, 1], F32)
    mgn = p.sb("mgn", [128, 1], F32)
    ident = p.sb("id_sb", [128, 128], BF16)
    convw = p.sb("convw_sb", [128, 96], F32)
    convb = p.sb("convb_sb", [128, 32], F32)
    identf = p.sb("idf_sb", [128, 128], F32)
    onesr1 = p.sb("onesr1", [1, 128], F32)

    p.sems(["ld", "st", "prep", "pe", "dve", "act", "lns", "bank", "mx", "vd"])
    sld, sst = p.s("ld"), p.s("st")
    spe, sdve, sact = p.s("pe"), p.s("dve"), p.s("act")

    with nc.Block() as block:
        emit_param_loads(p, block, [(ident, id_h), (convw, cw_h), (convb, cb_h),
                                    (identf, idf_h)])

        @block.vector
        def _(vector):
            vector.memset(onesr1[:], 1.0)
            vector.memset(mgk[:], MAGIC)
            vector.memset(mgn[:], -MAGIC)

    emit_wquant_global(p, w2t_h, w2q_s, 32, D, "w2")

    hh = p.sb("hh", [128, 32 * 514], BF16)
    hsil = p.sb("hsil", [128, 32 * TT], BF16)
    o1 = p.sb("o1", [128, 4 * TT], BF16)
    o2 = p.sb("o2", [128, 4 * TT], BF16)
    o3 = p.sb("o3", [128, 4 * TT], BF16)
    xbuf = p.sb("xbuf", [128, NSUB * D], F32)
    out = p.sb("outb", [128, NSUB * D], F32)
    y_fm = p.sb("y_fm", [128, 8 * TT], BF16)
    i32b = p.sb("i32c", [128, 4 * TT], F32)
    i32b2 = p.sb("i32c2", [128, 4 * TT], F32)
    ib2 = p.sb("ib2", [128, 4 * TT], F32)
    maxab = p.sb("maxab", [128, 8], F32)
    rrow = p.sb("rrow", [1, 1024], F32)
    csc = p.sb("csc", [128, TT], F32)
    cinv = p.sb("cinv", [128, TT], F32)
    strip = p.strip_bufs()
    strips = p.stream("strips", 12)
    hlds = p.stream("aux_ld")
    ps = [p.psum(f"ps{i}", [128, 512], F32) for i in range(4)]
    ps_t = [p.psum(f"ps_t{i}", [128, 1024], BF16) for i in range(2)]
    ps_r = p.psum("ps_r", [128, 1024], F32)

    with nc.Block() as block:
        @block.vector
        def _(vector):
            vector.memset(hh[:], 0.0)

    HW = 514
    for i in range(NTILES):
        r0 = i * TT
        lo = max(0, r0 - 1)
        hi2 = min(N, r0 + TT + 1)
        coff = lo - (r0 - 1)
        w = hi2 - lo
        with nc.Block() as block:
            @block.sync
            def _(sync):
                sync.wait_ge(sst, p.val("st"))
                for c in range(32):
                    hlds.issue(sync, bass.AP(hh, c * HW + coff, [[32 * HW, 128], [1, w]]),
                               hs_h[c * 128:(c + 1) * 128, lo:hi2])
                sync.dma_start(out=tok_sb(xbuf[:], NSUB, D),
                               in_=tok_in(x_h, r0)).then_inc(sld, 16)
            p.bump("ld", 16)

            @block.vector
            def _(vector):
                if i == NTILES - 1:
                    vector.memset(bass.AP(hh, HW - 1, [[32 * HW, 128], [HW, 32]]),
                                  0.0)
                for pr in range(hlds.np):
                    vector.wait_ge(hlds.sems[pr], hlds.counts[pr])
                vector.wait_ge(sld, p.val("ld"))

        with nc.Block() as block:
            dve0, act0 = p.val("dve"), p.val("act")

            @block.scalar
            def _(scalar):
                for g in range(8):
                    if g >= 1:
                        scalar.wait_ge(sdve, dve0 + 2 * (g - 1) + 1)
                    last = None
                    for cc in range(4):
                        c = 4 * g + cc
                        last = scalar.activation(
                            o1[:, cc * TT:(cc + 1) * TT],
                            bass.AP(hh, c * HW, [[32 * HW, 128], [1, TT]]),
                            AF.Identity, scale=convw[:, 3 * c:3 * c + 1],
                            bias=convb[:, c:c + 1])
                    last.then_inc(sact, 1)
                    scalar.drain()
                    scalar.wait_ge(sdve, dve0 + 2 * g + 2)
                    last = None
                    for cc in range(4):
                        c = 4 * g + cc
                        last = scalar.activation(hsil[:, c * TT:(c + 1) * TT],
                                                 o3[:, cc * TT:(cc + 1) * TT],
                                                 AF.Silu)
                    last.then_inc(sact, 1)
                    scalar.drain()

            @block.vector
            def _(vector):
                for g in range(8):
                    vector.wait_ge(sact, act0 + 2 * g + 1)
                    last = None
                    for cc in range(4):
                        c = 4 * g + cc
                        last = vector.scalar_tensor_tensor(
                            o2[:, cc * TT:(cc + 1) * TT],
                            bass.AP(hh, c * HW + 1, [[32 * HW, 128], [1, TT]]),
                            convw[:, 3 * c + 1:3 * c + 2],
                            o1[:, cc * TT:(cc + 1) * TT], OP.mult, OP.add)
                    last.then_inc(sdve, 1)
                    vector.wait_ge(sdve, p.val("dve") + 2 * g + 1)
                    if g >= 1:
                        vector.wait_ge(sact, act0 + 2 * (g - 1) + 2)
                    last = None
                    for cc in range(4):
                        c = 4 * g + cc
                        last = vector.scalar_tensor_tensor(
                            o3[:, cc * TT:(cc + 1) * TT],
                            bass.AP(hh, c * HW + 2, [[32 * HW, 128], [1, TT]]),
                            convw[:, 3 * c + 2:3 * c + 3],
                            o2[:, cc * TT:(cc + 1) * TT], OP.mult, OP.add)
                    last.then_inc(sdve, 1)
            p.bump("dve", 16)
            p.bump("act", 16)

        with nc.Block() as block:
            pe0, dve0 = p.val("pe"), p.val("dve")

            @block.tensor
            def _(tensor):
                t = 0
                for s in range(NSUB):
                    for gg in range(4):
                        if t >= 2:
                            tensor.wait_ge(sdve, dve0 + t - 1)
                        mm = None
                        for cc in range(8):
                            c = gg * 8 + cc
                            mm = tensor.transpose(
                                ps_t[t % 2][:, cc * 128:(cc + 1) * 128],
                                hsil[:, c * TT + s * 128:c * TT + s * 128 + 128],
                                ident[:])
                        mm.then_inc(spe, 1)
                        t += 1

            @block.vector
            def _(vector):
                t = 0
                for s in range(NSUB):
                    for gg in range(4):
                        vector.wait_ge(spe, pe0 + t + 1)
                        vfence(p, vector,
                               vector.reduce_max(maxab[:, 4 + s:5 + s],
                                                 ps_t[t % 2][:], axis=AX.X,
                                                 apply_absolute_value=True))
                        if gg == 0:
                            vector.tensor_copy(
                                maxab[:, s:s + 1],
                                maxab[:, 4 + s:5 + s]).then_inc(sdve, 1)
                        else:
                            vector.tensor_tensor(
                                maxab[:, s:s + 1], maxab[:, s:s + 1],
                                maxab[:, 4 + s:5 + s], OP.max).then_inc(sdve, 1)
                        vector.wait_ge(sdve, dve0 + t + 1)
                        t += 1
                vector.tensor_scalar(maxab[:, 0:4], maxab[:, 0:4], 1e-5, None, OP.max)
                vector.drain()
                vector.reciprocal(maxab[:, 4:8], maxab[:, 0:4])
                vector.drain()
                vector.tensor_scalar(maxab[:, 4:8], maxab[:, 4:8], 127.0, None,
                                     OP.mult)
                vector.tensor_scalar(maxab[:, 0:4], maxab[:, 0:4], 1.0 / 127.0,
                                     None, OP.mult)
            p.bump("pe", 16)
            p.bump("dve", 16)

        with nc.Block() as block:
            pe0, dve0 = p.val("pe"), p.val("dve")

            @block.tensor
            def _(tensor):
                mm = None
                for s in range(NSUB):
                    mm = tensor.transpose(ps_r[0:1, s * 128:(s + 1) * 128],
                                          maxab[:, 4 + s:5 + s], identf[:])
                for s in range(NSUB):
                    mm = tensor.transpose(ps_r[0:1, 512 + s * 128:512 + (s + 1) * 128],
                                          maxab[:, s:s + 1], identf[:])
                mm.then_inc(spe, 1)

            @block.vector
            def _(vector):
                vector.wait_ge(spe, pe0 + 1)
                vector.tensor_copy(rrow[0:1, :], ps_r[0:1, 0:1024])
            p.bump("pe", 1)

        with nc.Block() as block:
            pe0 = p.val("pe")

            @block.tensor
            def _(tensor):
                tensor.matmul(ps_r[:, 0:512], onesr1[:], rrow[0:1, 0:512],
                              start=True, stop=True)
                tensor.matmul(ps_r[:, 512:1024], onesr1[:], rrow[0:1, 512:1024],
                              start=True, stop=True).then_inc(spe, 1)

            @block.vector
            def _(vector):
                vector.wait_ge(spe, pe0 + 1)
                vector.tensor_copy(csc[:], ps_r[:, 0:512])
                vector.tensor_copy(cinv[:], ps_r[:, 512:1024])
            p.bump("pe", 1)

        with nc.Block() as block:
            svd = p.s("vd")
            a0 = p.val("act")
            v0, v3 = {}, {}
            ia = [i32b, i32b2]

            @block.vector
            def _(vector):
                csc_ap = bass.AP(csc, 0, [[TT, 128], [0, 4], [1, TT]])
                cinv_ap = bass.AP(cinv, 0, [[TT, 128], [0, 4], [1, TT]])
                for g in range(8):
                    hg = hsil[:, g * 4 * TT:(g + 1) * 4 * TT]
                    if g >= 2:
                        vector.wait_ge(svd, v3[g - 2])
                    vector.tensor_tensor(ia[g % 2][:], hg, csc_ap,
                                         OP.mult).then_inc(svd, 1)
                    v0[g] = p.bump("vd")
                    vector.wait_ge(sact, a0 + 2 * g + 2)
                    vector.tensor_tensor(hg, ia[g % 2][:], cinv_ap,
                                         OP.mult).then_inc(svd, 1)
                    v3[g] = p.bump("vd")

            @block.scalar
            def _(scalar):
                for g in range(8):
                    scalar.wait_ge(svd, v0[g])
                    if g >= 1:
                        scalar.wait_ge(sact, a0 + 2 * g)
                    scalar.activation(ib2[:], ia[g % 2][:], AF.Identity,
                                      bias=mgk[:, 0:1]).then_inc(sact, 1)
                    scalar.wait_ge(sact, a0 + 2 * g + 1)
                    scalar.activation(ia[g % 2][:], ib2[:], AF.Identity,
                                      bias=mgn[:, 0:1]).then_inc(sact, 1)
            p.bump("act", 16)

        with nc.Block() as block:
            evacs = []
            for m in range(8):
                def f(eng, pst, m=m):
                    return eng.tensor_copy(y_fm[:, m * TT:(m + 1) * TT], pst)
                evacs.append(f)
            emit_mm_stream(p, block, w2q_s, hsil, 32, 8, strip, strips, ps,
                           "vector", evacs)

        with nc.Block() as block:
            pe0, dve0 = p.val("pe"), p.val("dve")

            @block.tensor
            def _(tensor):
                for s in range(NSUB):
                    if s >= 2:
                        tensor.wait_ge(sdve, dve0 + s - 1)
                    mm = None
                    for c in range(8):
                        mm = tensor.transpose(
                            ps_t[s % 2][:, c * 128:(c + 1) * 128],
                            y_fm[:, c * TT + s * 128:c * TT + (s + 1) * 128],
                            ident[:])
                    mm.then_inc(spe, 1)

            @block.vector
            def _(vector):
                for s in range(NSUB):
                    vector.wait_ge(spe, pe0 + s + 1)
                    vector.tensor_tensor(out[:, s * D:(s + 1) * D],
                                         xbuf[:, s * D:(s + 1) * D],
                                         ps_t[s % 2][:, 0:D],
                                         OP.add).then_inc(sdve, 1)

            @block.sync
            def _(sync):
                sync.wait_ge(sdve, dve0 + NSUB)
                sync.dma_start(out=tok_in(x2_h, r0),
                               in_=tok_sb(out[:], NSUB, D)).then_inc(sst, 16)
            p.bump("pe", NSUB)
            p.bump("dve", NSUB)
            p.bump("st", 16)

    with nc.Block() as block:
        @block.sync
        def _(sync):
            sync.wait_ge(sst, p.val("st"))

    p.pop_scope()
    if standalone:
        return p.close()


def lc1_inmaps(x1_full, ins):
    com = {
        "w1t": np.ascontiguousarray(ins["L0_ffn_w1"].T),
        "ident": eye_bf16(),
    }
    return [dict(com, x1=np.ascontiguousarray(x1_full[c])) for c in range(NCORES)]


def lc2_inmaps(hs_list, x1_full, ins):
    cw = np.ascontiguousarray(
        np.asarray(ins["L0_ffn_conv_w"])[:, 0, :].reshape(32, 128, 3)
        .transpose(1, 0, 2).reshape(128, 96))
    com = {
        "w2t": np.ascontiguousarray(ins["L0_ffn_w2"].T),
        "convw": cw,
        "convb": vec_pm(ins["L0_ffn_conv_b"]),
        "ident": eye_bf16(),
        "identf": np.eye(128, dtype=np.float32),
    }
    return [dict(com, hs=np.ascontiguousarray(hs_list[c]),
                 x1=np.ascontiguousarray(x1_full[c])) for c in range(NCORES)]


def build_ld(p=None, H=None):
    standalone = p is None
    if standalone:
        p = Prog()
        p.strip_bufs()
    p.H = H or {}
    p.push_scope()
    nc = p.nc
    x_h = p.dram("x2", [N, D], F32, "ExternalInput")
    w_h = p.dram("ipw", [D, 4352], F32, "ExternalInput")
    wdt_h = p.dram("ipwdt", [D, 32], F32, "ExternalInput")
    dtb_h = p.dram("dtb", [32, 1], F32, "ExternalInput")
    id_h = p.dram("ident", [128, 128], BF16, "ExternalInput")
    idf_h = p.dram("identf", [128, 128], F32, "ExternalInput")
    zs_h = p.dram("zs", [2048, N], BF16, "ExternalOutput")
    xbcs_h = p.dram("xbcs", [2304, N], BF16, "ExternalOutput")
    dts_h = p.dram("dts", [N, 32], F32, "ExternalOutput")
    ipw_s = p.dram("ipw_bf", [D, 4352], BF16)

    stats = p.sb("stats", [128, 16 * NSUB], F32)
    eps = p.sb("eps", [128, 1], F32)
    one_t = p.sb("one_t", [128, 1], F32)
    ident = p.sb("id_sb", [128, 128], BF16)
    identf = p.sb("idf_sb", [128, 128], F32)
    dtb = p.sb("dtb_sb", [32, 1], F32)
    wdt = p.sb("wdt_sb", [128, 8 * 32], BF16)

    p.sems(["ld", "st", "prep", "pe", "dve", "act", "lns", "bank", "mx", "vd"])
    sld, sst = p.s("ld"), p.s("st")
    spe, sdve, sact = p.s("pe"), p.s("dve"), p.s("act")

    with nc.Block() as block:
        emit_param_loads(p, block, [(ident, id_h), (identf, idf_h),
                                    (dtb, dtb_h)])

        @block.vector
        def _(vector):
            vector.memset(eps[:], 1e-5)
            vector.memset(one_t[:], 1.0)

    with p.scope() as prep_scope:
        with nc.Block() as block:
            sts, _nh = emit_weight_prep(p, block, [
                {"src": wdt_h, "dst_sb": wdt},
                {"src": w_h, "dst_hbm": ipw_s},
            ], prep_scope)
        with nc.Block() as block:
            @block.sync
            def _(sync):
                for pr in range(sts.np):
                    if sts.counts[pr]:
                        sync.wait_ge(sts.sems[pr], sts.counts[pr])

    xbuf = p.sb("xbuf", [128, NSUB * D], F32)
    xln = p.sb("xln", [128, NSUB * D], BF16)
    xn_fm = p.sb("xn_fm", [128, 8 * TT], BF16)
    zbuf = p.sb("zbuf", [128, 16 * TT], BF16)
    xbcbuf = p.sb("xbcbuf", [128, 18 * TT], BF16)
    dtf = p.sb("dtf", [32, 6 * TT], F32)
    dtt = p.sb("dtt", [128, NSUB * 32], F32)
    strip = p.strip_bufs()
    strips = p.stream("strips", 12)
    osts = p.stream("aux_st")
    ps = [p.psum(f"ps{i}", [128, 512], F32) for i in range(4)]
    ps_dt = p.psum("ps_dt", [32, 512], F32)
    ps_t = [p.psum(f"ps_t{i}", [128, 1024], BF16) for i in range(2)]
    ps_tf = p.psum("ps_tf", [128, 128], F32)

    for i in range(NTILES):
        r0 = i * TT
        with nc.Block() as block:
            @block.sync
            def _(sync):
                sync.dma_start(out=tok_sb(xbuf[:], NSUB, D),
                               in_=tok_in(x_h, r0)).then_inc(sld, 16)
            p.bump("ld", 16)
            emit_ln(p, block, xbuf, xln, stats, eps,
                    wait_sem=sld, wait_val=p.val("ld"))

        with nc.Block() as block:
            emit_tok2fm(p, block, xln, xn_fm, ident, ps_t, nchunks=8)

        if i >= 1:
            with nc.Block() as block:
                @block.scalar
                def _(scalar):
                    for pr in range(osts.np):
                        if osts.counts[pr]:
                            scalar.wait_ge(osts.sems[pr], osts.counts[pr])

        with nc.Block() as block:
            evacs = []
            for m in range(34):
                if m < 16:
                    def f(eng, pst, m=m):
                        return eng.activation(zbuf[:, m * TT:(m + 1) * TT], pst,
                                              AF.Silu)
                else:
                    def f(eng, pst, m=m):
                        return eng.copy(xbcbuf[:, (m - 16) * TT:(m - 15) * TT], pst)
                evacs.append(f)
            emit_mm_stream(p, block, ipw_s, xn_fm, 8, 34, strip, strips, ps,
                           "scalar", evacs)
            mx_done = p.val("mx")

            pe0, dve0, act0 = p.val("pe"), p.val("dve"), p.val("act")

            @block.tensor
            def _(tensor):
                mm = None
                for k in range(8):
                    mm = tensor.matmul(ps_dt[:], wdt[:, k * 32:(k + 1) * 32],
                                       xn_fm[:, k * TT:(k + 1) * TT],
                                       start=(k == 0), stop=(k == 7))
                mm.then_inc(spe, 1)

            @block.vector
            def _(vector):
                vector.wait_ge(spe, pe0 + 1)
                vector.tensor_scalar(dtf[:, 0:TT], ps_dt[:], dtb[:, 0:1], None,
                                     OP.add).then_inc(sdve, 1)
                vector.drain()
                vector.wait_ge(sact, act0 + 2)
                vector.tensor_scalar(dtf[:, 4 * TT:5 * TT], dtf[:, 0:TT], 0.0,
                                     None, OP.max)
                vector.drain()
                vector.tensor_tensor(dtf[:, 5 * TT:6 * TT], dtf[:, 4 * TT:5 * TT],
                                     dtf[:, 3 * TT:4 * TT], OP.add).then_inc(sdve, 1)
                vector.drain()

            @block.scalar
            def _(scalar):
                scalar.wait_ge(sdve, dve0 + 1)
                scalar.activation(dtf[:, 1 * TT:2 * TT], dtf[:, 0:TT], AF.Abs)
                scalar.drain()
                scalar.activation(dtf[:, 2 * TT:3 * TT], dtf[:, 1 * TT:2 * TT],
                                  AF.Exp, scale=-1.0).then_inc(sact, 1)
                scalar.drain()
                scalar.activation(dtf[:, 3 * TT:4 * TT], dtf[:, 2 * TT:3 * TT],
                                  AF.Ln, bias=one_t[0:32, 0:1]).then_inc(sact, 1)

            @block.tensor
            def _(tensor):
                tensor.wait_ge(sdve, dve0 + 2)
                for s in range(NSUB):
                    if s >= 1:
                        tensor.wait_ge(sdve, dve0 + 2 + s)
                    tensor.transpose(ps_tf[:, (s % 2) * 32:(s % 2) * 32 + 32],
                                     dtf[:, 5 * TT + s * 128:5 * TT + (s + 1) * 128],
                                     identf[0:32, 0:32]).then_inc(spe, 1)

            @block.vector
            def _(vector):
                for s in range(NSUB):
                    vector.wait_ge(spe, pe0 + 1 + s + 1)
                    vector.tensor_copy(dtt[:, s * 32:(s + 1) * 32],
                                       ps_tf[:, (s % 2) * 32:(s % 2) * 32 + 32]
                                       ).then_inc(sdve, 1)

            p.bump("pe", 1 + NSUB)
            p.bump("dve", 2 + NSUB)
            p.bump("act", 2)

            @block.gpsimd
            def _(gpsimd):
                for c in range(16):
                    gpsimd.wait_ge(p.s("mx"), mx_done - 34 + 2 * (c // 2) + 2)
                    osts.issue(gpsimd, zs_h[c * 128:(c + 1) * 128, r0:r0 + TT],
                               zbuf[:, c * TT:(c + 1) * TT])
                for c in range(18):
                    gpsimd.wait_ge(p.s("mx"),
                                   mx_done - 34 + 2 * ((16 + c) // 2) + 2)
                    osts.issue(gpsimd, xbcs_h[c * 128:(c + 1) * 128, r0:r0 + TT],
                               xbcbuf[:, c * TT:(c + 1) * TT])
                gpsimd.wait_ge(sdve, p.val("dve"))
                osts.issue(gpsimd,
                           dts_h[r0:r0 + TT, :].rearrange("(s p) h -> p s h", p=128),
                           dtt[:].rearrange("p (s h) -> p s h", s=NSUB))

    with nc.Block() as block:
        @block.sync
        def _(sync):
            for pr in range(osts.np):
                if osts.counts[pr]:
                    sync.wait_ge(osts.sems[pr], osts.counts[pr])

    p.pop_scope()
    if standalone:
        return p.close()


def build_ld2(p=None, H=None):
    standalone = p is None
    if standalone:
        p = Prog()
        p.strip_bufs()
    p.H = H or {}
    p.push_scope()
    nc = p.nc
    xbcs_h = p.dram("xbcs", [2304, N], BF16, "ExternalInput")
    cw_h = p.dram("convw", [128, 18 * 4], F32, "ExternalInput")
    cb_h = p.dram("convb", [128, 18], F32, "ExternalInput")
    id_h = p.dram("ident", [128, 128], BF16, "ExternalInput")
    xpt_h = p.dram("xpt", [N, 2048], BF16, "ExternalOutput")
    bfs_h = p.dram("bfs", [128, N], BF16, "ExternalOutput")
    cfs_h = p.dram("cfs", [128, N], BF16, "ExternalOutput")
    bts_h = p.dram("bts", [N, 128], BF16, "ExternalOutput")

    ident = p.sb("id_sb", [128, 128], BF16)
    convw = p.sb("convw_sb", [128, 72], F32)
    convb = p.sb("convb_sb", [128, 18], F32)

    p.sems(["ld", "st", "prep", "pe", "dve", "act", "lns", "bank", "mx", "vd"])
    sld, sst = p.s("ld"), p.s("st")
    spe, sdve, sact = p.s("pe"), p.s("dve"), p.s("act")

    with nc.Block() as block:
        emit_param_loads(p, block, [(ident, id_h), (convw, cw_h), (convb, cb_h)])

    HW = TT + 3
    hh = p.sb("hh", [128, 18 * HW], BF16)
    o1 = p.sb("o1", [128, 6 * TT], BF16)
    o2 = p.sb("o2", [128, 6 * TT], BF16)
    hsil = p.sb("hsil", [128, 18 * TT], BF16)
    xp_tok = p.sb("xp_tok", [128, NSUB * 2048], BF16)
    bt_tok = p.sb("bt_tok", [128, NSUB * 128], BF16)
    hlds = p.stream("aux_ld")
    osts = p.stream("aux_st")
    ps_t = [p.psum(f"ps_t{i}", [128, 1024], BF16) for i in range(2)]

    with nc.Block() as block:
        @block.vector
        def _(vector):
            vector.memset(hh[:], 0.0)

    for i in range(NTILES):
        r0 = i * TT
        lo = max(0, r0 - 3)
        coff = lo - (r0 - 3)
        w = r0 + TT - lo
        with nc.Block() as block:
            @block.sync
            def _(sync):
                for c in range(18):
                    hlds.issue(sync,
                               bass.AP(hh, c * HW + coff, [[18 * HW, 128], [1, w]]),
                               xbcs_h[c * 128:(c + 1) * 128, lo:r0 + TT])

            @block.vector
            def _(vector):
                for pr in range(hlds.np):
                    vector.wait_ge(hlds.sems[pr], hlds.counts[pr])

        with nc.Block() as block:
            dve0, act0 = p.val("dve"), p.val("act")

            @block.vector
            def _(vector):
                for g in range(3):
                    if g >= 1:
                        vector.wait_ge(sact, act0 + g)
                    nch = 6
                    last = None
                    for cc in range(nch):
                        c = 6 * g + cc
                        last = vector.tensor_scalar(
                            o1[:, cc * TT:(cc + 1) * TT],
                            bass.AP(hh, c * HW, [[18 * HW, 128], [1, TT]]),
                            convw[:, 4 * c:4 * c + 1],
                            convb[:, c:c + 1], OP.mult, OP.add)
                    vfence(p, vector, last)
                    last = None
                    for cc in range(nch):
                        c = 6 * g + cc
                        last = vector.scalar_tensor_tensor(
                            o2[:, cc * TT:(cc + 1) * TT],
                            bass.AP(hh, c * HW + 1, [[18 * HW, 128], [1, TT]]),
                            convw[:, 4 * c + 1:4 * c + 2],
                            o1[:, cc * TT:(cc + 1) * TT], OP.mult, OP.add)
                    vfence(p, vector, last)
                    last = None
                    for cc in range(nch):
                        c = 6 * g + cc
                        last = vector.scalar_tensor_tensor(
                            o1[:, cc * TT:(cc + 1) * TT],
                            bass.AP(hh, c * HW + 2, [[18 * HW, 128], [1, TT]]),
                            convw[:, 4 * c + 2:4 * c + 3],
                            o2[:, cc * TT:(cc + 1) * TT], OP.mult, OP.add)
                    vfence(p, vector, last)
                    last = None
                    for cc in range(nch):
                        c = 6 * g + cc
                        last = vector.scalar_tensor_tensor(
                            o2[:, cc * TT:(cc + 1) * TT],
                            bass.AP(hh, c * HW + 3, [[18 * HW, 128], [1, TT]]),
                            convw[:, 4 * c + 3:4 * c + 4],
                            o1[:, cc * TT:(cc + 1) * TT], OP.mult, OP.add)
                    last.then_inc(sdve, 1)

            @block.scalar
            def _(scalar):
                for g in range(3):
                    scalar.wait_ge(sdve, dve0 + g + 1)
                    last = None
                    for cc in range(6):
                        c = 6 * g + cc
                        last = scalar.activation(hsil[:, c * TT:(c + 1) * TT],
                                                 o2[:, cc * TT:(cc + 1) * TT],
                                                 AF.Silu)
                    last.then_inc(sact, 1)
            p.bump("dve", 3)
            p.bump("act", 3)

        with nc.Block() as block:
            pe0, dve0 = p.val("pe"), p.val("dve")

            @block.tensor
            def _(tensor):
                t = 0
                for s in range(NSUB):
                    for gg in range(4):
                        if t >= 2:
                            tensor.wait_ge(sdve, dve0 + t - 1)
                        mm = None
                        for cc in range(4):
                            c = gg * 4 + cc
                            mm = tensor.transpose(
                                ps_t[t % 2][:, cc * 128:(cc + 1) * 128],
                                hsil[:, c * TT + s * 128:c * TT + s * 128 + 128],
                                ident[:])
                        mm.then_inc(spe, 1)
                        t += 1
                for s in range(NSUB):
                    if t >= 2:
                        tensor.wait_ge(sdve, dve0 + t - 1)
                    tensor.transpose(ps_t[t % 2][:, 0:128],
                                     hsil[:, 16 * TT + s * 128:16 * TT + s * 128 + 128],
                                     ident[:]).then_inc(spe, 1)
                    t += 1

            @block.scalar
            def _(scalar):
                t = 0
                for s in range(NSUB):
                    for gg in range(4):
                        scalar.wait_ge(spe, pe0 + t + 1)
                        scalar.copy(
                            xp_tok[:, s * 2048 + gg * 512:s * 2048 + (gg + 1) * 512],
                            ps_t[t % 2][:, 0:512]).then_inc(sdve, 1)
                        t += 1
                for s in range(NSUB):
                    scalar.wait_ge(spe, pe0 + t + 1)
                    scalar.copy(bt_tok[:, s * 128:(s + 1) * 128],
                                ps_t[t % 2][:, 0:128]).then_inc(sdve, 1)
                    t += 1
            p.bump("pe", 20)
            p.bump("dve", 20)

        with nc.Block() as block:
            @block.gpsimd
            def _(gpsimd):
                osts.issue(gpsimd,
                           xpt_h[r0:r0 + TT, :].rearrange("(s p) f -> p s f", p=128),
                           xp_tok[:].rearrange("p (s f) -> p s f", s=NSUB))
                osts.issue(gpsimd,
                           bts_h[r0:r0 + TT, :].rearrange("(s p) f -> p s f", p=128),
                           bt_tok[:].rearrange("p (s f) -> p s f", s=NSUB))
                osts.issue(gpsimd, bfs_h[:, r0:r0 + TT],
                           hsil[:, 16 * TT:17 * TT])
                osts.issue(gpsimd, cfs_h[:, r0:r0 + TT],
                           hsil[:, 17 * TT:18 * TT])

            @block.sync
            def _(sync):
                for pr in range(osts.np):
                    if osts.counts[pr]:
                        sync.wait_ge(osts.sems[pr], osts.counts[pr])

    p.pop_scope()
    if standalone:
        return p.close()


def build_le(nch=None, p=None, H=None):
    standalone = p is None
    if standalone:
        p = Prog()
        p.strip_bufs()
    p.H = H or {}
    p.push_scope()
    nc = p.nc
    xpt_h = p.dram("xpt", [N, 2048], BF16, "ExternalInput")
    bts_h = p.dram("bts", [N, 128], BF16, "ExternalInput")
    bfs_h = p.dram("bfs", [128, N], BF16, "ExternalInput")
    cfs_h = p.dram("cfs", [128, N], BF16, "ExternalInput")
    dts_h = p.dram("dts", [N, 32], F32, "ExternalInput")
    alog_h = p.dram("alog", [1, 32], F32, "ExternalInput")
    dvec_h = p.dram("dvec", [1, 32], F32, "ExternalInput")
    tri_h = p.dram("tri", [128, 128], F32, "ExternalInput")
    tris_h = p.dram("tris", [128, 128], F32, "ExternalInput")
    idf_h = p.dram("identf", [128, 128], F32, "ExternalInput")
    ys_h = p.dram("ys", [N, 2048], F32, "ExternalOutput")

    tri = p.sb("tri_sb", [128, 128], F32)
    tris = p.sb("tris_sb", [128, 128], F32)
    identf = p.sb("idf_sb", [128, 128], F32)
    alog = p.sb("alog_sb", [1, 32], F32)
    dvec = p.sb("dvec_sb", [1, 32], F32)
    onesr = p.sb("onesr", [1, 128], F32)
    arow = p.sb("arow", [1, 32], F32)
    A_rep = p.sb("A_rep", [128, 32], F32)
    D_rep = p.sb("D_rep", [128, 32], F32)
    S = p.sb("S_sb", [128, 2048], F32)
    Sbf = p.sb("Sbf", [128, 2048], BF16)
    xt = [p.sb(f"xt{i}", [128, 2048], BF16) for i in range(2)]
    bt = [p.sb(f"bt{i}", [128, 128], BF16) for i in range(2)]
    bf = [p.sb(f"bf{i}", [128, 128], BF16) for i in range(2)]
    cf = [p.sb(f"cf{i}", [128, 128], BF16) for i in range(2)]
    dtt = [p.sb(f"dtt{i}", [128, 32], F32) for i in range(2)]
    adt = p.sb("adt", [128, 32], F32)
    xtl = p.sb("xtl", [128, 2048], BF16)
    xD = p.sb("xD", [128, 2048], F32)
    cl = p.sb("cl", [128, 32], F32)
    clT = p.sb("clT", [32, 128], F32)
    llrow = p.sb("llrow", [1, 32], F32)
    Ll = p.sb("Ll", [128, 32], F32)
    Wd = p.sb("Wd", [128, 32], F32)
    expl = p.sb("expl", [128, 32], F32)
    decr = p.sb("decr", [128, 32], F32)
    cbt = p.sb("cbt_sb", [128, 128], BF16)
    rhs4 = [p.sb(f"rhs4{i}", [128, 512], F32) for i in range(2)]
    segx = [p.sb(f"segx{i}", [128, 512], BF16) for i in range(2)]
    P4 = [p.sb(f"P4{i}", [128, 512], BF16) for i in range(2)]
    Bw4 = [p.sb(f"Bw4{i}", [128, 512], BF16) for i in range(2)]
    e1 = p.sb("e1", [128, 256], F32)
    e2 = p.sb("e2", [128, 256], F32)
    ybuf = p.sb("ybuf", [128, 2048], F32)

    p.sems(["ld", "st", "pe", "dve", "act", "mx", "bank", "lns", "prep", "vd"])
    sld, sst = p.s("ld"), p.s("st")
    spe, sdve, sact, smx = p.s("pe"), p.s("dve"), p.s("act"), p.s("mx")
    lds = p.stream("ssd_ld", nparity=5)
    ysts = p.stream("ys_st")
    lb0 = lds.i

    ps_misc = p.psum("ps_misc", [128, 128], F32)
    ps_cbt = p.psum("ps_cbt", [128, 128], F32)
    ps_seg = [p.psum(f"ps_seg{i}", [128, 512], F32) for i in range(2)]
    ps_ab = [p.psum(f"ps_ab{i}", [128, 512], F32) for i in range(2)]
    ps_c = [p.psum(f"ps_c{i}", [128, 512], F32) for i in range(2)]

    NCH = (N // 128) if nch is None else nch

    def issue_loads(sync, c):
        r0 = c * 128
        if c >= 1:
            for j in range(5):
                sync.wait_ge(lds.sems[j], 16 * (lb0 // 5 + c))
        lds.issue(sync, dtt[c % 2][:], dts_h[r0:r0 + 128, :])
        lds.issue(sync, xt[c % 2][:], xpt_h[r0:r0 + 128, :])
        lds.issue(sync, bt[c % 2][:], bts_h[r0:r0 + 128, :])
        lds.issue(sync, bf[c % 2][:], bfs_h[:, r0:r0 + 128])
        lds.issue(sync, cf[c % 2][:], cfs_h[:, r0:r0 + 128])

    with nc.Block() as block:
        emit_param_loads(p, block, [(tri, tri_h), (tris, tris_h),
                                    (identf, idf_h), (alog, alog_h),
                                    (dvec, dvec_h)])

        @block.sync
        def _(sync):
            issue_loads(sync, 0)

        @block.vector
        def _(vector):
            vector.memset(S[:], 0.0)
            vector.memset(onesr[:], 1.0)

        @block.scalar
        def _(scalar):
            scalar.wait_ge(sld, p.val("ld"))
            scalar.activation(arow[:], alog[:], AF.Exp)

    with nc.Block() as block:
        pe0 = p.val("pe")

        @block.tensor
        def _(tensor):
            tensor.matmul(ps_misc[:, 0:32], onesr[:], arow[:],
                          start=True, stop=True)
            tensor.matmul(ps_misc[:, 32:64], onesr[:], dvec[:],
                          start=True, stop=True).then_inc(spe, 1)

        @block.vector
        def _(vector):
            vector.wait_ge(spe, pe0 + 1)
            vector.tensor_scalar(A_rep[:], ps_misc[:, 0:32], -1.0, None, OP.mult)
            vector.tensor_copy(D_rep[:], ps_misc[:, 32:64])
        p.bump("pe", 1)

    prev = {}
    with nc.Block() as block:
        bodies = {"sync": [], "vector": [], "tensor": [], "scalar": [],
                  "gpsimd": []}
        REC = {}
        for c in range(NCH):
            r0 = c * 128
            pe0, dve0, act0 = p.val("pe"), p.val("dve"), p.val("act")
            vd0 = p.val("vd")
            svd = p.s("vd")
            pv = dict(prev)

            def _sync(sync, c=c, dve0=dve0):
                if c + 1 < NCH:
                    sync.wait_ge(sdve, REC[("xtl", c)])
                    issue_loads(sync, c + 1)

            def _vector(vector, c=c, pe0=pe0, dve0=dve0, act0=act0, vd0=vd0,
                        pv=pv):
                nd = [dve0]

                def inc(instr, kind, g=None):
                    nd[0] += 1
                    REC[(kind, c, g) if g is not None else (kind, c)] = nd[0]
                    instr.then_inc(sdve, 1)

                for j in range(5):
                    lds.wait(vector, lds.tok(5 * c + j))
                if pv:
                    vector.wait_ge(sdve, pv["send"])
                    vector.wait_ge(sact, pv["act3"])
                vector.tensor_tensor(adt[:], dtt[c % 2][:], A_rep[:], OP.mult)
                dt_ap = bass.AP(dtt[c % 2], 0, [[32, 128], [1, 32], [0, 64]])
                inc(vector.tensor_tensor(xtl[:], xt[c % 2][:], dt_ap, OP.mult),
                    "xtl")
                d_ap = bass.AP(D_rep, 0, [[32, 128], [1, 32], [0, 64]])
                n0, n1 = (c + 1) // 2, c // 2
                if n0:
                    vector.wait_ge(ysts.sems[0], 16 * n0)
                if n1:
                    vector.wait_ge(ysts.sems[1], 16 * n1)
                vector.tensor_tensor(xD[:], xt[c % 2][:], d_ap, OP.mult)
                vector.tensor_copy(Sbf[:], S[:]).then_inc(svd, 1)
                vector.wait_ge(spe, pe0 + 1)
                inc(vector.tensor_copy(cl[:], ps_misc[:, 0:32]), "cl")
                vector.wait_ge(spe, pe0 + 2)
                inc(vector.tensor_tensor(cbt[:], ps_cbt[:], tri[:], OP.mult),
                    "cbt")
                vector.wait_ge(spe, pe0 + 3)
                inc(vector.tensor_copy(clT[:], ps_misc[0:32, 0:128]), "clT")
                vector.wait_ge(spe, pe0 + 4)
                inc(vector.tensor_copy(llrow[:], ps_misc[0:1, 0:32]), "llrow")
                vector.wait_ge(spe, pe0 + 5)
                vector.tensor_copy(Ll[:], ps_misc[:, 32:64]).then_inc(svd, 1)
                vector.wait_ge(svd, vd0 + 2)
                inc(vector.tensor_tensor(Wd[:], Ll[:], cl[:], OP.subtract), "Wd")
                vector.wait_ge(sact, act0 + 3)
                dec_ap = bass.AP(decr, 0, [[32, 128], [1, 32], [0, 64]])
                inc(vector.tensor_tensor(S[:], S[:], dec_ap, OP.mult), "dec")
                nvd = [vd0 + 2]
                for step in range(10):
                    if step < 8:
                        g = step
                        if g >= 2:
                            vector.wait_ge(spe, pe0 + 5 + 3 * (g - 1))
                        elif pv:
                            vector.wait_ge(spe, pv["pe_end"])
                        tri_ap = bass.AP(tri, 0, [[128, 128], [0, 4], [1, 128]])
                        adt_ap = bass.AP(adt, 4 * g,
                                         [[32, 128], [1, 4], [0, 128]])
                        inc(vector.tensor_tensor(rhs4[g % 2][:], tri_ap, adt_ap,
                                                 OP.mult), "A", g)
                    if 1 <= step <= 8:
                        g = step - 1
                        if g >= 2:
                            vector.wait_ge(spe, pe0 + 5 + 3 * (g - 2) + 3)
                        elif pv:
                            vector.wait_ge(spe, pv["pe_end"])
                        vector.wait_ge(sact, act0 + 3 + 2 * g + 2)
                        cbt_ap = bass.AP(cbt, 0, [[128, 128], [0, 4], [1, 128]])
                        inc(vector.tensor_tensor(P4[g % 2][:], segx[g % 2][:],
                                                 cbt_ap, OP.mult), "P", g)
                        bt_ap = bass.AP(bt[c % 2], 0,
                                        [[128, 128], [0, 4], [1, 128]])
                        wd_ap = bass.AP(Wd, 4 * g, [[32, 128], [1, 4], [0, 128]])
                        inc(vector.tensor_tensor(Bw4[g % 2][:], bt_ap, wd_ap,
                                                 OP.mult), "W", g)
                    if step >= 2:
                        g = step - 2
                        vector.wait_ge(spe, pe0 + 5 + 3 * g + 3)
                        eap = bass.AP(expl, 4 * g, [[32, 128], [1, 4], [0, 64]])
                        vector.tensor_tensor(e1[:], ps_ab[g % 2][:, 256:512],
                                             eap, OP.mult).then_inc(svd, 1)
                        nvd[0] += 1
                        vector.wait_ge(svd, nvd[0])
                        vector.tensor_tensor(e2[:], e1[:],
                                             ps_ab[g % 2][:, 0:256],
                                             OP.add).then_inc(svd, 1)
                        nvd[0] += 1
                        vector.wait_ge(svd, nvd[0])
                        inc(vector.tensor_tensor(
                            ybuf[:, g * 256:(g + 1) * 256], e2[:],
                            xD[:, g * 256:(g + 1) * 256], OP.add), "Y", g)
                        inc(vector.tensor_tensor(
                            S[:, g * 256:(g + 1) * 256],
                            S[:, g * 256:(g + 1) * 256],
                            ps_c[g % 2][:, 0:256], OP.add), "S", g)

            def _tensor(tensor, c=c, pe0=pe0, dve0=dve0, act0=act0, vd0=vd0,
                        pv=pv):
                tensor.wait_ge(svd, vd0 + 1)
                if pv:
                    tensor.wait_ge(svd, pv["vd_ll"])
                    tensor.wait_ge(sdve, pv["dve_cbt"])
                tensor.matmul(ps_misc[:, 0:32], tri[:], adt[:],
                              start=True, stop=True).then_inc(spe, 1)
                for j in range(3, 5):
                    lds.wait(tensor, lds.tok(5 * c + j))
                tensor.matmul(ps_cbt[:], bf[c % 2][:], cf[c % 2][:],
                              start=True, stop=True).then_inc(spe, 1)
                tensor.wait_ge(sdve, REC[("cl", c)])
                tensor.transpose(ps_misc[0:32, 0:128], cl[:],
                                 identf[:]).then_inc(spe, 1)
                tensor.wait_ge(sdve, REC[("clT", c)])
                tensor.transpose(ps_misc[0:1, 0:32], clT[:, 127:128],
                                 identf[0:32, 0:32]).then_inc(spe, 1)
                tensor.wait_ge(sdve, REC[("llrow", c)])
                tensor.matmul(ps_misc[:, 32:64], onesr[:], llrow[:],
                              start=True, stop=True).then_inc(spe, 1)
                for g in range(8):
                    tensor.wait_ge(sdve, REC[("A", c, g)])
                    if g >= 2:
                        tensor.wait_ge(sact, act0 + 3 + 2 * (g - 1))
                    elif pv:
                        tensor.wait_ge(sact, pv["act_end"])
                    mm = None
                    for hh_ in range(4):
                        mm = tensor.matmul(
                            ps_seg[g % 2][:, hh_ * 128:(hh_ + 1) * 128], tris[:],
                            rhs4[g % 2][:, hh_ * 128:(hh_ + 1) * 128],
                            start=True, stop=True)
                    mm.then_inc(spe, 1)
                    tensor.wait_ge(sdve, REC[("W", c, g)])
                    if g >= 2:
                        tensor.wait_ge(sdve, REC[("S", c, g - 2)])
                    elif pv:
                        tensor.wait_ge(sdve, pv["dve_end"])
                    mm = None
                    for hh_ in range(4):
                        h = 4 * g + hh_
                        mm = tensor.matmul(
                            ps_ab[g % 2][:, hh_ * 64:(hh_ + 1) * 64],
                            P4[g % 2][:, hh_ * 128:(hh_ + 1) * 128],
                            xtl[:, h * 64:(h + 1) * 64], start=True, stop=True)
                        mm = tensor.matmul(
                            ps_ab[g % 2][:, 256 + hh_ * 64:256 + (hh_ + 1) * 64],
                            cf[c % 2][:], Sbf[:, h * 64:(h + 1) * 64],
                            start=True, stop=True)
                    for hh_ in range(4):
                        h = 4 * g + hh_
                        mm = tensor.matmul(
                            ps_c[g % 2][:, hh_ * 64:(hh_ + 1) * 64],
                            Bw4[g % 2][:, hh_ * 128:(hh_ + 1) * 128],
                            xtl[:, h * 64:(h + 1) * 64], start=True, stop=True)
                    mm.then_inc(spe, 2)

            def _scalar(scalar, c=c, pe0=pe0, dve0=dve0, act0=act0, vd0=vd0,
                        pv=pv):
                scalar.wait_ge(sdve, REC[("cl", c)])
                scalar.activation(expl[:], cl[:], AF.Exp).then_inc(sact, 1)
                scalar.wait_ge(sdve, REC[("Wd", c)])
                scalar.activation(Wd[:], Wd[:], AF.Exp).then_inc(sact, 1)
                scalar.wait_ge(svd, vd0 + 2)
                scalar.activation(decr[:], Ll[:], AF.Exp).then_inc(sact, 1)
                for g in range(8):
                    scalar.wait_ge(spe, pe0 + 5 + 3 * g + 1)
                    if g >= 2:
                        scalar.wait_ge(sdve, REC[("P", c, g - 2)])
                    elif pv:
                        scalar.wait_ge(sdve, pv["dve_end"])
                    scalar.activation(segx[g % 2][:], ps_seg[g % 2][:],
                                      AF.Exp).then_inc(sact, 2)

            def _gpsimd(gpsimd, c=c, r0=r0):
                gpsimd.wait_ge(sdve, REC[("S", c, 7)])
                ysts.issue(gpsimd, ys_h[r0:r0 + 128, :], ybuf[:])

            bodies["sync"].append(_sync)
            bodies["vector"].append(_vector)
            bodies["tensor"].append(_tensor)
            bodies["scalar"].append(_scalar)
            bodies["gpsimd"].append(_gpsimd)

            prev = {
                "send": dve0 + 7 + 5 * 8,
                "pe_end": pe0 + 5 + 3 * 8,
                "act_end": act0 + 3 + 16,
                "act3": act0 + 3,
                "dve_end": dve0 + 7 + 5 * 8,
                "vd_ll": vd0 + 2,
                "dve_cbt": dve0 + 3,
            }
            p.bump("vd", 2 + 16)
            p.bump("dve", 7 + 5 * 8)
            p.bump("pe", 5 + 3 * 8)
            p.bump("act", 3 + 16)

        @block.vector
        def _(vector):
            for f in bodies["vector"]:
                f(vector)

        @block.sync
        def _(sync):
            for f in bodies["sync"]:
                f(sync)

        @block.tensor
        def _(tensor):
            for f in bodies["tensor"]:
                f(tensor)

        @block.scalar
        def _(scalar):
            for f in bodies["scalar"]:
                f(scalar)

        @block.gpsimd
        def _(gpsimd):
            for f in bodies["gpsimd"]:
                f(gpsimd)

    with nc.Block() as block:
        @block.sync
        def _(sync):
            for pr in range(ysts.np):
                if ysts.counts[pr]:
                    sync.wait_ge(ysts.sems[pr], ysts.counts[pr])

    p.pop_scope()
    if standalone:
        return p.close()


def build_lf(p=None, H=None):
    standalone = p is None
    if standalone:
        p = Prog()
        p.strip_bufs()
    p.H = H or {}
    p.push_scope()
    nc = p.nc
    ys_h = p.dram("ys", [N, 2048], F32, "ExternalInput")
    zs_h = p.dram("zs", [2048, N], BF16, "ExternalInput")
    x2_h = p.dram("x2", [N, D], F32, "ExternalInput")
    nw_h = p.dram("nw", [1, 2048], F32, "ExternalInput")
    opw_h = p.dram("opw", [2048, D], F32, "ExternalInput")
    id_h = p.dram("ident", [128, 128], BF16, "ExternalInput")
    idf_h = p.dram("identf", [128, 128], F32, "ExternalInput")
    x3_h = p.dram("x3", [N, D], F32, "ExternalOutput")
    xln_h = p.dram("xln", [N, D], F32, "ExternalOutput")
    ml_h = p.dram("mloc", [1, 1], F32, "ExternalOutput")

    stats = p.sb("stats", [128, 16 * NSUB], F32)
    rms = p.sb("rms", [128, 12], F32)
    eps = p.sb("eps", [128, 1], F32)
    ident = p.sb("id_sb", [128, 128], BF16)
    identf = p.sb("idf_sb", [128, 128], F32)
    onesr = p.sb("onesr", [1, 128], F32)
    nwrow = p.sb("nwrow", [1, 2048], F32)
    nwrep = p.sb("nwrep", [128, 2048], F32)
    opw = p.sb("opw_sb", [128, 16 * D], BF16)
    mrun = p.sb("mrun", [128, 2], F32)

    p.sems(["ld", "st", "prep", "pe", "dve", "act", "lns", "bank", "mx", "vd"])
    sld, sst = p.s("ld"), p.s("st")
    spe, sdve, sact = p.s("pe"), p.s("dve"), p.s("act")

    with nc.Block() as block:
        emit_param_loads(p, block, [(ident, id_h), (identf, idf_h),
                                    (nwrow, nw_h)])

        @block.vector
        def _(vector):
            vector.memset(eps[:], 1e-5)
            vector.memset(onesr[:], 1.0)
            vector.memset(mrun[:], 0.0)

    ps = [p.psum(f"ps{i}", [128, 512], F32) for i in range(4)]
    ps_t = [p.psum(f"ps_t{i}", [128, 1024], BF16) for i in range(2)]
    ps_b = p.psum("ps_b", [128, 512], F32)

    with nc.Block() as block:
        pe0 = p.val("pe")

        dve0f = p.val("dve")

        @block.tensor
        def _(tensor):
            for j in range(4):
                if j >= 1:
                    tensor.wait_ge(sdve, dve0f + j)
                tensor.matmul(ps_b[:], onesr[:],
                              nwrow[0:1, j * 512:(j + 1) * 512],
                              start=True, stop=True).then_inc(spe, 1)

        @block.vector
        def _(vector):
            for j in range(4):
                vector.wait_ge(spe, pe0 + j + 1)
                vector.tensor_copy(nwrep[:, j * 512:(j + 1) * 512],
                                   ps_b[:]).then_inc(sdve, 1)
        p.bump("pe", 4)
        p.bump("dve", 4)

    with p.scope() as prep_scope:
        with nc.Block() as block:
            emit_weight_prep(p, block, [{"src": opw_h, "dst_sb": opw}],
                             prep_scope)

    ybuf = p.sb("ybuf", [128, NSUB * 2048], F32)
    zbuf = p.sb("zbuf", [128, 16 * TT], BF16)
    szt = p.sb("szt", [128, NSUB * 2048], BF16)
    gbf = szt
    g_fm = p.sb("g_fm", [128, 16 * TT], BF16)
    x2b = p.sb("x2b", [128, NSUB * D], F32)
    x3b = p.sb("x3b", [128, NSUB * D], F32)
    xlnb = p.sb("xlnb", [128, NSUB * D], F32)
    y2 = p.sb("y2", [128, 8 * TT], BF16)
    sqd = p.sb("sqd", [128, 2048], BF16)
    gscr = p.sb("gscr", [128, 2048], F32)

    for i in range(NTILES):
        r0 = i * TT
        with nc.Block() as block:
            st0 = p.val("st")

            @block.sync
            def _(sync):
                sync.wait_ge(sst, st0)
                sync.dma_start(out=tok_sb(ybuf[:], NSUB, 2048),
                               in_=ys_h[r0:r0 + TT, :].rearrange(
                                   "(s p) f -> p s f", p=128)).then_inc(sld, 16)
                sync.dma_start(
                    out=zbuf[:].rearrange("p (c t) -> p c t", c=16),
                    in_=zs_h[:, r0:r0 + TT].rearrange("(c p) t -> p c t", p=128),
                ).then_inc(sld, 16)
                sync.dma_start(out=tok_sb(x2b[:], NSUB, D),
                               in_=tok_in(x2_h, r0)).then_inc(sld, 16)
            p.bump("ld", 48)

        with nc.Block() as block:
            pe0, dve0 = p.val("pe"), p.val("dve")

            @block.tensor
            def _(tensor):
                tensor.wait_ge(sld, p.val("ld"))
                t = 0
                for s in range(NSUB):
                    for gg in range(2):
                        if t >= 2:
                            tensor.wait_ge(sdve, dve0 + t - 1)
                        mm = None
                        for cc in range(8):
                            c = gg * 8 + cc
                            mm = tensor.transpose(
                                ps_t[t % 2][:, cc * 128:(cc + 1) * 128],
                                zbuf[:, c * TT + s * 128:c * TT + s * 128 + 128],
                                ident[:])
                        mm.then_inc(spe, 1)
                        t += 1

            @block.vector
            def _(vector):
                t = 0
                for s in range(NSUB):
                    for gg in range(2):
                        vector.wait_ge(spe, pe0 + t + 1)
                        vector.tensor_copy(
                            szt[:, s * 2048 + gg * 1024:s * 2048 + (gg + 1) * 1024],
                            ps_t[t % 2][:]).then_inc(sdve, 1)
                        t += 1
            p.bump("pe", 8)
            p.bump("dve", 8)

        with nc.Block() as block:
            dve0, act0 = p.val("dve"), p.val("act")

            @block.vector
            def _(vector):
                for s in range(NSUB):
                    vector.tensor_tensor(ybuf[:, s * 2048:(s + 1) * 2048],
                                         ybuf[:, s * 2048:(s + 1) * 2048],
                                         szt[:, s * 2048:(s + 1) * 2048],
                                         OP.mult).then_inc(sdve, 1)
                    vector.drain()
                vector.wait_ge(sact, act0 + NSUB)
                vector.tensor_scalar(rms[:, 4:8], rms[:, 0:4], 1.0 / 2048.0,
                                     None, OP.mult).then_inc(sdve, 1)
                vector.wait_ge(sact, act0 + NSUB + 1)
                vector.reciprocal(rms[:, 8:12], rms[:, 4:8])
                vector.drain()
                for s in range(NSUB):
                    vector.tensor_scalar(gscr[:], ybuf[:, s * 2048:(s + 1) * 2048],
                                         rms[:, 8 + s:9 + s], None, OP.mult)
                    vector.drain()
                    vector.tensor_tensor(gbf[:, s * 2048:(s + 1) * 2048],
                                         gscr[:], nwrep[:], OP.mult)
                    vector.drain()

            @block.scalar
            def _(scalar):
                for s in range(NSUB):
                    scalar.wait_ge(sdve, dve0 + s + 1)
                    scalar.activation(sqd[:], ybuf[:, s * 2048:(s + 1) * 2048],
                                      AF.Square, accum_out=rms[:, s:s + 1])
                    scalar.drain()
                scalar.nop().then_inc(sact, NSUB)
                scalar.wait_ge(sdve, dve0 + NSUB + 1)
                scalar.activation(rms[:, 4:8], rms[:, 4:8], AF.Sqrt,
                                  bias=eps[:, 0:1]).then_inc(sact, 1)
            p.bump("dve", NSUB + 1)
            p.bump("act", NSUB + 1)

        with nc.Block() as block:
            emit_tok2fm(p, block, gbf, g_fm, ident, ps_t, nchunks=16)

        with nc.Block() as block:
            pe0, dve0 = p.val("pe"), p.val("dve")

            @block.tensor
            def _(tensor):
                for g2 in range(2):
                    if g2 == 1:
                        tensor.wait_ge(sdve, dve0 + 4)
                    mm = None
                    for k in range(16):
                        for mi in range(4):
                            m = g2 * 4 + mi
                            mm = tensor.matmul(
                                ps[mi][:],
                                opw[:, k * D + m * 128:k * D + (m + 1) * 128],
                                g_fm[:, k * TT:(k + 1) * TT],
                                start=(k == 0), stop=(k == 15))
                    mm.then_inc(spe, 1)
                for s in range(NSUB):
                    tensor.wait_ge(sdve, dve0 + 8 + max(0, s - 1))
                    mm = None
                    for c in range(8):
                        mm = tensor.transpose(
                            ps_t[s % 2][:, c * 128:(c + 1) * 128],
                            y2[:, c * TT + s * 128:c * TT + (s + 1) * 128],
                            ident[:])
                    mm.then_inc(spe, 1)

            @block.vector
            def _(vector):
                for g2 in range(2):
                    vector.wait_ge(spe, pe0 + g2 + 1)
                    for mi in range(4):
                        m = g2 * 4 + mi
                        vector.tensor_copy(y2[:, m * TT:(m + 1) * TT],
                                           ps[mi][:]).then_inc(sdve, 1)
                for s in range(NSUB):
                    vector.wait_ge(spe, pe0 + 2 + s + 1)
                    vector.tensor_tensor(x3b[:, s * D:(s + 1) * D],
                                         x2b[:, s * D:(s + 1) * D],
                                         ps_t[s % 2][:, 0:D],
                                         OP.add).then_inc(sdve, 1)
            p.bump("pe", 2 + NSUB)
            p.bump("dve", 8 + NSUB)

        with nc.Block() as block:
            emit_ln(p, block, x3b, xlnb, stats, eps, done_inc=(sdve, 1))
            p.bump("dve")
            dve0 = p.val("dve")

            @block.vector
            def _(vector):
                vector.wait_ge(sdve, dve0)
                for s in range(NSUB):
                    vector.reduce_max(mrun[:, 1:2], xlnb[:, s * D:(s + 1) * D],
                                      axis=AX.X, apply_absolute_value=True)
                    vector.drain()
                    vector.tensor_tensor(mrun[:, 0:1], mrun[:, 0:1],
                                         mrun[:, 1:2], OP.max)
                    vector.drain()
                vector.nop().then_inc(sdve, 1)
            p.bump("dve")

            @block.sync
            def _(sync):
                sync.wait_ge(sdve, p.val("dve"))
                sync.dma_start(out=tok_in(x3_h, r0),
                               in_=tok_sb(x3b[:], NSUB, D)).then_inc(sst, 16)
                sync.dma_start(out=tok_in(xln_h, r0),
                               in_=tok_sb(xlnb[:], NSUB, D)).then_inc(sst, 16)
            p.bump("st", 32)

    with nc.Block() as block:
        pe0, dve0 = p.val("pe"), p.val("dve")

        @block.tensor
        def _(tensor):
            tensor.transpose(ps_b[0:1, 0:128], mrun[:, 0:1],
                             identf[:]).then_inc(spe, 1)

        @block.vector
        def _(vector):
            vector.wait_ge(spe, pe0 + 1)
            vector.reduce_max(mrun[0:1, 1:2], ps_b[0:1, 0:128],
                              axis=AX.X).then_inc(sdve, 1)
        p.bump("pe", 1)
        p.bump("dve", 1)

    with nc.Block() as block:
        @block.sync
        def _(sync):
            sync.dma_start(out=ml_h, in_=mrun[0:1, 1:2]).then_inc(sst, 16)
            p.bump("st", 16)
            sync.wait_ge(sst, p.val("st"))

    p.pop_scope()
    if standalone:
        return p.close()


def ld_inmaps(x2_full, ins):
    ipw = np.asarray(ins["L1_attn_in_proj_w"])
    com = {
        "ipw": np.ascontiguousarray(ipw[:, 0:4352]),
        "ipwdt": np.ascontiguousarray(ipw[:, 4352:4384]),
        "dtb": np.ascontiguousarray(np.asarray(ins["L1_attn_dt_bias"]).reshape(32, 1)),
        "ident": eye_bf16(),
        "identf": np.eye(128, dtype=np.float32),
    }
    return [dict(com, x2=np.ascontiguousarray(x2_full[c])) for c in range(NCORES)]


def ld2_inmaps(xbcs_list, ins):
    cwr = np.asarray(ins["L1_attn_conv_w"])[::-1]
    cw = np.ascontiguousarray(cwr.T.reshape(18, 128, 4).transpose(1, 0, 2)
                              .reshape(128, 72))
    com = {
        "convw": cw,
        "convb": vec_pm(ins["L1_attn_conv_b"]),
        "ident": eye_bf16(),
    }
    return [dict(com, xbcs=np.ascontiguousarray(xbcs_list[c]))
            for c in range(NCORES)]


def le_inmaps(xpt_l, bts_l, bfs_l, cfs_l, dts_l, ins):
    r = np.arange(128)
    tri = (r[:, None] <= r[None, :]).astype(np.float32)
    tris = (r[:, None] > r[None, :]).astype(np.float32)
    com = {
        "alog": np.ascontiguousarray(np.asarray(ins["L1_attn_A_log"]).reshape(1, 32)),
        "dvec": np.ascontiguousarray(np.asarray(ins["L1_attn_D"]).reshape(1, 32)),
        "tri": tri, "tris": tris,
        "identf": np.eye(128, dtype=np.float32),
    }
    return [dict(com, xpt=np.ascontiguousarray(xpt_l[c]),
                 bts=np.ascontiguousarray(bts_l[c]),
                 bfs=np.ascontiguousarray(bfs_l[c]),
                 cfs=np.ascontiguousarray(cfs_l[c]),
                 dts=np.ascontiguousarray(dts_l[c])) for c in range(NCORES)]


def lf_inmaps(ys_l, zs_l, x2_full, ins):
    com = {
        "nw": np.ascontiguousarray(np.asarray(ins["L1_attn_norm_w"]).reshape(1, 2048)),
        "opw": np.ascontiguousarray(ins["L1_attn_out_proj_w"]),
        "ident": eye_bf16(),
        "identf": np.eye(128, dtype=np.float32),
    }
    return [dict(com, ys=np.ascontiguousarray(ys_l[c]),
                 zs=np.ascontiguousarray(zs_l[c]),
                 x2=np.ascontiguousarray(x2_full[c])) for c in range(NCORES)]


def emit_wquant_rows(p, block_seq, w_h, wT_s, nch, ident, identf, name, C=1024):
    nc = p.nc
    spe, sdve = p.s("pe"), p.s("dve")
    with p.scope() as scope:
        wtmp = [p.sb_in(scope, f"{name}wt{i}", [128, C], F32) for i in range(2)]
        wq = p.sb_in(scope, f"{name}wq", [128, C], F32)
        wqb = p.sb_in(scope, f"{name}wqb", [128, C], BF16)
        pcs = p.sb_in(scope, f"{name}pc", [128, C], BF16)
        sca = p.sb_in(scope, f"{name}sca", [128, 4], F32)
        ps_tw = [scope.enter_context(nc.psum_tensor(p._nm(f"{name}pst{i}"),
                                                    [128, 1024], BF16))
                 for i in range(2)]
        lds = p.stream("wqrld")
        sts = p.stream("wqrst", nparity=8)
        lb0, sb0 = lds.i, sts.i

        with nc.Block() as block:
            dve0, pe0 = p.val("dve"), p.val("pe")
            npieces = C // 128
            NGR = (npieces + 7) // 8
            PERK = 2 + NGR

            @block.sync
            def _(sync):
                for k in range(nch):
                    if k >= 2:
                        sync.wait_ge(sdve, dve0 + (k - 2) * PERK + 1)
                    lds.issue(sync, wtmp[k % 2][:], w_h[k * 128:(k + 1) * 128, :])

            @block.vector
            def _(vector):
                for k in range(nch):
                    lds.wait(vector, lds.tok(lb0 + k))
                    vector.reduce_max(sca[:, 0:1], wtmp[k % 2][:], axis=AX.X,
                                      apply_absolute_value=True)
                    vector.drain()
                    vector.reciprocal(sca[:, 1:2], sca[:, 0:1])
                    vector.drain()
                    vector.tensor_scalar(sca[:, 1:2], sca[:, 1:2], 127.0, None,
                                         OP.mult)
                    vector.tensor_scalar(sca[:, 2:3], sca[:, 0:1], 1.0 / 127.0,
                                         None, OP.mult)
                    vector.drain()
                    vector.tensor_scalar(wq[:], wtmp[k % 2][:], sca[:, 1:2],
                                         MAGIC, OP.mult, OP.add).then_inc(sdve, 1)
                    vector.drain()
                    vector.tensor_scalar(wq[:], wq[:], MAGIC, -128.0,
                                         OP.subtract, OP.max)
                    vector.drain()
                    vector.tensor_scalar(wqb[:], wq[:], 127.0, sca[:, 2:3],
                                         OP.min, OP.mult).then_inc(sdve, 1)
                    vector.drain()
                    ngr = (npieces + 7) // 8
                    for g in range(ngr):
                        gw = min(8, npieces - g * 8)
                        gi = k * ngr + g
                        vector.wait_ge(spe, pe0 + gi + 1)
                        if k >= 1:
                            sts.prefix_wait(vector,
                                            sb0 + (k - 1) * npieces + g * 8 + gw)
                        vector.tensor_copy(
                            pcs[:, g * 1024:g * 1024 + gw * 128],
                            ps_tw[gi % 2][:, 0:gw * 128]).then_inc(sdve, 1)

            @block.tensor
            def _(tensor):
                ngr = (npieces + 7) // 8
                for k in range(nch):
                    tensor.wait_ge(sdve, dve0 + k * PERK + 2)
                    for g in range(ngr):
                        gw = min(8, npieces - g * 8)
                        gi = k * ngr + g
                        if gi >= 2:
                            gp = gi - 2
                            kp, gp2 = divmod(gp, ngr)
                            tensor.wait_ge(sdve, dve0 + kp * PERK + 2 + gp2 + 1)
                        mm = None
                        for jj in range(gw):
                            j = g * 8 + jj
                            mm = tensor.transpose(
                                ps_tw[gi % 2][:, jj * 128:(jj + 1) * 128],
                                wqb[:, j * 128:(j + 1) * 128],
                                ident[:])
                        mm.then_inc(spe, 1)

            @block.gpsimd
            def _(gpsimd):
                for k in range(nch):
                    for j in range(npieces):
                        t = sb0 + k * npieces + j
                        if t >= 8:
                            gpsimd.wait_ge(sts.sems[t % 8], 16 * (t // 8))
                        gpsimd.wait_ge(sdve, dve0 + k * PERK + 2 + (j // 8) + 1)
                        sts.issue(gpsimd,
                                  wT_s[j * 128:(j + 1) * 128, k * 128:(k + 1) * 128],
                                  pcs[:, j * 128:(j + 1) * 128])

            for k in range(nch):
                p.bump("dve", PERK)
                p.bump("pe", NGR)

        with nc.Block() as block:
            @block.sync
            def _(sync):
                for pr in range(sts.np):
                    if sts.counts[pr]:
                        sync.wait_ge(sts.sems[pr], sts.counts[pr])


def build_lh(ntiles=NTILES, p=None, H=None, mall_w=8):
    standalone = p is None
    if standalone:
        p = Prog()
        p.strip_bufs()
    p.H = H or {}
    p.push_scope()
    nc = p.nc
    I2 = 2816
    xln_h = p.dram("xln", [N, D], F32, "ExternalInput")
    mall_h = p.dram("mall", [1, mall_w], F32, "ExternalInput")
    clip_h = p.dram("clipx", [1, 1], F32, "ExternalInput")
    wg_h = p.dram("wg", [I2, D], F32, "ExternalInput")
    wu_h = p.dram("wu", [I2, D], F32, "ExternalInput")
    id_h = p.dram("ident", [128, 128], BF16, "ExternalInput")
    idf_h = p.dram("identf", [128, 128], F32, "ExternalInput")
    hs_h = p.dram("hs2", [I2, N], BF16, "ExternalOutput")
    m2_h = p.dram("m2loc", [1, 1], F32, "ExternalOutput")
    wgT_s = p.dram("wgT", [D, I2], BF16)
    wuT_s = p.dram("wuT", [D, I2], BF16)

    ident = p.sb("id_sb", [128, 128], BF16)
    identf = p.sb("idf_sb", [128, 128], F32)
    onesr = p.sb("onesr", [1, 128], F32)
    mall = p.sb("mall_sb", [1, mall_w], F32)
    clipx = p.sb("clip_sb", [1, 1], F32)
    gsc = p.sb("gsc", [1, 8], F32)
    gscb = p.sb("gscb", [128, 2], F32)
    mrun = p.sb("mrun", [128, 2], F32)
    qst = p.sb("qst", [128, 4 * NSUB], F32)

    p.sems(["ld", "st", "prep", "pe", "dve", "act", "lns", "bank", "mx", "vd"])
    sld, sst = p.s("ld"), p.s("st")
    spe, sdve, sact = p.s("pe"), p.s("dve"), p.s("act")

    with nc.Block() as block:
        emit_param_loads(p, block, [(ident, id_h), (identf, idf_h),
                                    (mall, mall_h), (clipx, clip_h)])

        @block.vector
        def _(vector):
            vector.memset(onesr[:], 1.0)
            vector.memset(mrun[:], 0.0)

    emit_wquant_rows(p, None, wg_h, wgT_s, 22, ident, identf, "wg")
    emit_wquant_rows(p, None, wu_h, wuT_s, 22, ident, identf, "wu")

    ps = [p.psum(f"ps{i}", [128, 512], F32) for i in range(4)]
    ps_t = [p.psum(f"ps_t{i}", [128, 1024], BF16) for i in range(2)]
    ps_b = p.psum("ps_b", [128, 512], F32)

    with nc.Block() as block:
        pe0, dve0, act0 = p.val("pe"), p.val("dve"), p.val("act")

        @block.vector
        def _(vector):
            vector.reduce_max(gsc[0:1, 0:1], mall[0:1, :],
                              axis=AX.X).then_inc(sdve, 1)
            vector.drain()
            vector.wait_ge(sact, act0 + 1)
            vector.tensor_tensor(gsc[0:1, 2:3], gsc[0:1, 0:1], gsc[0:1, 1:2],
                                 OP.mult)
            vector.drain()
            vector.tensor_scalar(gsc[0:1, 2:3], gsc[0:1, 2:3], 1.0 / 127.0,
                                 None, OP.mult)
            vector.drain()
            vector.reciprocal(gsc[0:1, 3:4], gsc[0:1, 2:3]).then_inc(sdve, 1)
            vector.drain()

        @block.scalar
        def _(scalar):
            scalar.wait_ge(sdve, dve0 + 1)
            scalar.activation(gsc[0:1, 1:2], clipx[0:1, 0:1],
                              AF.Sigmoid).then_inc(sact, 1)

        @block.tensor
        def _(tensor):
            tensor.wait_ge(sdve, dve0 + 2)
            tensor.matmul(ps_b[:, 0:2], onesr[:], gsc[0:1, 2:4],
                          start=True, stop=True).then_inc(spe, 1)

        @block.vector
        def _(vector):
            vector.wait_ge(spe, pe0 + 1)
            vector.tensor_copy(gscb[:, 0:1], ps_b[:, 1:2])
            vector.tensor_copy(gscb[:, 1:2], ps_b[:, 0:1])
        p.bump("pe", 1)
        p.bump("dve", 2)
        p.bump("act", 1)


    xbuf = p.sb("xbuf", [128, NSUB * D], F32)
    xq = p.sb("xq", [128, NSUB * D], BF16)
    xq_fm = p.sb("xq_fm", [128, 8 * TT], BF16)
    f32s = p.sb("f32s", [128, D], F32)
    sg = p.sb("sg", [128, 22 * TT], BF16)
    su = p.sb("su", [128, 22 * TT], BF16)
    strip = p.strip_bufs()
    strips = p.stream("strips", 12)
    hsts = p.stream("aux_st")

    for i in range(ntiles):
        r0 = i * TT
        with nc.Block() as block:
            @block.sync
            def _(sync):
                sync.dma_start(out=tok_sb(xbuf[:], NSUB, D),
                               in_=tok_in(xln_h, r0)).then_inc(sld, 16)
            p.bump("ld", 16)

            @block.vector
            def _(vector):
                vector.wait_ge(sld, p.val("ld"))
                for s in range(NSUB):
                    vector.tensor_scalar(f32s[:], xbuf[:, s * D:(s + 1) * D],
                                         gscb[:, 0:1], MAGIC, OP.mult, OP.add)
                    vector.drain()
                    vector.tensor_scalar(f32s[:], f32s[:], MAGIC, -128.0,
                                         OP.subtract, OP.max)
                    vector.drain()
                    vector.tensor_scalar(xq[:, s * D:(s + 1) * D], f32s[:],
                                         127.0, gscb[:, 1:2], OP.min, OP.mult)
                    vector.drain()

        with nc.Block() as block:
            emit_tok2fm(p, block, xq, xq_fm, ident, ps_t, nchunks=8)

        if i >= 1:
            with nc.Block() as block:
                @block.scalar
                def _(scalar):
                    for pr in range(hsts.np):
                        if hsts.counts[pr]:
                            scalar.wait_ge(hsts.sems[pr], hsts.counts[pr])

                @block.vector
                def _(vector):
                    for pr in range(hsts.np):
                        if hsts.counts[pr]:
                            vector.wait_ge(hsts.sems[pr], hsts.counts[pr])

        with nc.Block() as block:
            evacs = []
            for m in range(22):
                def f(eng, pst, m=m):
                    return eng.activation(sg[:, m * TT:(m + 1) * TT], pst, AF.Silu)
                evacs.append(f)
            emit_mm_stream(p, block, wgT_s, xq_fm, 8, 22, strip, strips, ps,
                           "scalar", evacs)
            evacs = []
            for m in range(22):
                def f(eng, pst, m=m):
                    return eng.tensor_copy(su[:, m * TT:(m + 1) * TT], pst)
                evacs.append(f)
            emit_mm_stream(p, block, wuT_s, xq_fm, 8, 22, strip, strips, ps,
                           "vector", evacs)

        with nc.Block() as block:
            dve0 = p.val("dve")

            @block.vector
            def _(vector):
                vector.tensor_tensor(sg[:], sg[:], su[:], OP.mult)
                vector.drain()
                vector.reduce_max(mrun[:, 1:2], sg[:], axis=AX.X,
                                  apply_absolute_value=True)
                vector.drain()
                vector.tensor_tensor(mrun[:, 0:1], mrun[:, 0:1], mrun[:, 1:2],
                                     OP.max).then_inc(sdve, 1)
                vector.drain()
            p.bump("dve", 1)

            @block.gpsimd
            def _(gpsimd):
                gpsimd.wait_ge(sdve, p.val("dve"))
                for c in range(22):
                    hsts.issue(gpsimd, hs_h[c * 128:(c + 1) * 128, r0:r0 + TT],
                               sg[:, c * TT:(c + 1) * TT])

    with nc.Block() as block:
        pe0, dve0 = p.val("pe"), p.val("dve")

        @block.tensor
        def _(tensor):
            tensor.transpose(ps_b[0:1, 0:128], mrun[:, 0:1],
                             identf[:]).then_inc(spe, 1)

        @block.vector
        def _(vector):
            vector.wait_ge(spe, pe0 + 1)
            vector.reduce_max(mrun[0:1, 1:2], ps_b[0:1, 0:128],
                              axis=AX.X).then_inc(sdve, 1)
        p.bump("pe", 1)
        p.bump("dve", 1)

    with nc.Block() as block:
        @block.sync
        def _(sync):
            sync.wait_ge(sdve, p.val("dve"))
            sync.dma_start(out=m2_h, in_=mrun[0:1, 1:2]).then_inc(sst, 16)
            p.bump("st", 16)
            sync.wait_ge(sst, p.val("st"))
            for pr in range(hsts.np):
                if hsts.counts[pr]:
                    sync.wait_ge(hsts.sems[pr], hsts.counts[pr])

    p.pop_scope()
    if standalone:
        return p.close()


def build_li(p=None, H=None, mall_w=8):
    standalone = p is None
    if standalone:
        p = Prog()
        p.strip_bufs()
    p.H = H or {}
    p.push_scope()
    nc = p.nc
    I2 = 2816
    hs_h = p.dram("hs2", [I2, N], BF16, "ExternalInput")
    m2a_h = p.dram("m2all", [1, mall_w], F32, "ExternalInput")
    clip_h = p.dram("clipd", [1, 1], F32, "ExternalInput")
    wd_h = p.dram("wd", [D, I2], F32, "ExternalInput")
    x3_h = p.dram("x3", [N, D], F32, "ExternalInput")
    id_h = p.dram("ident", [128, 128], BF16, "ExternalInput")
    idf_h = p.dram("identf", [128, 128], F32, "ExternalInput")
    out_h = p.dram("out", [N, D], F32, "ExternalOutput")
    wdT_s = p.dram("wdT", [I2, D], BF16)

    ident = p.sb("id_sb", [128, 128], BF16)
    identf = p.sb("idf_sb", [128, 128], F32)
    onesr = p.sb("onesr", [1, 128], F32)
    m2a = p.sb("m2a_sb", [1, mall_w], F32)
    clipd = p.sb("clip_sb", [1, 1], F32)
    gsc = p.sb("gsc", [1, 8], F32)
    gscb = p.sb("gscb", [128, 2], F32)

    p.sems(["ld", "st", "prep", "pe", "dve", "act", "lns", "bank", "mx", "vd"])
    sld, sst = p.s("ld"), p.s("st")
    spe, sdve, sact = p.s("pe"), p.s("dve"), p.s("act")

    with nc.Block() as block:
        emit_param_loads(p, block, [(ident, id_h), (identf, idf_h),
                                    (m2a, m2a_h), (clipd, clip_h)])

        @block.vector
        def _(vector):
            vector.memset(onesr[:], 1.0)

    emit_wquant_rows(p, None, wd_h, wdT_s, 8, ident, identf, "wd", C=I2)

    ps = [p.psum(f"ps{i}", [128, 512], F32) for i in range(4)]
    ps_t = [p.psum(f"ps_t{i}", [128, 1024], BF16) for i in range(2)]
    ps_b = p.psum("ps_b", [128, 512], F32)

    with nc.Block() as block:
        pe0, dve0, act0 = p.val("pe"), p.val("dve"), p.val("act")

        @block.vector
        def _(vector):
            vector.reduce_max(gsc[0:1, 0:1], m2a[0:1, :],
                              axis=AX.X).then_inc(sdve, 1)
            vector.drain()
            vector.wait_ge(sact, act0 + 1)
            vector.tensor_tensor(gsc[0:1, 2:3], gsc[0:1, 0:1], gsc[0:1, 1:2],
                                 OP.mult)
            vector.drain()
            vector.tensor_scalar(gsc[0:1, 2:3], gsc[0:1, 2:3], 1.0 / 127.0,
                                 None, OP.mult)
            vector.drain()
            vector.reciprocal(gsc[0:1, 3:4], gsc[0:1, 2:3]).then_inc(sdve, 1)
            vector.drain()

        @block.scalar
        def _(scalar):
            scalar.wait_ge(sdve, dve0 + 1)
            scalar.activation(gsc[0:1, 1:2], clipd[0:1, 0:1],
                              AF.Sigmoid).then_inc(sact, 1)

        @block.tensor
        def _(tensor):
            tensor.wait_ge(sdve, dve0 + 2)
            tensor.matmul(ps_b[:, 0:2], onesr[:], gsc[0:1, 2:4],
                          start=True, stop=True).then_inc(spe, 1)

        @block.vector
        def _(vector):
            vector.wait_ge(spe, pe0 + 1)
            vector.tensor_copy(gscb[:, 0:1], ps_b[:, 1:2])
            vector.tensor_copy(gscb[:, 1:2], ps_b[:, 0:1])
        p.bump("pe", 1)
        p.bump("dve", 2)
        p.bump("act", 1)


    hbuf = p.sb("hbuf", [128, 22 * TT], BF16)
    f32s = p.sb("f32s", [128, TT], F32)
    x3b = p.sb("x3b", [128, NSUB * D], F32)
    outb = p.sb("outb", [128, NSUB * D], F32)
    y_fm = p.sb("y_fm", [128, 8 * TT], BF16)
    strip = p.strip_bufs()
    strips = p.stream("strips", 12)
    hlds = p.stream("aux_ld")

    for i in range(NTILES):
        r0 = i * TT
        with nc.Block() as block:
            st0 = p.val("st")

            @block.sync
            def _(sync):
                sync.wait_ge(sst, st0)
                for c in range(22):
                    hlds.issue(sync, hbuf[:, c * TT:(c + 1) * TT],
                               hs_h[c * 128:(c + 1) * 128, r0:r0 + TT])
                sync.dma_start(out=tok_sb(x3b[:], NSUB, D),
                               in_=tok_in(x3_h, r0)).then_inc(sld, 16)
            p.bump("ld", 16)

            @block.vector
            def _(vector):
                for pr in range(hlds.np):
                    vector.wait_ge(hlds.sems[pr], hlds.counts[pr])
                vector.wait_ge(sld, p.val("ld"))
                for c in range(22):
                    vector.tensor_scalar(f32s[:], hbuf[:, c * TT:(c + 1) * TT],
                                         gscb[:, 0:1], MAGIC, OP.mult, OP.add)
                    vector.drain()
                    vector.tensor_scalar(f32s[:], f32s[:], MAGIC, -128.0,
                                         OP.subtract, OP.max)
                    vector.drain()
                    vector.tensor_scalar(hbuf[:, c * TT:(c + 1) * TT], f32s[:],
                                         127.0, gscb[:, 1:2], OP.min, OP.mult)
                    vector.drain()

        with nc.Block() as block:
            evacs = []
            for m in range(8):
                def f(eng, pst, m=m):
                    return eng.tensor_copy(y_fm[:, m * TT:(m + 1) * TT], pst)
                evacs.append(f)
            emit_mm_stream(p, block, wdT_s, hbuf, 22, 8, strip, strips, ps,
                           "vector", evacs)

        with nc.Block() as block:
            pe0, dve0 = p.val("pe"), p.val("dve")

            @block.tensor
            def _(tensor):
                for s in range(NSUB):
                    if s >= 2:
                        tensor.wait_ge(sdve, dve0 + s - 1)
                    mm = None
                    for c in range(8):
                        mm = tensor.transpose(
                            ps_t[s % 2][:, c * 128:(c + 1) * 128],
                            y_fm[:, c * TT + s * 128:c * TT + (s + 1) * 128],
                            ident[:])
                    mm.then_inc(spe, 1)

            @block.vector
            def _(vector):
                for s in range(NSUB):
                    vector.wait_ge(spe, pe0 + s + 1)
                    vector.tensor_tensor(outb[:, s * D:(s + 1) * D],
                                         x3b[:, s * D:(s + 1) * D],
                                         ps_t[s % 2][:, 0:D],
                                         OP.add).then_inc(sdve, 1)

            @block.sync
            def _(sync):
                sync.wait_ge(sdve, dve0 + NSUB)
                sync.dma_start(out=tok_in(out_h, r0),
                               in_=tok_sb(outb[:], NSUB, D)).then_inc(sst, 16)
            p.bump("pe", NSUB)
            p.bump("dve", NSUB)
            p.bump("st", 16)

    with nc.Block() as block:
        @block.sync
        def _(sync):
            sync.wait_ge(sst, p.val("st"))

    p.pop_scope()
    if standalone:
        return p.close()


def lh_inmaps(xln_l, mall, ins):
    com = {
        "mall": np.ascontiguousarray(np.asarray(mall, np.float32).reshape(1, 8)),
        "clipx": np.ascontiguousarray(np.asarray(ins["L1_ffn_clip_x"],
                                                 np.float32).reshape(1, 1)),
        "wg": np.ascontiguousarray(ins["L1_ffn_w_gate"]),
        "wu": np.ascontiguousarray(ins["L1_ffn_w_up"]),
        "ident": eye_bf16(),
        "identf": np.eye(128, dtype=np.float32),
    }
    return [dict(com, xln=np.ascontiguousarray(xln_l[c])) for c in range(NCORES)]


def li_inmaps(hs_l, m2all, x3_l, ins):
    com = {
        "m2all": np.ascontiguousarray(np.asarray(m2all, np.float32).reshape(1, 8)),
        "clipd": np.ascontiguousarray(np.asarray(ins["L1_ffn_clip_down"],
                                                 np.float32).reshape(1, 1)),
        "wd": np.ascontiguousarray(ins["L1_ffn_w_down"]),
        "ident": eye_bf16(),
        "identf": np.eye(128, dtype=np.float32),
    }
    return [dict(com, hs2=np.ascontiguousarray(hs_l[c]),
                 x3=np.ascontiguousarray(x3_l[c])) for c in range(NCORES)]


def build_la(ntiles=NTILES, p=None, H=None):
    standalone = p is None
    if standalone:
        p = Prog()
        p.strip_bufs()
    p.H = H or {}
    p.push_scope()
    nc = p.nc
    x_h = p.dram("x", [N, D], F32, "ExternalInput")
    w_h = {}
    for w in ["wq", "wk", "wv", "wo"]:
        w_h[w] = p.dram(w, [D, D], F32, "ExternalInput")
    b_h = {}
    for b in ["bq", "bk", "bv", "bo", "ln1w", "ln1b"]:
        b_h[b] = p.dram(b, [128, 8], F32, "ExternalInput")
    id_h = p.dram("ident", [128, 128], BF16, "ExternalInput")
    bo_h = p.dram("bones", [128, 128], BF16, "ExternalInput")
    xa_h = p.dram("xa", [N, D], F32, "ExternalOutput")
    wo_s = p.dram("wo_bf", [D, D], BF16)

    stats = p.sb("stats", [128, 16 * NSUB], F32)
    eps = p.sb("eps", [128, 1], F32)
    ident = p.sb("id_sb", [128, 128], BF16)
    bones = p.sb("bones_sb", [128, 128], BF16)
    par = {}
    for b in b_h:
        par[b] = p.sb(b + "_sb", [128, 8], F32)
    wsb = {}
    for w in ["wq", "wk", "wv"]:
        wsb[w] = p.sb(w + "_sb", [128, 8 * D], BF16)

    p.sems(["ld", "st", "prep", "pe", "dve", "act", "lns", "bank", "mx", "vd"])
    sld, sst = p.s("ld"), p.s("st")
    spe, sdve, sact = p.s("pe"), p.s("dve"), p.s("act")
    sbank, smx = p.s("bank"), p.s("mx")

    with nc.Block() as block:
        emit_param_loads(p, block, [(ident, id_h), (bones, bo_h)] +
                         [(par[b], b_h[b]) for b in b_h])

        @block.vector
        def _(vector):
            vector.memset(eps[:], 1e-5)

    with p.scope() as prep_scope:
        with nc.Block() as block:
            sts, _nh = emit_weight_prep(p, block, [
                {"src": w_h["wq"], "dst_sb": wsb["wq"]},
                {"src": w_h["wk"], "dst_sb": wsb["wk"]},
                {"src": w_h["wv"], "dst_sb": wsb["wv"]},
                {"src": w_h["wo"], "dst_hbm": wo_s},
            ], prep_scope)
        with nc.Block() as block:
            @block.sync
            def _(sync):
                for pr in range(sts.np):
                    if sts.counts[pr]:
                        sync.wait_ge(sts.sems[pr], sts.counts[pr])

    xbuf = p.sb("xbuf", [128, NSUB * D], F32)
    qx = p.sb("qx", [128, NSUB * D], BF16)
    xn2 = p.sb("xn2", [128, NSUB * D], BF16)
    xn_fm = p.sb("xn_fm", [128, 8 * TT], BF16)
    fq = p.sb("fq", [128, 8 * TT], BF16)
    fk = p.sb("fk", [128, 16 * TT], BF16)
    fkd = p.sb("fkd", [128, 15 * TT], BF16)
    v_fm = p.sb("v_fm", [128, 8 * TT], BF16)
    veh = p.sb("veh", [128, NSUB * 2048], BF16)
    tmin = p.sb("tmin", [128, TT], F32)
    texp = p.sb("texp", [128, TT], BF16)
    Pb = [p.sb(f"P{i}", [128, 8 * TT], BF16) for i in range(2)]
    g_sb = p.sb("g_sb", [16, 16 * TT], BF16)
    G_t = p.sb("G_t", [128, NSUB * 256], BF16)
    qk = p.sb("qk", [128, NSUB * 16], F32)
    rcp = p.sb("rcp", [128, NSUB * 16], F32)
    Gn = p.sb("Gn", [128, NSUB * 256], BF16)
    qkv = p.sb("qkv", [128, NSUB * 1024], BF16)
    term = v_fm
    qkvhe = fk
    y_fm = fq
    strip = p.strip_bufs()
    strips = p.stream("strips", 12)

    ps = [p.psum(f"ps{i}", [128, 512], F32) for i in range(4)]
    ps_g = [p.psum(f"psg{i}", [16, 512], F32) for i in range(2)]
    ps_t = [p.psum(f"ps_t{i}", [128, 1024], BF16) for i in range(2)]

    for i in range(ntiles):
        r0 = i * TT
        with nc.Block() as block:
            st0 = p.val("st")

            @block.sync
            def _(sync):
                sync.wait_ge(sst, st0)
                sync.dma_start(out=tok_sb(xbuf[:], NSUB, D),
                               in_=tok_in(x_h, r0)).then_inc(sld, 16)

            p.bump("ld", 16)
            emit_ln(p, block, xbuf, qx, stats, eps,
                    wait_sem=sld, wait_val=p.val("ld"), done_inc=(sdve, 1))
            qx_done = p.bump("dve")
            emit_ln(p, block, qx, xn2, stats, eps,
                    wait_sem=sdve, wait_val=qx_done)

        with nc.Block() as block:
            emit_tok2fm(p, block, xn2, xn_fm, ident, ps_t, nchunks=8,
                        affine=(par["ln1w"], par["ln1b"], 0))

        with nc.Block() as block:
            pe0 = p.val("pe")
            bank0 = p.val("bank")
            act0 = p.val("act")
            dve0 = p.val("dve")
            mx0 = p.val("mx")

            @block.tensor
            def _(tensor):
                g = 0
                for wname in ["wq", "wk", "wv"]:
                    wt = wsb[wname]
                    for m in range(8):
                        if g >= 4:
                            tensor.wait_ge(sbank, bank0 + g - 3)
                        mm = None
                        for k in range(8):
                            mm = tensor.matmul(
                                ps[g % 4][:],
                                wt[:, k * D + m * 128:k * D + (m + 1) * 128],
                                xn_fm[:, k * TT:(k + 1) * TT],
                                start=(k == 0), stop=(k == 7))
                        mm.then_inc(spe, 1)
                        g += 1

            @block.vector
            def _(vector):
                g = 0
                nact = 0
                nmx = 0
                for pi, (wname, dst) in enumerate([("bq", fq), ("bk", fk)]):
                    for m in range(8):
                        vector.wait_ge(spe, pe0 + g + 1)
                        if g >= 1:
                            vector.wait_ge(sact, act0 + nact)
                        vector.tensor_scalar(tmin[:, 0:TT],
                                             ps[g % 4][:], par[wname][:, m:m + 1],
                                             0.0, OP.add, OP.min).then_inc(sdve, 1)
                        vector.tensor_scalar(dst[:, m * TT:(m + 1) * TT],
                                             ps[g % 4][:], par[wname][:, m:m + 1],
                                             0.0, OP.add, OP.max).then_inc(sbank, 1)
                        vector.wait_ge(sact, act0 + nact + 1)
                        vector.wait_ge(sbank, bank0 + g + 1)
                        vector.tensor_tensor(dst[:, m * TT:(m + 1) * TT],
                                             dst[:, m * TT:(m + 1) * TT],
                                             texp[:, 0:TT],
                                             OP.add).then_inc(smx, 1)
                        nact += 1
                        nmx += 1
                        g += 1
                for m in range(8):
                    vector.wait_ge(spe, pe0 + g + 1)
                    vector.tensor_scalar(v_fm[:, m * TT:(m + 1) * TT], ps[g % 4][:],
                                         par["bv"][:, m:m + 1], None,
                                         OP.add).then_inc(sbank, 1)
                    g += 1

            @block.scalar
            def _(scalar):
                for g in range(16):
                    scalar.wait_ge(sdve, dve0 + g + 1)
                    if g >= 1:
                        scalar.wait_ge(smx, mx0 + g)
                    scalar.activation(texp[:, 0:TT], tmin[:, 0:TT],
                                      AF.Exp).then_inc(sact, 1)

            p.bump("pe", 24)
            p.bump("bank", 24)
            p.bump("dve", 16)
            p.bump("act", 16)
            p.bump("mx", 16)

        with nc.Block() as block:
            pe0, dve0 = p.val("pe"), p.val("dve")

            @block.tensor
            def _(tensor):
                t = 0
                for m in range(8):
                    for s in range(NSUB):
                        if t >= 2:
                            tensor.wait_ge(sdve, dve0 + 2 * (t - 1))
                        tensor.transpose(ps_t[t % 2][:, 0:128],
                                         v_fm[:, m * TT + s * 128:m * TT + (s + 1) * 128],
                                         ident[:]).then_inc(spe, 1)
                        t += 1

            @block.scalar
            def _(scalar):
                t = 0
                for m in range(8):
                    for s in range(NSUB):
                        scalar.wait_ge(spe, pe0 + t + 1)
                        src = bass.AP(ps_t[t % 2], 0, [[1024, 128], [64, 2], [1, 64]])
                        for dup in range(2):
                            dst = bass.AP(veh, s * 2048 + 2 * m + 16 * dup,
                                          [[NSUB * 2048, 128], [1, 2], [32, 64]])
                            scalar.copy(dst, src).then_inc(sdve, 1)
                        t += 1

            p.bump("pe", 32)
            p.bump("dve", 64)

        with nc.Block() as block:
            pe0, dve0, act0 = p.val("pe"), p.val("dve"), p.val("act")

            @block.vector
            def _(vector):
                vector.tensor_copy(fk[:, 8 * TT:16 * TT], fk[:, 0:8 * TT])
                for c in range(8):
                    c3 = (c + 1) % 8
                    vector.tensor_copy(fkd[0:64, c * TT:(c + 1) * TT],
                                       fk[64:128, c * TT:(c + 1) * TT])
                    vector.tensor_copy(fkd[64:128, c * TT:(c + 1) * TT],
                                       fk[0:64, c3 * TT:(c3 + 1) * TT])
                vector.drain()
                vector.tensor_copy(fkd[:, 8 * TT:15 * TT], fkd[:, 0:7 * TT])
                vector.drain()
                for o in range(16):
                    if o >= 2:
                        vector.wait_ge(spe, pe0 + o - 1)
                    src = fk if o % 2 == 0 else fkd
                    off = (o // 2) * TT
                    vector.tensor_tensor(Pb[o % 2][:], fq[:, 0:8 * TT],
                                         src[:, off:off + 8 * TT],
                                         OP.mult).then_inc(sdve, 1)

            @block.tensor
            def _(tensor):
                for o in range(16):
                    tensor.wait_ge(sdve, dve0 + o + 1)
                    if o >= 2:
                        tensor.wait_ge(sact, act0 + o - 1)
                    mm = None
                    for c in range(8):
                        mm = tensor.matmul(ps_g[o % 2][:],
                                           bones[:, c * 16:(c + 1) * 16],
                                           Pb[o % 2][:, c * TT:(c + 1) * TT],
                                           start=(c == 0), stop=(c == 7))
                    mm.then_inc(spe, 1)

            @block.scalar
            def _(scalar):
                for o in range(16):
                    scalar.wait_ge(spe, pe0 + o + 1)
                    scalar.copy(g_sb[:, o * TT:(o + 1) * TT],
                                ps_g[o % 2][:]).then_inc(sact, 1)

            p.bump("dve", 16)
            p.bump("pe", 16)
            p.bump("act", 16)

        with nc.Block() as block:
            pe0 = p.val("pe")

            @block.tensor
            def _(tensor):
                mm = None
                for o in range(16):
                    for s in range(NSUB):
                        mm = tensor.transpose(
                            ps_t[0][:, s * 256 + o * 16:s * 256 + (o + 1) * 16],
                            g_sb[:, o * TT + s * 128:o * TT + (s + 1) * 128],
                            ident[0:16, 0:16])
                mm.then_inc(spe, 1)

            @block.vector
            def _(vector):
                vector.wait_ge(spe, pe0 + 1)
                vector.tensor_copy(G_t[:], ps_t[0][:])
                vector.drain()
                for s in range(NSUB):
                    red_in = bass.AP(G_t, s * 256,
                                     [[NSUB * 256, 128], [1, 16], [16, 16]])
                    vector.tensor_reduce(qk[:, s * 16:(s + 1) * 16], red_in,
                                         axis=AX.X, op=OP.add)
                vector.drain()
                vector.tensor_scalar(qk[:], qk[:], 1e-6, None, OP.add)
                vector.drain()
                vector.reciprocal(rcp[:], qk[:])
                vector.drain()
                rcp_ap = bass.AP(rcp, 0, [[NSUB * 16, 128], [16, NSUB],
                                          [0, 16], [1, 16]])
                vector.tensor_tensor(Gn[:], G_t[:], rcp_ap, OP.mult)

            p.bump("pe", 1)

        with nc.Block() as block:
            @block.vector
            def _(vector):
                svd = p.s("vd")
                for o in range(16):
                    g_ap = bass.AP(Gn, o * 16, [[NSUB * 256, 128], [256, NSUB],
                                                [0, 64], [1, 16]])
                    v_ap = bass.AP(veh, o, [[NSUB * 2048, 128], [2048, NSUB],
                                            [32, 64], [1, 16]])
                    if o == 0:
                        vector.tensor_tensor(qkv[:], g_ap, v_ap,
                                             OP.mult).then_inc(svd, 1)
                        p.bump("vd")
                    else:
                        vector.wait_ge(svd, p.val("vd"))
                        vector.tensor_tensor(term[:], g_ap, v_ap,
                                             OP.mult).then_inc(svd, 1)
                        p.bump("vd")
                        vector.wait_ge(svd, p.val("vd"))
                        if o < 15:
                            vector.tensor_tensor(qkv[:], qkv[:], term[:],
                                                 OP.add).then_inc(svd, 1)
                        else:
                            out_ap = bass.AP(qkvhe, 0,
                                             [[qkvhe.shape[1], 128], [1024, NSUB],
                                              [1, 64], [64, 16]])
                            vector.tensor_tensor(out_ap, qkv[:], term[:],
                                                 OP.add).then_inc(svd, 1)
                        p.bump("vd")

        with nc.Block() as block:
            emit_tok2fm(p, block, qkvhe, xn_fm, ident, ps_t, nchunks=8)

        with nc.Block() as block:
            evacs = []
            for m in range(8):
                def f(eng, pst, m=m):
                    return eng.tensor_scalar(y_fm[:, m * TT:(m + 1) * TT], pst,
                                             par["bo"][:, m:m + 1], None, OP.add)
                evacs.append(f)
            emit_mm_stream(p, block, wo_s, xn_fm, 8, 8, strip, strips, ps,
                           "vector", evacs)

        with nc.Block() as block:
            pe0, dve0, st0 = p.val("pe"), p.val("dve"), p.val("st")

            @block.tensor
            def _(tensor):
                for s in range(NSUB):
                    if s >= 2:
                        tensor.wait_ge(sdve, dve0 + s - 1)
                    mm = None
                    for c in range(8):
                        mm = tensor.transpose(
                            ps_t[s % 2][:, c * 128:(c + 1) * 128],
                            y_fm[:, c * TT + s * 128:c * TT + (s + 1) * 128],
                            ident[:])
                    mm.then_inc(spe, 1)

            @block.vector
            def _(vector):
                vector.wait_ge(sst, st0)
                for s in range(NSUB):
                    vector.wait_ge(spe, pe0 + s + 1)
                    vector.tensor_tensor(xbuf[:, s * D:(s + 1) * D],
                                         qx[:, s * D:(s + 1) * D],
                                         ps_t[s % 2][:, 0:D],
                                         OP.add).then_inc(sdve, 1)

            @block.sync
            def _(sync):
                sync.wait_ge(sdve, dve0 + NSUB)
                sync.dma_start(out=tok_in(xa_h, r0),
                               in_=tok_sb(xbuf[:], NSUB, D)).then_inc(sst, 16)

            p.bump("pe", NSUB)
            p.bump("dve", NSUB)
            p.bump("st", 16)

    with nc.Block() as block:
        @block.sync
        def _(sync):
            sync.wait_ge(sst, p.val("st"))

    p.pop_scope()
    if standalone:
        return p.close()


def la_inmaps(x_full, ins):
    bones = np.zeros((128, 128), dtype=BF)
    for c in range(8):
        for r in range(128):
            bones[r, c * 16 + 2 * c + r // 64] = 1
    com = {
        "wq": np.ascontiguousarray(ins["L0_attn_wq"]),
        "wk": np.ascontiguousarray(ins["L0_attn_wk"]),
        "wv": np.ascontiguousarray(ins["L0_attn_wv"]),
        "wo": np.ascontiguousarray(ins["L0_attn_wo"]),
        "bq": vec_pm(ins["L0_attn_bq"]),
        "bk": vec_pm(ins["L0_attn_bk"]),
        "bv": vec_pm(ins["L0_attn_bv"]),
        "bo": vec_pm(ins["L0_attn_bo"]),
        "ln1w": vec_pm(ins["L0_attn_ln1_w"]),
        "ln1b": vec_pm(ins["L0_attn_ln1_b"]),
        "ident": eye_bf16(),
        "bones": bones,
    }
    return [dict(com, x=np.ascontiguousarray(x_full[c])) for c in range(NCORES)]


def build_fused():
    p = Prog()
    nc = p.nc
    p.strip_bufs()
    E = lambda n, sh, dt: p.dram(n, sh, dt, "ExternalInput")
    I = lambda n, sh, dt: p.dram(n, sh, dt)

    x = E("x", [N, D], F32)
    out = p.dram("out", [N, D], F32, "ExternalOutput")
    ident = E("ident", [128, 128], BF16)
    identf = E("identf", [128, 128], F32)

    ew = {}
    for n, sh in [("wq", [D, D]), ("wk", [D, D]), ("wv", [D, D]), ("wo", [D, D]),
                  ("bq", [128, 8]), ("bk", [128, 8]), ("bv", [128, 8]),
                  ("bo", [128, 8]), ("ln1w", [128, 8]), ("ln1b", [128, 8]),
                  ("fcw", [D, 4096]), ("fcb", [128, 32]), ("pjw", [4096, D]),
                  ("pjb", [128, 8]), ("lnw", [128, 8]), ("lnb", [128, 8]),
                  ("w1t", [D, 4096]), ("w2t", [4096, D]),
                  ("convw", [128, 96]), ("convb", [128, 32]),
                  ("ipw", [D, 4352]), ("ipwdt", [D, 32]), ("dtb", [32, 1]),
                  ("convw2", [128, 72]), ("convb2", [128, 18]),
                  ("alog", [1, 32]), ("dvec", [1, 32]),
                  ("tri", [128, 128]), ("tris", [128, 128]),
                  ("nw", [1, 2048]), ("opw", [2048, D]),
                  ("clipx", [1, 1]), ("wg", [2816, D]), ("wu", [2816, D]),
                  ("clipd", [1, 1]), ("wd", [D, 2816])]:
        ew[n] = E(n, sh, F32)
    bones = E("bones", [128, 128], BF16)

    xa = I("i_xa", [N, D], F32)
    x1 = I("i_x1", [N, D], F32)
    hs = I("i_hs", [4096, N], BF16)
    x2 = I("i_x2", [N, D], F32)
    zs = I("i_zs", [2048, N], BF16)
    xbcs = I("i_xbcs", [2304, N], BF16)
    dts = I("i_dts", [N, 32], F32)
    xpt = I("i_xpt", [N, 2048], BF16)
    bts = I("i_bts", [N, 128], BF16)
    bfs = I("i_bfs", [128, N], BF16)
    cfs = I("i_cfs", [128, N], BF16)
    ys = I("i_ys", [N, 2048], F32)
    x3 = I("i_x3", [N, D], F32)
    xln = I("i_xln", [N, D], F32)
    mloc = I("i_mloc", [1, 1], F32)
    mlg = I("i_mlg", [1, 1], F32)
    hs2 = I("i_hs2", [2816, N], BF16)
    m2loc = I("i_m2loc", [1, 1], F32)
    m2g = I("i_m2g", [1, 1], F32)

    com = {"ident": ident, "identf": identf}

    build_la(p=p, H=dict(com, x=x, xa=xa, bones=bones,
                         **{k: ew[k] for k in ["wq", "wk", "wv", "wo", "bq",
                                               "bk", "bv", "bo", "ln1w", "ln1b"]}))
    build_lb(p=p, H=dict(com, xa=xa, xo=x, y=x1,
                         **{k: ew[k] for k in ["fcw", "fcb", "pjw", "pjb",
                                               "lnw", "lnb"]}))
    build_lc1(p=p, H=dict(com, x1=x1, hs=hs, w1t=ew["w1t"]))
    build_lc2(p=p, H=dict(com, hs=hs, x1=x1, x2=x2, w2t=ew["w2t"],
                          convw=ew["convw"], convb=ew["convb"]))
    build_ld(p=p, H=dict(com, x2=x2, zs=zs, xbcs=xbcs, dts=dts,
                         ipw=ew["ipw"], ipwdt=ew["ipwdt"], dtb=ew["dtb"]))
    build_ld2(p=p, H=dict(com, xbcs=xbcs, xpt=xpt, bfs=bfs, cfs=cfs, bts=bts,
                          convw=ew["convw2"], convb=ew["convb2"]))
    build_le(p=p, H=dict(com, xpt=xpt, bts=bts, bfs=bfs, cfs=cfs, dts=dts,
                         ys=ys, alog=ew["alog"], dvec=ew["dvec"],
                         tri=ew["tri"], tris=ew["tris"]))
    build_lf(p=p, H=dict(com, ys=ys, zs=zs, x2=x2, x3=x3, xln=xln, mloc=mloc,
                         nw=ew["nw"], opw=ew["opw"]))

    ccsem = p.sem("cc")
    with nc.Block() as block:
        @block.gpsimd
        def _(gpsimd):
            gpsimd.collective_compute(
                "AllReduce", OP.max, replica_groups=[list(range(NCORES))],
                ins=[mloc.opt()], outs=[mlg.opt()]).then_inc(ccsem, 1)
            gpsimd.wait_ge(ccsem, p.bump("cc"))

    build_lh(p=p, H=dict(com, xln=xln, mall=mlg, hs2=hs2, m2loc=m2loc,
                         clipx=ew["clipx"], wg=ew["wg"], wu=ew["wu"]),
             mall_w=1)

    with nc.Block() as block:
        @block.gpsimd
        def _(gpsimd):
            gpsimd.collective_compute(
                "AllReduce", OP.max, replica_groups=[list(range(NCORES))],
                ins=[m2loc.opt()], outs=[m2g.opt()]).then_inc(ccsem, 1)
            gpsimd.wait_ge(ccsem, p.bump("cc"))

    build_li(p=p, H=dict(com, hs2=hs2, m2all=m2g, x3=x3, out=out,
                         clipd=ew["clipd"], wd=ew["wd"]),
             mall_w=1)

    return p.close()


def fused_inmaps(ins):
    x = np.ascontiguousarray(np.asarray(ins["L0_attn_q_x"], np.float32))
    bones = np.zeros((128, 128), dtype=BF)
    for c in range(8):
        for r in range(128):
            bones[r, c * 16 + 2 * c + r // 64] = 1
    r = np.arange(128)
    ipw = np.asarray(ins["L1_attn_in_proj_w"])
    cwr = np.asarray(ins["L1_attn_conv_w"])[::-1]
    com = {
        "ident": eye_bf16(), "identf": np.eye(128, dtype=np.float32),
        "bones": bones,
        "wq": np.ascontiguousarray(ins["L0_attn_wq"]),
        "wk": np.ascontiguousarray(ins["L0_attn_wk"]),
        "wv": np.ascontiguousarray(ins["L0_attn_wv"]),
        "wo": np.ascontiguousarray(ins["L0_attn_wo"]),
        "bq": vec_pm(ins["L0_attn_bq"]), "bk": vec_pm(ins["L0_attn_bk"]),
        "bv": vec_pm(ins["L0_attn_bv"]), "bo": vec_pm(ins["L0_attn_bo"]),
        "ln1w": vec_pm(ins["L0_attn_ln1_w"]), "ln1b": vec_pm(ins["L0_attn_ln1_b"]),
        "fcw": np.ascontiguousarray(ins["L0_attn_fc_w"]),
        "fcb": vec_pm(ins["L0_attn_fc_b"]),
        "pjw": np.ascontiguousarray(ins["L0_attn_proj_w"]),
        "pjb": vec_pm(ins["L0_attn_proj_b"]),
        "lnw": vec_pm(ins["L0_attn_ln2_w"]), "lnb": vec_pm(ins["L0_attn_ln2_b"]),
        "w1t": np.ascontiguousarray(ins["L0_ffn_w1"].T),
        "w2t": np.ascontiguousarray(ins["L0_ffn_w2"].T),
        "convw": np.ascontiguousarray(
            np.asarray(ins["L0_ffn_conv_w"])[:, 0, :].reshape(32, 128, 3)
            .transpose(1, 0, 2).reshape(128, 96)),
        "convb": vec_pm(ins["L0_ffn_conv_b"]),
        "ipw": np.ascontiguousarray(ipw[:, 0:4352]),
        "ipwdt": np.ascontiguousarray(ipw[:, 4352:4384]),
        "dtb": np.ascontiguousarray(np.asarray(ins["L1_attn_dt_bias"]).reshape(32, 1)),
        "convw2": np.ascontiguousarray(cwr.T.reshape(18, 128, 4)
                                       .transpose(1, 0, 2).reshape(128, 72)),
        "convb2": vec_pm(ins["L1_attn_conv_b"]),
        "alog": np.ascontiguousarray(np.asarray(ins["L1_attn_A_log"]).reshape(1, 32)),
        "dvec": np.ascontiguousarray(np.asarray(ins["L1_attn_D"]).reshape(1, 32)),
        "tri": (r[:, None] <= r[None, :]).astype(np.float32),
        "tris": (r[:, None] > r[None, :]).astype(np.float32),
        "nw": np.ascontiguousarray(np.asarray(ins["L1_attn_norm_w"]).reshape(1, 2048)),
        "opw": np.ascontiguousarray(ins["L1_attn_out_proj_w"]),
        "clipx": np.ascontiguousarray(np.asarray(ins["L1_ffn_clip_x"],
                                                 np.float32).reshape(1, 1)),
        "wg": np.ascontiguousarray(ins["L1_ffn_w_gate"]),
        "wu": np.ascontiguousarray(ins["L1_ffn_w_up"]),
        "clipd": np.ascontiguousarray(np.asarray(ins["L1_ffn_clip_down"],
                                                 np.float32).reshape(1, 1)),
        "wd": np.ascontiguousarray(ins["L1_ffn_w_down"]),
    }
    return [dict(com, x=np.ascontiguousarray(x[c])) for c in range(NCORES)]


def _run(nc, in_maps):
    res = run_bass_kernel_spmd(nc, in_maps, core_ids=list(range(NCORES)))
    return res.results


def kernel(**ins):
    nc = build_fused()
    res = _run(nc, fused_inmaps(ins))
    return np.stack([np.asarray(c["out"], np.float32) for c in res], axis=0)


def kernel_unfused(**ins):
    x = np.ascontiguousarray(np.asarray(ins["L0_attn_q_x"], np.float32))

    r = _run(build_la(), la_inmaps(x, ins))
    xa = [c["xa"] for c in r]

    r = _run(build_lb(), lb_inmaps(xa, x, ins))
    x1 = [c["y"] for c in r]

    r = _run(build_lc1(), lc1_inmaps(x1, ins))
    hs = [c["hs"] for c in r]

    r = _run(build_lc2(), lc2_inmaps(hs, x1, ins))
    x2 = [c["x2"] for c in r]

    r = _run(build_ld(), ld_inmaps(x2, ins))
    zs = [c["zs"] for c in r]
    xbcs = [c["xbcs"] for c in r]
    dts = [c["dts"] for c in r]

    r = _run(build_ld2(), ld2_inmaps(xbcs, ins))
    xpt = [c["xpt"] for c in r]
    bts = [c["bts"] for c in r]
    bfs = [c["bfs"] for c in r]
    cfs = [c["cfs"] for c in r]

    r = _run(build_le(), le_inmaps(xpt, bts, bfs, cfs, dts, ins))
    ys = [c["ys"] for c in r]

    r = _run(build_lf(), lf_inmaps(ys, zs, x2, ins))
    x3 = [c["x3"] for c in r]
    xln = [c["xln"] for c in r]
    mall = [np.asarray(c["mloc"]).ravel()[0] for c in r]

    r = _run(build_lh(), lh_inmaps(xln, mall, ins))
    hs2 = [c["hs2"] for c in r]
    m2all = [np.asarray(c["m2loc"]).ravel()[0] for c in r]

    r = _run(build_li(), li_inmaps(hs2, m2all, x3, ins))
    return np.stack([np.asarray(c["out"], np.float32) for c in r], axis=0)


# version 28
# speedup vs baseline: 1.0262x; 1.0262x over previous
import numpy as np
import ml_dtypes
from contextlib import ExitStack
import concourse.bass as bass
import concourse.mybir as mybir
from concourse.bass_utils import run_bass_kernel_spmd
from concourse.alu_op_type import AluOpType as OP

F32 = mybir.dt.float32
BF16 = mybir.dt.bfloat16
I32 = mybir.dt.int32
AF = mybir.ActivationFunctionType
AX = mybir.AxisListType

NCORES = 8
N = 4096
D = 1024
TT = 512
NTILES = N // TT
NSUB = TT // 128

BF = ml_dtypes.bfloat16


def vec_pm(v, p=128):
    return np.ascontiguousarray(np.asarray(v).reshape(-1, p).T)


def eye_bf16():
    return np.eye(128, dtype=BF)


def tok_in(hbm_ap, r0, tt=TT):
    return hbm_ap[r0:r0 + tt, :].rearrange("(s p) d -> p s d", p=128)


def tok_sb(sb_ap, nsub, d):
    return sb_ap.rearrange("p (s d) -> p s d", s=nsub)


class Prog:
    def __init__(self):
        self.nc = bass.Bass("TRN2", target_bir_lowering=False)
        self.stack = ExitStack()
        self.cnt = {}
        self.H = {}
        self._scopes = []
        self._uid = 0
        self._streams = {}

    def _nm(self, name):
        self._uid += 1
        return f"{name}__{self._uid}"

    def _tgt(self):
        return self._scopes[-1] if self._scopes else self.stack

    def push_scope(self):
        self._scopes.append(ExitStack())

    def pop_scope(self):
        self._scopes.pop().close()

    def dram(self, name, shape, dt, kind=None):
        if name in self.H:
            return self.H[name]
        if kind:
            return self.nc.dram_tensor(name, shape, dt, kind=kind).ap()
        return self.nc.dram_tensor(name, shape, dt).ap()

    def strip_bufs(self):
        if not hasattr(self, "_strips"):
            self._strips = [
                self.stack.enter_context(
                    self.nc.sbuf_tensor(self._nm(f"gstrip{i}"), [128, 512], BF16))
                for i in range(6)]
        return self._strips

    def stream(self, name, nparity=2):
        if name not in self._streams:
            self._streams[name] = DmaStream(self, name, nparity)
        st = self._streams[name]
        assert st.np == nparity
        return st

    def sb(self, name, shape, dt):
        return self._tgt().enter_context(
            self.nc.sbuf_tensor(self._nm(name), shape, dt))

    def scope(self):
        return ExitStack()

    def sb_in(self, scope, name, shape, dt):
        return scope.enter_context(self.nc.sbuf_tensor(self._nm(name), shape, dt))

    def psum(self, name, shape, dt):
        return self._tgt().enter_context(
            self.nc.psum_tensor(self._nm(name), shape, dt))

    def sem(self, name):
        if name in self.cnt:
            return self.cnt[name][0]
        s = self.stack.enter_context(self.nc.semaphore(name))
        self.cnt[name] = [s, 0]
        return s

    def sems(self, names):
        for n in names:
            self.sem(n)

    def s(self, name):
        return self.cnt[name][0]

    def bump(self, name, n=1):
        self.cnt[name][1] += n
        return self.cnt[name][1]

    def val(self, name):
        return self.cnt[name][1]

    def close(self):
        self.stack.close()
        return self.nc


def ap3(t, offset, dims):
    F = t.shape[1]
    return bass.AP(t, offset, [[F, 128]] + dims)


class DmaStream:

    def __init__(self, p, name, nparity=2):
        self.sems = [p.stack.enter_context(p.nc.semaphore(f"{name}{i}"))
                     for i in range(nparity)]
        self.counts = [0] * nparity
        self.np = nparity
        self.i = 0

    def issue(self, eng, out, in_, **kw):
        par = self.i % self.np
        self.i += 1
        self.counts[par] += 16
        eng.dma_start(out=out, in_=in_, **kw).then_inc(self.sems[par], 16)
        return (par, self.counts[par])

    def tok(self, idx):
        par = idx % self.np
        n16 = (idx // self.np + 1) * 16
        return (par, n16)

    def wait(self, eng, token):
        eng.wait_ge(self.sems[token[0]], token[1])

    def prefix_wait(self, eng, M):
        for pr in range(self.np):
            n = len(range(pr, M, self.np))
            if n:
                eng.wait_ge(self.sems[pr], 16 * n)


def emit_ln(p, block, x_tok, xn_bf, stats, eps_tile,
            nsub=NSUB, fw=D, wait_sem=None, wait_val=0, done_inc=None):
    n = nsub
    nh = fw // 512
    assert nh * 6 <= 12
    mv0 = 12 * n
    rs0 = 14 * n
    lns = p.s("lns")
    b0 = p.val("lns")

    @block.vector
    def _(vector):
        if wait_sem is not None:
            vector.wait_ge(wait_sem, wait_val)
        for s in range(n):
            for h in range(nh):
                vector.bn_stats(stats[:, 12 * s + 6 * h:12 * s + 6 * h + 6],
                                x_tok[:, s * fw + h * 512:s * fw + (h + 1) * 512])
        vector.drain()
        last = None
        for s in range(n):
            last = vector.bn_aggr(stats[:, mv0 + 2 * s:mv0 + 2 * s + 2],
                                  stats[:, 12 * s:12 * s + 6 * nh])
        last.then_inc(lns, 1)
        vector.wait_ge(lns, b0 + 2)
        vector.reciprocal(stats[:, rs0:rs0 + n],
                          ap3(stats, mv0 + 1, [[2, n]]))
        vector.drain()
        last = None
        for s in range(n):
            last = vector.tensor_scalar(xn_bf[:, s * fw:(s + 1) * fw],
                                        x_tok[:, s * fw:(s + 1) * fw],
                                        stats[:, mv0 + 2 * s:mv0 + 2 * s + 1],
                                        stats[:, rs0 + s:rs0 + s + 1],
                                        OP.subtract, OP.mult)
        if done_inc is not None:
            last.then_inc(done_inc[0], done_inc[1])

    @block.scalar
    def _(scalar):
        scalar.wait_ge(lns, b0 + 1)
        scalar.activation(ap3(stats, mv0 + 1, [[2, n]]),
                          ap3(stats, mv0 + 1, [[2, n]]),
                          AF.Sqrt, bias=eps_tile[:, 0:1]).then_inc(lns, 1)

    p.bump("lns", 2)


def emit_tok2fm(p, block, src_tok_bf, fm_out, ident, ps_tr, nchunks,
                nsub=NSUB, affine=None, fm_chunk_w=TT, wait=None):
    F = nchunks * 128
    T = nsub * 128
    spe, sdve = p.s("pe"), p.s("dve")
    pe0, dve0 = p.val("pe"), p.val("dve")

    @block.tensor
    def _(tensor):
        if wait is not None:
            tensor.wait_ge(wait[0], wait[1])
        for c in range(nchunks):
            if c >= 2:
                tensor.wait_ge(sdve, dve0 + c - 1)
            mm = None
            for s in range(nsub):
                mm = tensor.transpose(ps_tr[c % 2][:, s * 128:(s + 1) * 128],
                                      src_tok_bf[:, s * F + c * 128:s * F + (c + 1) * 128],
                                      ident[:])
            mm.then_inc(spe, 1)

    @block.scalar
    def _(scalar):
        for c in range(nchunks):
            scalar.wait_ge(spe, pe0 + c + 1)
            dst = fm_out[:, c * fm_chunk_w:c * fm_chunk_w + T]
            if affine is None:
                scalar.copy(dst, ps_tr[c % 2][:, 0:T]).then_inc(sdve, 1)
            else:
                w, b, col0 = affine
                scalar.activation(dst, ps_tr[c % 2][:, 0:T], AF.Identity,
                                  scale=w[:, col0 + c:col0 + c + 1],
                                  bias=b[:, col0 + c:col0 + c + 1]
                                  ).then_inc(sdve, 1)

    p.bump("pe", nchunks)
    p.bump("dve", nchunks)


def emit_weight_prep(p, block, jobs, scope, cap=2048):
    maxc = min(cap, max(j["src"].shape[1] for j in jobs))
    wtmp = [p.sb_in(scope, f"wtmp{i}", [128, maxc], F32) for i in range(2)]
    wbf = [p.sb_in(scope, f"wbf{i}", [128, maxc], BF16) for i in range(2)]
    sprep = p.s("prep")
    prep0 = p.val("prep")
    lds = p.stream("wprep_ld")
    sts = p.stream("wprep_st")
    lb0, sb0 = lds.i, sts.i

    chunks = []
    for j in jobs:
        R, C = j["src"].shape
        for k in range(R // 128):
            for c0 in range(0, C, maxc):
                chunks.append((j, k, c0, min(maxc, C - c0)))
    n_hbm = 0
    hbm_idx = []
    for (j, k, c0, w) in chunks:
        hbm_idx.append(n_hbm if "dst_hbm" in j else -1)
        if "dst_hbm" in j:
            n_hbm += 1

    @block.sync
    def _(sync):
        for i, (j, k, c0, w) in enumerate(chunks):
            if i >= 2:
                sync.wait_ge(sprep, prep0 + i - 1)
            lds.issue(sync, wtmp[i % 2][:, 0:w],
                      j["src"][k * 128:(k + 1) * 128, c0:c0 + w])

    @block.vector
    def _(vector):
        for i, (j, k, c0, w) in enumerate(chunks):
            C = j["src"].shape[1]
            lds.wait(vector, lds.tok(lb0 + i))
            if "dst_sb" in j:
                vector.tensor_copy(j["dst_sb"][:, k * C + c0:k * C + c0 + w],
                                   wtmp[i % 2][:, 0:w]).then_inc(sprep, 1)
            else:
                if hbm_idx[i] >= 2:
                    sts.wait(vector, sts.tok(sb0 + hbm_idx[i] - 2))
                vector.tensor_copy(wbf[hbm_idx[i] % 2][:, 0:w],
                                   wtmp[i % 2][:, 0:w]).then_inc(sprep, 1)

    @block.gpsimd
    def _(gpsimd):
        for i, (j, k, c0, w) in enumerate(chunks):
            if hbm_idx[i] >= 0:
                gpsimd.wait_ge(sprep, prep0 + i + 1)
                sts.issue(gpsimd, j["dst_hbm"][k * 128:(k + 1) * 128, c0:c0 + w],
                          wbf[hbm_idx[i] % 2][:, 0:w])

    for i in range(len(chunks)):
        p.bump("prep", 1)
    return sts, n_hbm


def emit_param_loads(p, block, pairs):
    sld = p.s("ld")

    @block.sync
    def _(sync):
        for t, h in pairs:
            sync.dma_start(out=t[:], in_=h).then_inc(sld, 16)

    for _ in pairs:
        p.bump("ld", 16)
    total = p.val("ld")

    @block.vector
    def _(vector):
        vector.wait_ge(sld, total)

    return total


def build_lb(p=None, H=None):
    standalone = p is None
    if standalone:
        p = Prog()
        p.strip_bufs()
    p.H = H or {}
    p.push_scope()
    nc = p.nc
    MLP = 4096
    xa_h = p.dram("xa", [N, D], F32, "ExternalInput")
    xo_h = p.dram("xo", [N, D], F32, "ExternalInput")
    fcw_h = p.dram("fcw", [D, MLP], F32, "ExternalInput")
    fcb_h = p.dram("fcb", [128, MLP // 128], F32, "ExternalInput")
    pjw_h = p.dram("pjw", [MLP, D], F32, "ExternalInput")
    pjb_h = p.dram("pjb", [128, D // 128], F32, "ExternalInput")
    lnw_h = p.dram("lnw", [128, D // 128], F32, "ExternalInput")
    lnb_h = p.dram("lnb", [128, D // 128], F32, "ExternalInput")
    id_h = p.dram("ident", [128, 128], BF16, "ExternalInput")
    y_h = p.dram("y", [N, D], F32, "ExternalOutput")
    fcw_s = p.dram("fcw_bf", [D, MLP], BF16)

    stats = p.sb("stats", [128, 16 * NSUB], F32)
    eps = p.sb("eps", [128, 1], F32)
    ident = p.sb("id_sb", [128, 128], BF16)
    fcb = p.sb("fcb_sb", [128, MLP // 128], F32)
    pjb = p.sb("pjb_sb", [128, D // 128], F32)
    lnw = p.sb("lnw_sb", [128, D // 128], F32)
    lnb = p.sb("lnb_sb", [128, D // 128], F32)
    pjw = p.sb("pjw_sb", [128, 32 * D], BF16)

    p.sems(["ld", "wld", "st", "prep", "pe", "dve", "act", "lns", "pefree", "bank", "mx", "vd"])
    sld, swld, sst = p.s("ld"), p.s("wld"), p.s("st")
    spe, sdve, sact, spefree = p.s("pe"), p.s("dve"), p.s("act"), p.s("pefree")

    with nc.Block() as block:
        emit_param_loads(p, block, [(ident, id_h), (fcb, fcb_h), (pjb, pjb_h),
                                    (lnw, lnw_h), (lnb, lnb_h)])

        @block.vector
        def _(vector):
            vector.memset(eps[:], 1e-5)

    with p.scope() as prep_scope:
        with nc.Block() as block:
            sts, _nh = emit_weight_prep(p, block, [
                {"src": pjw_h, "dst_sb": pjw},
                {"src": fcw_h, "dst_hbm": fcw_s},
            ], prep_scope)
        with nc.Block() as block:
            @block.sync
            def _(sync):
                for par in range(sts.np):
                    if sts.counts[par]:
                        sync.wait_ge(sts.sems[par], sts.counts[par])

    xa = p.sb("xa_sb", [128, NSUB * D], F32)
    xo = p.sb("xo_sb", [128, NSUB * D], F32)
    xsum = p.sb("xsum_sb", [128, NSUB * D], F32)
    out = p.sb("out_sb", [128, NSUB * D], F32)
    xn_bf = p.sb("xn_bf", [128, NSUB * D], BF16)
    xn_fm = p.sb("xn_fm", [128, 8 * TT], BF16)
    h_fm = p.sb("h_fm", [128, 32 * TT], BF16)
    y_fm = p.sb("y_fm", [128, 8 * TT], BF16)
    strip = p.strip_bufs()
    strips = p.stream("strips", 12)

    ps = [p.psum(f"ps{i}", [128, 512], F32) for i in range(4)]
    ps_tr = [p.psum(f"ps_tr{i}", [128, 1024], BF16) for i in range(2)]

    for i in range(NTILES):
        r0 = i * TT
        with nc.Block() as block:
            @block.sync
            def _(sync):
                sync.dma_start(out=tok_sb(xa[:], NSUB, D),
                               in_=tok_in(xa_h, r0)).then_inc(sld, 16)
                sync.dma_start(out=tok_sb(xo[:], NSUB, D),
                               in_=tok_in(xo_h, r0)).then_inc(sld, 16)

            p.bump("ld", 32)
            emit_ln(p, block, xa, xn_bf, stats, eps,
                    wait_sem=sld, wait_val=p.val("ld"))

            @block.scalar
            def _(scalar):
                pass

            @block.gpsimd
            def _(gpsimd):
                gpsimd.wait_ge(sld, p.val("ld"))
                gpsimd.tensor_tensor(xsum[:], xa[:], xo[:], OP.add)

        with nc.Block() as block:
            emit_tok2fm(p, block, xn_bf, xn_fm, ident, ps_tr, nchunks=8,
                        affine=(lnw, lnb, 0))

        with nc.Block() as block:
            evacs = []
            for m in range(32):
                def f(eng, pst, m=m):
                    return eng.activation(h_fm[:, m * TT:(m + 1) * TT], pst,
                                          AF.Gelu, bias=fcb[:, m:m + 1])
                evacs.append(f)
            emit_mm_stream(p, block, fcw_s, xn_fm, 8, 32, strip, strips, ps,
                           "scalar", evacs)

        with nc.Block() as block:
            pe0, dve0, st0 = p.val("pe"), p.val("dve"), p.val("st")

            @block.tensor
            def _(tensor):
                for g in range(2):
                    if g == 1:
                        tensor.wait_ge(sdve, dve0 + 4)
                    mm = None
                    for k in range(32):
                        for mi in range(4):
                            m = g * 4 + mi
                            mm = tensor.matmul(
                                ps[mi][:],
                                pjw[:, k * D + m * 128:k * D + (m + 1) * 128],
                                h_fm[:, k * TT:(k + 1) * TT],
                                start=(k == 0), stop=(k == 31))
                    mm.then_inc(spe, 1)
                for s in range(NSUB):
                    tensor.wait_ge(sdve, dve0 + 8 + max(0, s - 1))
                    mm = None
                    for c in range(8):
                        mm = tensor.transpose(
                            ps_tr[s % 2][:, c * 128:(c + 1) * 128],
                            y_fm[:, c * TT + s * 128:c * TT + (s + 1) * 128], ident[:])
                    mm.then_inc(spe, 1)

            @block.vector
            def _(vector):
                for g in range(2):
                    vector.wait_ge(spe, pe0 + g + 1)
                    for mi in range(4):
                        m = g * 4 + mi
                        vector.tensor_scalar(y_fm[:, m * TT:(m + 1) * TT], ps[mi][:],
                                             pjb[:, m:m + 1], None,
                                             OP.add).then_inc(sdve, 1)
                vector.wait_ge(sst, st0)
                for s in range(NSUB):
                    vector.wait_ge(spe, pe0 + 2 + s + 1)
                    vector.tensor_tensor(out[:, s * D:(s + 1) * D],
                                         xsum[:, s * D:(s + 1) * D],
                                         ps_tr[s % 2][:, 0:D],
                                         OP.add).then_inc(sdve, 1)

            @block.sync
            def _(sync):
                sync.wait_ge(sdve, dve0 + 8 + NSUB)
                sync.dma_start(out=tok_in(y_h, r0),
                               in_=tok_sb(out[:], NSUB, D)).then_inc(sst, 16)

            p.bump("pe", 2 + NSUB)
            p.bump("dve", 8 + NSUB)
            p.bump("st", 16)

    with nc.Block() as block:
        @block.sync
        def _(sync):
            sync.wait_ge(sst, p.val("st"))

    p.pop_scope()
    if standalone:
        return p.close()


def lb_inmaps(xa_full, xo_full, ins):
    com = {
        "fcw": np.ascontiguousarray(ins["L0_attn_fc_w"]),
        "fcb": vec_pm(ins["L0_attn_fc_b"]),
        "pjw": np.ascontiguousarray(ins["L0_attn_proj_w"]),
        "pjb": vec_pm(ins["L0_attn_proj_b"]),
        "lnw": vec_pm(ins["L0_attn_ln2_w"]),
        "lnb": vec_pm(ins["L0_attn_ln2_b"]),
        "ident": eye_bf16(),
    }
    return [dict(com, xa=np.ascontiguousarray(xa_full[c]),
                 xo=np.ascontiguousarray(xo_full[c])) for c in range(NCORES)]


def emit_mm_stream(p, block, w_s, rhs_fm, nk, nm, strip, strips, ps, evac_engine,
                   evac_fns, pe_sem="bank", ev_sem="mx", rhs_tt=TT):
    spe, sev = p.s(pe_sem), p.s(ev_sem)
    pe0, ev0 = p.val(pe_sem), p.val(ev_sem)
    st_e = p.stream("strips_e", 6)
    st_o = p.stream("strips_o", 6)
    je0, jo0 = st_e.i, st_o.i
    nmsc = nm // 2
    nbuf = 2 * len(strip)
    if not hasattr(p, "_strip_ledger"):
        p._strip_ledger = []
    led = p._strip_ledger
    gj0 = len(led)
    for j in range(nmsc * nk):
        led.append(pe0 + j + 1)
    if callable(rhs_fm):
        rhs_of = rhs_fm
    else:
        rhs_of = lambda k: rhs_fm[:, k * rhs_tt:(k + 1) * rhs_tt]

    def slot(j):
        b = (gj0 + j) % nbuf
        return strip[b // 2][:, (b % 2) * 256:(b % 2) * 256 + 256]

    def issue_half(eng, par, stm):
        for j in range(nmsc * nk):
            if j % 2 != par:
                continue
            msc, k = divmod(j, nk)
            if gj0 + j >= nbuf:
                eng.wait_ge(spe, led[gj0 + j - nbuf])
            stm.issue(eng, slot(j),
                      w_s[k * 128:(k + 1) * 128, msc * 256:(msc + 1) * 256])

    @block.sync
    def _(sync):
        issue_half(sync, 0, st_e)

    @block.gpsimd
    def _(gpsimd):
        issue_half(gpsimd, 1, st_o)

    @block.tensor
    def _(tensor):
        for msc in range(nmsc):
            b0, b1 = (0, 1) if msc % 2 == 0 else (2, 3)
            if msc >= 2:
                tensor.wait_ge(sev, ev0 + 2 * (msc - 1))
            elif ev0 > 0:
                tensor.wait_ge(sev, ev0)
            for k in range(nk):
                j = msc * nk + k
                if j % 2 == 0:
                    st_e.wait(tensor, st_e.tok(je0 + j // 2))
                else:
                    st_o.wait(tensor, st_o.tok(jo0 + j // 2))
                sb_ = slot(j)
                st_, sp_ = (k == 0), (k == nk - 1)
                tensor.matmul(ps[b1][:, 0:rhs_tt], sb_[:, 128:256],
                              rhs_of(k), start=st_, stop=sp_)
                tensor.matmul(ps[b0][:, 0:rhs_tt], sb_[:, 0:128],
                              rhs_of(k), start=st_, stop=sp_).then_inc(spe, 1)

    def emit_evacs(eng):
        for msc in range(nmsc):
            b0, b1 = (0, 1) if msc % 2 == 0 else (2, 3)
            eng.wait_ge(spe, pe0 + nk * (msc + 1))
            evac_fns[2 * msc](eng, ps[b0][:, 0:rhs_tt])
            evac_fns[2 * msc + 1](eng, ps[b1][:, 0:rhs_tt]).then_inc(sev, 2)

    if evac_engine == "scalar":
        @block.scalar
        def _(scalar):
            emit_evacs(scalar)
    else:
        @block.vector
        def _(vector):
            emit_evacs(vector)

    p.bump(pe_sem, nmsc * nk)
    p.bump(ev_sem, nmsc * 2)


def vfence(p, eng, instr):
    svd = p.s("vd")
    instr.then_inc(svd, 1)
    eng.wait_ge(svd, p.bump("vd"))


MAGIC = 12582912.0


def emit_quant_tok(p, block, vector, x_tok, out_bf, qst, i32buf, nsub=NSUB,
                   fw=D, qmax=127.0, lo=-128.0, hi=127.0):
    n = nsub
    for s in range(n):
        vector.reduce_max(qst[:, s:s + 1], x_tok[:, s * fw:(s + 1) * fw],
                          axis=AX.X, apply_absolute_value=True)
    vector.drain()
    vfence(p, vector, vector.tensor_scalar(qst[:, 0:n], qst[:, 0:n], 1e-5,
                                           None, OP.max))
    vfence(p, vector, vector.reciprocal(qst[:, n:2 * n], qst[:, 0:n]))
    vector.tensor_scalar(qst[:, n:2 * n], qst[:, n:2 * n], qmax, None, OP.mult)
    vfence(p, vector,
           vector.tensor_scalar(qst[:, 2 * n:3 * n], qst[:, 0:n], 1.0 / qmax,
                                None, OP.mult))
    for s in range(n):
        vfence(p, vector,
               vector.tensor_scalar(i32buf[:], x_tok[:, s * fw:(s + 1) * fw],
                                    qst[:, n + s:n + s + 1], MAGIC,
                                    OP.mult, OP.add))
        vfence(p, vector,
               vector.tensor_scalar(out_bf[:, s * fw:(s + 1) * fw], i32buf[:],
                                    MAGIC, qst[:, 2 * n + s:2 * n + s + 1],
                                    OP.subtract, OP.mult))


def build_lc1(p=None, H=None):
    standalone = p is None
    if standalone:
        p = Prog()
        p.strip_bufs()
    p.H = H or {}
    p.push_scope()
    nc = p.nc
    MLP = 4096
    x_h = p.dram("x1", [N, D], F32, "ExternalInput")
    w1t_h = p.dram("w1t", [D, MLP], F32, "ExternalInput")
    id_h = p.dram("ident", [128, 128], BF16, "ExternalInput")
    h_out = p.dram("hs", [MLP, N], BF16, "ExternalOutput")
    w1q_s = p.dram("w1q_bf", [D, MLP], BF16)

    stats = p.sb("stats", [128, 16 * NSUB], F32)
    qst = p.sb("qst", [128, 4 * NSUB], F32)
    eps = p.sb("eps", [128, 1], F32)
    ident = p.sb("id_sb", [128, 128], BF16)
    wsc = p.sb("wsc", [128, 4], F32)

    p.sems(["ld", "st", "prep", "pe", "dve", "act", "lns", "bank", "mx", "vd"])
    sld, sst = p.s("ld"), p.s("st")
    spe, sdve, sact = p.s("pe"), p.s("dve"), p.s("act")

    with nc.Block() as block:
        emit_param_loads(p, block, [(ident, id_h)])

        @block.vector
        def _(vector):
            vector.memset(eps[:], 1e-5)
            vector.memset(wsc[:], 0.0)

    emit_wquant_global(p, w1t_h, w1q_s, 8, MLP, "w1")

    xbuf = p.sb("xbuf", [128, NSUB * D], F32)
    xln = p.sb("xln", [128, NSUB * D], F32)
    xq = p.sb("xq", [128, NSUB * D], BF16)
    xq_fm = p.sb("xq_fm", [128, 8 * TT], BF16)
    i32b = p.sb("i32b", [128, D], F32)
    hbuf = p.sb("hbuf", [128, 32 * TT], BF16)
    strip = p.strip_bufs()
    strips = p.stream("strips", 12)
    hsts = p.stream("aux_st")
    ps = [p.psum(f"ps{i}", [128, 512], F32) for i in range(4)]
    ps_t = [p.psum(f"ps_t{i}", [128, 1024], BF16) for i in range(2)]

    for i in range(NTILES):
        r0 = i * TT
        with nc.Block() as block:
            @block.sync
            def _(sync):
                sync.dma_start(out=tok_sb(xbuf[:], NSUB, D),
                               in_=tok_in(x_h, r0)).then_inc(sld, 16)

            p.bump("ld", 16)
            emit_ln(p, block, xbuf, xln, stats, eps,
                    wait_sem=sld, wait_val=p.val("ld"), done_inc=(sdve, 1))
            qd = p.bump("dve")

            @block.vector
            def _(vector):
                vector.wait_ge(sdve, qd)
                emit_quant_tok(p, block, vector, xln, xq, qst, i32b)

        with nc.Block() as block:
            emit_tok2fm(p, block, xq, xq_fm, ident, ps_t, nchunks=8)

        if i >= 1:
            with nc.Block() as block:
                @block.scalar
                def _(scalar):
                    for pr in range(hsts.np):
                        if hsts.counts[pr]:
                            scalar.wait_ge(hsts.sems[pr], hsts.counts[pr])

        with nc.Block() as block:
            evacs = []
            for m in range(32):
                def f(eng, pst, m=m):
                    return eng.copy(hbuf[:, m * TT:(m + 1) * TT], pst)
                evacs.append(f)
            emit_mm_stream(p, block, w1q_s, xq_fm, 8, 32, strip, strips, ps,
                           "scalar", evacs)
            mx_done = p.val("mx")

            @block.gpsimd
            def _(gpsimd):
                for c in range(32):
                    gpsimd.wait_ge(p.s("mx"), mx_done - 32 + 2 * (c // 2) + 2)
                    hsts.issue(gpsimd,
                               h_out[c * 128:(c + 1) * 128, r0:r0 + TT],
                               hbuf[:, c * TT:(c + 1) * TT])

    with nc.Block() as block:
        @block.sync
        def _(sync):
            for pr in range(hsts.np):
                if hsts.counts[pr]:
                    sync.wait_ge(hsts.sems[pr], hsts.counts[pr])

    p.pop_scope()
    if standalone:
        return p.close()


def emit_wquant_global(p, w_h, w_s, nk, C, name):
    nc = p.nc
    spe, sdve = p.s("pe"), p.s("dve")
    with p.scope() as scope:
        wtmp = [p.sb_in(scope, f"{name}wt{i}", [128, C], F32) for i in range(2)]
        wqb = [p.sb_in(scope, f"{name}wqb{i}", [128, C], BF16) for i in range(2)]
        wi32 = p.sb_in(scope, f"{name}wi32", [128, C], F32)
        acc = p.sb_in(scope, f"{name}acc", [128, 2], F32)
        sca = p.sb_in(scope, f"{name}sca", [128, 4], F32)
        ones1 = p.sb_in(scope, f"{name}o1", [128, 1], F32)
        onesr = p.sb_in(scope, f"{name}or", [1, 128], F32)
        ps1 = scope.enter_context(nc.psum_tensor(p._nm(f"{name}ps1"), [1, 8], F32))
        psb = scope.enter_context(nc.psum_tensor(p._nm(f"{name}psb"), [128, 8], F32))
        lds = p.stream("wqgld")
        sts = p.stream("wqgst")
        lb0, sb0 = lds.i, sts.i

        with nc.Block() as block:
            dve0 = p.val("dve")

            @block.sync
            def _(sync):
                for k in range(nk):
                    if k >= 2:
                        sync.wait_ge(sdve, dve0 + k - 1)
                    lds.issue(sync, wtmp[k % 2][:], w_h[k * 128:(k + 1) * 128, :])

            @block.vector
            def _(vector):
                vector.memset(ones1[:], 1.0)
                vector.memset(onesr[:], 1.0)
                vector.memset(acc[:], 0.0)
                vector.drain()
                for k in range(nk):
                    lds.wait(vector, lds.tok(lb0 + k))
                    vector.reduce_sum(sca[:, 0:1], wtmp[k % 2][:], axis=AX.X,
                                      apply_absolute_value=True)
                    vector.drain()
                    vector.tensor_tensor(acc[:, 0:1], acc[:, 0:1], sca[:, 0:1],
                                         OP.add).then_inc(sdve, 1)
                    vector.drain()
            p.bump("dve", nk)

        with nc.Block() as block:
            pe0, dve0 = p.val("pe"), p.val("dve")

            @block.tensor
            def _(tensor):
                tensor.matmul(ps1[:, 0:1], ones1[:], acc[:, 0:1],
                              start=True, stop=True).then_inc(spe, 1)

            @block.vector
            def _(vector):
                vector.wait_ge(spe, pe0 + 1)
                vector.tensor_scalar(sca[:1, 0:1], ps1[:, 0:1],
                                     1.0 / (nk * 128 * C), 1e-5, OP.mult, OP.max)
                vector.drain()
                vector.reciprocal(sca[:1, 1:2], sca[:1, 0:1])
                vector.drain()
            p.bump("pe", 1)

        with nc.Block() as block:
            pe0 = p.val("pe")

            @block.tensor
            def _(tensor):
                tensor.matmul(psb[:, 0:2], onesr[:], sca[0:1, 0:2],
                              start=True, stop=True).then_inc(spe, 1)

            @block.vector
            def _(vector):
                vector.wait_ge(spe, pe0 + 1)
                vector.tensor_copy(sca[:, 2:4], psb[:, 0:2])
            p.bump("pe", 1)

        with nc.Block() as block:
            dve0 = p.val("dve")

            @block.sync
            def _(sync):
                for k in range(nk):
                    if k >= 2:
                        sync.wait_ge(sdve, dve0 + k - 1)
                    lds.issue(sync, wtmp[k % 2][:], w_h[k * 128:(k + 1) * 128, :])

            @block.vector
            def _(vector):
                for k in range(nk):
                    lds.wait(vector, lds.tok(lb0 + nk + k))
                    if k >= 2:
                        sts.wait(vector, sts.tok(sb0 + k - 2))
                    vector.tensor_scalar(wi32[:], wtmp[k % 2][:], sca[:, 3:4],
                                         MAGIC, OP.mult, OP.add)
                    vector.drain()
                    vector.tensor_scalar(wi32[:], wi32[:], MAGIC, -1.0,
                                         OP.subtract, OP.max)
                    vector.drain()
                    vector.tensor_scalar(wqb[k % 2][:], wi32[:], 1.0, sca[:, 2:3],
                                         OP.min, OP.mult).then_inc(sdve, 1)
                    vector.drain()

            @block.gpsimd
            def _(gpsimd):
                for k in range(nk):
                    gpsimd.wait_ge(sdve, dve0 + k + 1)
                    sts.issue(gpsimd, w_s[k * 128:(k + 1) * 128, :], wqb[k % 2][:])
            p.bump("dve", nk)

        with nc.Block() as block:
            @block.sync
            def _(sync):
                for pr in range(sts.np):
                    if sts.counts[pr]:
                        sync.wait_ge(sts.sems[pr], sts.counts[pr])


def build_lc2(p=None, H=None):
    standalone = p is None
    if standalone:
        p = Prog()
        p.strip_bufs()
    p.H = H or {}
    p.push_scope()
    nc = p.nc
    MLP = 4096
    hs_h = p.dram("hs", [MLP, N], BF16, "ExternalInput")
    x_h = p.dram("x1", [N, D], F32, "ExternalInput")
    w2t_h = p.dram("w2t", [MLP, D], F32, "ExternalInput")
    cw_h = p.dram("convw", [128, 96], F32, "ExternalInput")
    cb_h = p.dram("convb", [128, 32], F32, "ExternalInput")
    id_h = p.dram("ident", [128, 128], BF16, "ExternalInput")
    idf_h = p.dram("identf", [128, 128], F32, "ExternalInput")
    x2_h = p.dram("x2", [N, D], F32, "ExternalOutput")
    w2q_s = p.dram("w2q_bf", [MLP, D], BF16)

    qst = p.sb("qst", [128, 4 * NSUB], F32)
    ident = p.sb("id_sb", [128, 128], BF16)
    convw = p.sb("convw_sb", [128, 96], F32)
    convb = p.sb("convb_sb", [128, 32], F32)
    identf = p.sb("idf_sb", [128, 128], F32)
    onesr1 = p.sb("onesr1", [1, 128], F32)

    p.sems(["ld", "st", "prep", "pe", "dve", "act", "lns", "bank", "mx", "vd"])
    sld, sst = p.s("ld"), p.s("st")
    spe, sdve, sact = p.s("pe"), p.s("dve"), p.s("act")

    with nc.Block() as block:
        emit_param_loads(p, block, [(ident, id_h), (convw, cw_h), (convb, cb_h),
                                    (identf, idf_h)])

        @block.vector
        def _(vector):
            vector.memset(onesr1[:], 1.0)

    emit_wquant_global(p, w2t_h, w2q_s, 32, D, "w2")

    hh = p.sb("hh", [128, 32 * 514], BF16)
    hsil = p.sb("hsil", [128, 32 * TT], BF16)
    o1 = p.sb("o1", [128, 4 * TT], BF16)
    o2 = p.sb("o2", [128, 4 * TT], BF16)
    o3 = p.sb("o3", [128, 4 * TT], BF16)
    xbuf = p.sb("xbuf", [128, NSUB * D], F32)
    out = p.sb("outb", [128, NSUB * D], F32)
    y_fm = p.sb("y_fm", [128, 8 * TT], BF16)
    i32b = p.sb("i32c", [128, 4 * TT], F32)
    maxab = p.sb("maxab", [128, 8], F32)
    rrow = p.sb("rrow", [1, 1024], F32)
    csc = p.sb("csc", [128, TT], F32)
    cinv = p.sb("cinv", [128, TT], F32)
    strip = p.strip_bufs()
    strips = p.stream("strips", 12)
    hlds = p.stream("aux_ld")
    ps = [p.psum(f"ps{i}", [128, 512], F32) for i in range(4)]
    ps_t = [p.psum(f"ps_t{i}", [128, 1024], BF16) for i in range(2)]
    ps_r = p.psum("ps_r", [128, 1024], F32)

    with nc.Block() as block:
        @block.vector
        def _(vector):
            vector.memset(hh[:], 0.0)

    HW = 514
    for i in range(NTILES):
        r0 = i * TT
        lo = max(0, r0 - 1)
        hi2 = min(N, r0 + TT + 1)
        coff = lo - (r0 - 1)
        w = hi2 - lo
        with nc.Block() as block:
            @block.sync
            def _(sync):
                sync.wait_ge(sst, p.val("st"))
                for c in range(32):
                    hlds.issue(sync, bass.AP(hh, c * HW + coff, [[32 * HW, 128], [1, w]]),
                               hs_h[c * 128:(c + 1) * 128, lo:hi2])
                sync.dma_start(out=tok_sb(xbuf[:], NSUB, D),
                               in_=tok_in(x_h, r0)).then_inc(sld, 16)
            p.bump("ld", 16)

            @block.vector
            def _(vector):
                if i == NTILES - 1:
                    vector.memset(bass.AP(hh, HW - 1, [[32 * HW, 128], [HW, 32]]),
                                  0.0)
                for pr in range(hlds.np):
                    vector.wait_ge(hlds.sems[pr], hlds.counts[pr])
                vector.wait_ge(sld, p.val("ld"))

        with nc.Block() as block:
            dve0, act0 = p.val("dve"), p.val("act")

            @block.scalar
            def _(scalar):
                for g in range(8):
                    if g >= 1:
                        scalar.wait_ge(sdve, dve0 + 2 * (g - 1) + 1)
                    last = None
                    for cc in range(4):
                        c = 4 * g + cc
                        last = scalar.activation(
                            o1[:, cc * TT:(cc + 1) * TT],
                            bass.AP(hh, c * HW, [[32 * HW, 128], [1, TT]]),
                            AF.Identity, scale=convw[:, 3 * c:3 * c + 1],
                            bias=convb[:, c:c + 1])
                    last.then_inc(sact, 1)
                    scalar.drain()
                    scalar.wait_ge(sdve, dve0 + 2 * g + 2)
                    last = None
                    for cc in range(4):
                        c = 4 * g + cc
                        last = scalar.activation(hsil[:, c * TT:(c + 1) * TT],
                                                 o3[:, cc * TT:(cc + 1) * TT],
                                                 AF.Silu)
                    last.then_inc(sact, 1)
                    scalar.drain()

            @block.vector
            def _(vector):
                for g in range(8):
                    vector.wait_ge(sact, act0 + 2 * g + 1)
                    last = None
                    for cc in range(4):
                        c = 4 * g + cc
                        last = vector.scalar_tensor_tensor(
                            o2[:, cc * TT:(cc + 1) * TT],
                            bass.AP(hh, c * HW + 1, [[32 * HW, 128], [1, TT]]),
                            convw[:, 3 * c + 1:3 * c + 2],
                            o1[:, cc * TT:(cc + 1) * TT], OP.mult, OP.add)
                    last.then_inc(sdve, 1)
                    vector.wait_ge(sdve, p.val("dve") + 2 * g + 1)
                    if g >= 1:
                        vector.wait_ge(sact, act0 + 2 * (g - 1) + 2)
                    last = None
                    for cc in range(4):
                        c = 4 * g + cc
                        last = vector.scalar_tensor_tensor(
                            o3[:, cc * TT:(cc + 1) * TT],
                            bass.AP(hh, c * HW + 2, [[32 * HW, 128], [1, TT]]),
                            convw[:, 3 * c + 2:3 * c + 3],
                            o2[:, cc * TT:(cc + 1) * TT], OP.mult, OP.add)
                    last.then_inc(sdve, 1)
            p.bump("dve", 16)
            p.bump("act", 16)

        with nc.Block() as block:
            pe0, dve0 = p.val("pe"), p.val("dve")

            @block.tensor
            def _(tensor):
                t = 0
                for s in range(NSUB):
                    for gg in range(4):
                        if t >= 2:
                            tensor.wait_ge(sdve, dve0 + t - 1)
                        mm = None
                        for cc in range(8):
                            c = gg * 8 + cc
                            mm = tensor.transpose(
                                ps_t[t % 2][:, cc * 128:(cc + 1) * 128],
                                hsil[:, c * TT + s * 128:c * TT + s * 128 + 128],
                                ident[:])
                        mm.then_inc(spe, 1)
                        t += 1

            @block.vector
            def _(vector):
                t = 0
                for s in range(NSUB):
                    for gg in range(4):
                        vector.wait_ge(spe, pe0 + t + 1)
                        vfence(p, vector,
                               vector.reduce_max(maxab[:, 4 + s:5 + s],
                                                 ps_t[t % 2][:], axis=AX.X,
                                                 apply_absolute_value=True))
                        if gg == 0:
                            vector.tensor_copy(
                                maxab[:, s:s + 1],
                                maxab[:, 4 + s:5 + s]).then_inc(sdve, 1)
                        else:
                            vector.tensor_tensor(
                                maxab[:, s:s + 1], maxab[:, s:s + 1],
                                maxab[:, 4 + s:5 + s], OP.max).then_inc(sdve, 1)
                        vector.wait_ge(sdve, dve0 + t + 1)
                        t += 1
                vector.tensor_scalar(maxab[:, 0:4], maxab[:, 0:4], 1e-5, None, OP.max)
                vector.drain()
                vector.reciprocal(maxab[:, 4:8], maxab[:, 0:4])
                vector.drain()
                vector.tensor_scalar(maxab[:, 4:8], maxab[:, 4:8], 127.0, None,
                                     OP.mult)
                vector.tensor_scalar(maxab[:, 0:4], maxab[:, 0:4], 1.0 / 127.0,
                                     None, OP.mult)
            p.bump("pe", 16)
            p.bump("dve", 16)

        with nc.Block() as block:
            pe0, dve0 = p.val("pe"), p.val("dve")

            @block.tensor
            def _(tensor):
                mm = None
                for s in range(NSUB):
                    mm = tensor.transpose(ps_r[0:1, s * 128:(s + 1) * 128],
                                          maxab[:, 4 + s:5 + s], identf[:])
                for s in range(NSUB):
                    mm = tensor.transpose(ps_r[0:1, 512 + s * 128:512 + (s + 1) * 128],
                                          maxab[:, s:s + 1], identf[:])
                mm.then_inc(spe, 1)

            @block.vector
            def _(vector):
                vector.wait_ge(spe, pe0 + 1)
                vector.tensor_copy(rrow[0:1, :], ps_r[0:1, 0:1024])
            p.bump("pe", 1)

        with nc.Block() as block:
            pe0 = p.val("pe")

            @block.tensor
            def _(tensor):
                tensor.matmul(ps_r[:, 0:512], onesr1[:], rrow[0:1, 0:512],
                              start=True, stop=True)
                tensor.matmul(ps_r[:, 512:1024], onesr1[:], rrow[0:1, 512:1024],
                              start=True, stop=True).then_inc(spe, 1)

            @block.vector
            def _(vector):
                vector.wait_ge(spe, pe0 + 1)
                vector.tensor_copy(csc[:], ps_r[:, 0:512])
                vector.tensor_copy(cinv[:], ps_r[:, 512:1024])
            p.bump("pe", 1)

        with nc.Block() as block:
            @block.vector
            def _(vector):
                csc_ap = bass.AP(csc, 0, [[TT, 128], [0, 4], [1, TT]])
                for g in range(8):
                    hg = hsil[:, g * 4 * TT:(g + 1) * 4 * TT]
                    vfence(p, vector, vector.tensor_tensor(i32b[:], hg, csc_ap,
                                                           OP.mult))
                    vfence(p, vector, vector.tensor_scalar(hg, i32b[:],
                                                           MAGIC, MAGIC,
                                                           OP.add, OP.subtract))

        with nc.Block() as block:
            evacs = []
            for m in range(8):
                def f(eng, pst, m=m):
                    return eng.tensor_tensor(y_fm[:, m * TT:(m + 1) * TT], pst,
                                             cinv[:], OP.mult)
                evacs.append(f)
            emit_mm_stream(p, block, w2q_s, hsil, 32, 8, strip, strips, ps,
                           "vector", evacs)

        with nc.Block() as block:
            pe0, dve0 = p.val("pe"), p.val("dve")

            @block.tensor
            def _(tensor):
                for s in range(NSUB):
                    if s >= 2:
                        tensor.wait_ge(sdve, dve0 + s - 1)
                    mm = None
                    for c in range(8):
                        mm = tensor.transpose(
                            ps_t[s % 2][:, c * 128:(c + 1) * 128],
                            y_fm[:, c * TT + s * 128:c * TT + (s + 1) * 128],
                            ident[:])
                    mm.then_inc(spe, 1)

            @block.vector
            def _(vector):
                for s in range(NSUB):
                    vector.wait_ge(spe, pe0 + s + 1)
                    vector.tensor_tensor(out[:, s * D:(s + 1) * D],
                                         xbuf[:, s * D:(s + 1) * D],
                                         ps_t[s % 2][:, 0:D],
                                         OP.add).then_inc(sdve, 1)

            @block.sync
            def _(sync):
                sync.wait_ge(sdve, dve0 + NSUB)
                sync.dma_start(out=tok_in(x2_h, r0),
                               in_=tok_sb(out[:], NSUB, D)).then_inc(sst, 16)
            p.bump("pe", NSUB)
            p.bump("dve", NSUB)
            p.bump("st", 16)

    with nc.Block() as block:
        @block.sync
        def _(sync):
            sync.wait_ge(sst, p.val("st"))

    p.pop_scope()
    if standalone:
        return p.close()


def lc1_inmaps(x1_full, ins):
    com = {
        "w1t": np.ascontiguousarray(ins["L0_ffn_w1"].T),
        "ident": eye_bf16(),
    }
    return [dict(com, x1=np.ascontiguousarray(x1_full[c])) for c in range(NCORES)]


def lc2_inmaps(hs_list, x1_full, ins):
    cw = np.ascontiguousarray(
        np.asarray(ins["L0_ffn_conv_w"])[:, 0, :].reshape(32, 128, 3)
        .transpose(1, 0, 2).reshape(128, 96))
    com = {
        "w2t": np.ascontiguousarray(ins["L0_ffn_w2"].T),
        "convw": cw,
        "convb": vec_pm(ins["L0_ffn_conv_b"]),
        "ident": eye_bf16(),
        "identf": np.eye(128, dtype=np.float32),
    }
    return [dict(com, hs=np.ascontiguousarray(hs_list[c]),
                 x1=np.ascontiguousarray(x1_full[c])) for c in range(NCORES)]


def build_ld(p=None, H=None):
    standalone = p is None
    if standalone:
        p = Prog()
        p.strip_bufs()
    p.H = H or {}
    p.push_scope()
    nc = p.nc
    x_h = p.dram("x2", [N, D], F32, "ExternalInput")
    w_h = p.dram("ipw", [D, 4352], F32, "ExternalInput")
    wdt_h = p.dram("ipwdt", [D, 32], F32, "ExternalInput")
    dtb_h = p.dram("dtb", [32, 1], F32, "ExternalInput")
    id_h = p.dram("ident", [128, 128], BF16, "ExternalInput")
    idf_h = p.dram("identf", [128, 128], F32, "ExternalInput")
    zs_h = p.dram("zs", [2048, N], BF16, "ExternalOutput")
    xbcs_h = p.dram("xbcs", [2304, N], BF16, "ExternalOutput")
    dts_h = p.dram("dts", [N, 32], F32, "ExternalOutput")
    ipw_s = p.dram("ipw_bf", [D, 4352], BF16)

    stats = p.sb("stats", [128, 16 * NSUB], F32)
    eps = p.sb("eps", [128, 1], F32)
    one_t = p.sb("one_t", [128, 1], F32)
    ident = p.sb("id_sb", [128, 128], BF16)
    identf = p.sb("idf_sb", [128, 128], F32)
    dtb = p.sb("dtb_sb", [32, 1], F32)
    wdt = p.sb("wdt_sb", [128, 8 * 32], BF16)

    p.sems(["ld", "st", "prep", "pe", "dve", "act", "lns", "bank", "mx", "vd"])
    sld, sst = p.s("ld"), p.s("st")
    spe, sdve, sact = p.s("pe"), p.s("dve"), p.s("act")

    with nc.Block() as block:
        emit_param_loads(p, block, [(ident, id_h), (identf, idf_h),
                                    (dtb, dtb_h)])

        @block.vector
        def _(vector):
            vector.memset(eps[:], 1e-5)
            vector.memset(one_t[:], 1.0)

    with p.scope() as prep_scope:
        with nc.Block() as block:
            sts, _nh = emit_weight_prep(p, block, [
                {"src": wdt_h, "dst_sb": wdt},
                {"src": w_h, "dst_hbm": ipw_s},
            ], prep_scope)
        with nc.Block() as block:
            @block.sync
            def _(sync):
                for pr in range(sts.np):
                    if sts.counts[pr]:
                        sync.wait_ge(sts.sems[pr], sts.counts[pr])

    xbuf = p.sb("xbuf", [128, NSUB * D], F32)
    xln = p.sb("xln", [128, NSUB * D], BF16)
    xn_fm = p.sb("xn_fm", [128, 8 * TT], BF16)
    zbuf = p.sb("zbuf", [128, 16 * TT], BF16)
    xbcbuf = p.sb("xbcbuf", [128, 18 * TT], BF16)
    dtf = p.sb("dtf", [32, 6 * TT], F32)
    dtt = p.sb("dtt", [128, NSUB * 32], F32)
    strip = p.strip_bufs()
    strips = p.stream("strips", 12)
    osts = p.stream("aux_st")
    ps = [p.psum(f"ps{i}", [128, 512], F32) for i in range(4)]
    ps_dt = p.psum("ps_dt", [32, 512], F32)
    ps_t = [p.psum(f"ps_t{i}", [128, 1024], BF16) for i in range(2)]
    ps_tf = p.psum("ps_tf", [128, 128], F32)

    for i in range(NTILES):
        r0 = i * TT
        with nc.Block() as block:
            @block.sync
            def _(sync):
                sync.dma_start(out=tok_sb(xbuf[:], NSUB, D),
                               in_=tok_in(x_h, r0)).then_inc(sld, 16)
            p.bump("ld", 16)
            emit_ln(p, block, xbuf, xln, stats, eps,
                    wait_sem=sld, wait_val=p.val("ld"))

        with nc.Block() as block:
            emit_tok2fm(p, block, xln, xn_fm, ident, ps_t, nchunks=8)

        if i >= 1:
            with nc.Block() as block:
                @block.scalar
                def _(scalar):
                    for pr in range(osts.np):
                        if osts.counts[pr]:
                            scalar.wait_ge(osts.sems[pr], osts.counts[pr])

        with nc.Block() as block:
            evacs = []
            for m in range(34):
                if m < 16:
                    def f(eng, pst, m=m):
                        return eng.activation(zbuf[:, m * TT:(m + 1) * TT], pst,
                                              AF.Silu)
                else:
                    def f(eng, pst, m=m):
                        return eng.copy(xbcbuf[:, (m - 16) * TT:(m - 15) * TT], pst)
                evacs.append(f)
            emit_mm_stream(p, block, ipw_s, xn_fm, 8, 34, strip, strips, ps,
                           "scalar", evacs)
            mx_done = p.val("mx")

            pe0, dve0, act0 = p.val("pe"), p.val("dve"), p.val("act")

            @block.tensor
            def _(tensor):
                mm = None
                for k in range(8):
                    mm = tensor.matmul(ps_dt[:], wdt[:, k * 32:(k + 1) * 32],
                                       xn_fm[:, k * TT:(k + 1) * TT],
                                       start=(k == 0), stop=(k == 7))
                mm.then_inc(spe, 1)

            @block.vector
            def _(vector):
                vector.wait_ge(spe, pe0 + 1)
                vector.tensor_scalar(dtf[:, 0:TT], ps_dt[:], dtb[:, 0:1], None,
                                     OP.add).then_inc(sdve, 1)
                vector.drain()
                vector.wait_ge(sact, act0 + 2)
                vector.tensor_scalar(dtf[:, 4 * TT:5 * TT], dtf[:, 0:TT], 0.0,
                                     None, OP.max)
                vector.drain()
                vector.tensor_tensor(dtf[:, 5 * TT:6 * TT], dtf[:, 4 * TT:5 * TT],
                                     dtf[:, 3 * TT:4 * TT], OP.add).then_inc(sdve, 1)
                vector.drain()

            @block.scalar
            def _(scalar):
                scalar.wait_ge(sdve, dve0 + 1)
                scalar.activation(dtf[:, 1 * TT:2 * TT], dtf[:, 0:TT], AF.Abs)
                scalar.drain()
                scalar.activation(dtf[:, 2 * TT:3 * TT], dtf[:, 1 * TT:2 * TT],
                                  AF.Exp, scale=-1.0).then_inc(sact, 1)
                scalar.drain()
                scalar.activation(dtf[:, 3 * TT:4 * TT], dtf[:, 2 * TT:3 * TT],
                                  AF.Ln, bias=one_t[0:32, 0:1]).then_inc(sact, 1)

            @block.tensor
            def _(tensor):
                tensor.wait_ge(sdve, dve0 + 2)
                for s in range(NSUB):
                    if s >= 1:
                        tensor.wait_ge(sdve, dve0 + 2 + s)
                    tensor.transpose(ps_tf[:, (s % 2) * 32:(s % 2) * 32 + 32],
                                     dtf[:, 5 * TT + s * 128:5 * TT + (s + 1) * 128],
                                     identf[0:32, 0:32]).then_inc(spe, 1)

            @block.vector
            def _(vector):
                for s in range(NSUB):
                    vector.wait_ge(spe, pe0 + 1 + s + 1)
                    vector.tensor_copy(dtt[:, s * 32:(s + 1) * 32],
                                       ps_tf[:, (s % 2) * 32:(s % 2) * 32 + 32]
                                       ).then_inc(sdve, 1)

            p.bump("pe", 1 + NSUB)
            p.bump("dve", 2 + NSUB)
            p.bump("act", 2)

            @block.gpsimd
            def _(gpsimd):
                for c in range(16):
                    gpsimd.wait_ge(p.s("mx"), mx_done - 34 + 2 * (c // 2) + 2)
                    osts.issue(gpsimd, zs_h[c * 128:(c + 1) * 128, r0:r0 + TT],
                               zbuf[:, c * TT:(c + 1) * TT])
                for c in range(18):
                    gpsimd.wait_ge(p.s("mx"),
                                   mx_done - 34 + 2 * ((16 + c) // 2) + 2)
                    osts.issue(gpsimd, xbcs_h[c * 128:(c + 1) * 128, r0:r0 + TT],
                               xbcbuf[:, c * TT:(c + 1) * TT])
                gpsimd.wait_ge(sdve, p.val("dve"))
                osts.issue(gpsimd,
                           dts_h[r0:r0 + TT, :].rearrange("(s p) h -> p s h", p=128),
                           dtt[:].rearrange("p (s h) -> p s h", s=NSUB))

    with nc.Block() as block:
        @block.sync
        def _(sync):
            for pr in range(osts.np):
                if osts.counts[pr]:
                    sync.wait_ge(osts.sems[pr], osts.counts[pr])

    p.pop_scope()
    if standalone:
        return p.close()


def build_ld2(p=None, H=None):
    standalone = p is None
    if standalone:
        p = Prog()
        p.strip_bufs()
    p.H = H or {}
    p.push_scope()
    nc = p.nc
    xbcs_h = p.dram("xbcs", [2304, N], BF16, "ExternalInput")
    cw_h = p.dram("convw", [128, 18 * 4], F32, "ExternalInput")
    cb_h = p.dram("convb", [128, 18], F32, "ExternalInput")
    id_h = p.dram("ident", [128, 128], BF16, "ExternalInput")
    xpt_h = p.dram("xpt", [N, 2048], BF16, "ExternalOutput")
    bfs_h = p.dram("bfs", [128, N], BF16, "ExternalOutput")
    cfs_h = p.dram("cfs", [128, N], BF16, "ExternalOutput")
    bts_h = p.dram("bts", [N, 128], BF16, "ExternalOutput")

    ident = p.sb("id_sb", [128, 128], BF16)
    convw = p.sb("convw_sb", [128, 72], F32)
    convb = p.sb("convb_sb", [128, 18], F32)

    p.sems(["ld", "st", "prep", "pe", "dve", "act", "lns", "bank", "mx", "vd"])
    sld, sst = p.s("ld"), p.s("st")
    spe, sdve, sact = p.s("pe"), p.s("dve"), p.s("act")

    with nc.Block() as block:
        emit_param_loads(p, block, [(ident, id_h), (convw, cw_h), (convb, cb_h)])

    HW = TT + 3
    hh = p.sb("hh", [128, 18 * HW], BF16)
    o1 = p.sb("o1", [128, 6 * TT], BF16)
    o2 = p.sb("o2", [128, 6 * TT], BF16)
    hsil = p.sb("hsil", [128, 18 * TT], BF16)
    xp_tok = p.sb("xp_tok", [128, NSUB * 2048], BF16)
    bt_tok = p.sb("bt_tok", [128, NSUB * 128], BF16)
    hlds = p.stream("aux_ld")
    osts = p.stream("aux_st")
    ps_t = [p.psum(f"ps_t{i}", [128, 1024], BF16) for i in range(2)]

    with nc.Block() as block:
        @block.vector
        def _(vector):
            vector.memset(hh[:], 0.0)

    for i in range(NTILES):
        r0 = i * TT
        lo = max(0, r0 - 3)
        coff = lo - (r0 - 3)
        w = r0 + TT - lo
        with nc.Block() as block:
            @block.sync
            def _(sync):
                for c in range(18):
                    hlds.issue(sync,
                               bass.AP(hh, c * HW + coff, [[18 * HW, 128], [1, w]]),
                               xbcs_h[c * 128:(c + 1) * 128, lo:r0 + TT])

            @block.vector
            def _(vector):
                for pr in range(hlds.np):
                    vector.wait_ge(hlds.sems[pr], hlds.counts[pr])

        with nc.Block() as block:
            dve0, act0 = p.val("dve"), p.val("act")

            @block.vector
            def _(vector):
                for g in range(3):
                    if g >= 1:
                        vector.wait_ge(sact, act0 + g)
                    nch = 6
                    last = None
                    for cc in range(nch):
                        c = 6 * g + cc
                        last = vector.tensor_scalar(
                            o1[:, cc * TT:(cc + 1) * TT],
                            bass.AP(hh, c * HW, [[18 * HW, 128], [1, TT]]),
                            convw[:, 4 * c:4 * c + 1],
                            convb[:, c:c + 1], OP.mult, OP.add)
                    vfence(p, vector, last)
                    last = None
                    for cc in range(nch):
                        c = 6 * g + cc
                        last = vector.scalar_tensor_tensor(
                            o2[:, cc * TT:(cc + 1) * TT],
                            bass.AP(hh, c * HW + 1, [[18 * HW, 128], [1, TT]]),
                            convw[:, 4 * c + 1:4 * c + 2],
                            o1[:, cc * TT:(cc + 1) * TT], OP.mult, OP.add)
                    vfence(p, vector, last)
                    last = None
                    for cc in range(nch):
                        c = 6 * g + cc
                        last = vector.scalar_tensor_tensor(
                            o1[:, cc * TT:(cc + 1) * TT],
                            bass.AP(hh, c * HW + 2, [[18 * HW, 128], [1, TT]]),
                            convw[:, 4 * c + 2:4 * c + 3],
                            o2[:, cc * TT:(cc + 1) * TT], OP.mult, OP.add)
                    vfence(p, vector, last)
                    last = None
                    for cc in range(nch):
                        c = 6 * g + cc
                        last = vector.scalar_tensor_tensor(
                            o2[:, cc * TT:(cc + 1) * TT],
                            bass.AP(hh, c * HW + 3, [[18 * HW, 128], [1, TT]]),
                            convw[:, 4 * c + 3:4 * c + 4],
                            o1[:, cc * TT:(cc + 1) * TT], OP.mult, OP.add)
                    last.then_inc(sdve, 1)

            @block.scalar
            def _(scalar):
                for g in range(3):
                    scalar.wait_ge(sdve, dve0 + g + 1)
                    last = None
                    for cc in range(6):
                        c = 6 * g + cc
                        last = scalar.activation(hsil[:, c * TT:(c + 1) * TT],
                                                 o2[:, cc * TT:(cc + 1) * TT],
                                                 AF.Silu)
                    last.then_inc(sact, 1)
            p.bump("dve", 3)
            p.bump("act", 3)

        with nc.Block() as block:
            pe0, dve0 = p.val("pe"), p.val("dve")

            @block.tensor
            def _(tensor):
                t = 0
                for s in range(NSUB):
                    for gg in range(4):
                        if t >= 2:
                            tensor.wait_ge(sdve, dve0 + t - 1)
                        mm = None
                        for cc in range(4):
                            c = gg * 4 + cc
                            mm = tensor.transpose(
                                ps_t[t % 2][:, cc * 128:(cc + 1) * 128],
                                hsil[:, c * TT + s * 128:c * TT + s * 128 + 128],
                                ident[:])
                        mm.then_inc(spe, 1)
                        t += 1
                for s in range(NSUB):
                    if t >= 2:
                        tensor.wait_ge(sdve, dve0 + t - 1)
                    tensor.transpose(ps_t[t % 2][:, 0:128],
                                     hsil[:, 16 * TT + s * 128:16 * TT + s * 128 + 128],
                                     ident[:]).then_inc(spe, 1)
                    t += 1

            @block.scalar
            def _(scalar):
                t = 0
                for s in range(NSUB):
                    for gg in range(4):
                        scalar.wait_ge(spe, pe0 + t + 1)
                        scalar.copy(
                            xp_tok[:, s * 2048 + gg * 512:s * 2048 + (gg + 1) * 512],
                            ps_t[t % 2][:, 0:512]).then_inc(sdve, 1)
                        t += 1
                for s in range(NSUB):
                    scalar.wait_ge(spe, pe0 + t + 1)
                    scalar.copy(bt_tok[:, s * 128:(s + 1) * 128],
                                ps_t[t % 2][:, 0:128]).then_inc(sdve, 1)
                    t += 1
            p.bump("pe", 20)
            p.bump("dve", 20)

        with nc.Block() as block:
            @block.gpsimd
            def _(gpsimd):
                osts.issue(gpsimd,
                           xpt_h[r0:r0 + TT, :].rearrange("(s p) f -> p s f", p=128),
                           xp_tok[:].rearrange("p (s f) -> p s f", s=NSUB))
                osts.issue(gpsimd,
                           bts_h[r0:r0 + TT, :].rearrange("(s p) f -> p s f", p=128),
                           bt_tok[:].rearrange("p (s f) -> p s f", s=NSUB))
                osts.issue(gpsimd, bfs_h[:, r0:r0 + TT],
                           hsil[:, 16 * TT:17 * TT])
                osts.issue(gpsimd, cfs_h[:, r0:r0 + TT],
                           hsil[:, 17 * TT:18 * TT])

            @block.sync
            def _(sync):
                for pr in range(osts.np):
                    if osts.counts[pr]:
                        sync.wait_ge(osts.sems[pr], osts.counts[pr])

    p.pop_scope()
    if standalone:
        return p.close()


def build_le(nch=None, p=None, H=None):
    standalone = p is None
    if standalone:
        p = Prog()
        p.strip_bufs()
    p.H = H or {}
    p.push_scope()
    nc = p.nc
    xpt_h = p.dram("xpt", [N, 2048], BF16, "ExternalInput")
    bts_h = p.dram("bts", [N, 128], BF16, "ExternalInput")
    bfs_h = p.dram("bfs", [128, N], BF16, "ExternalInput")
    cfs_h = p.dram("cfs", [128, N], BF16, "ExternalInput")
    dts_h = p.dram("dts", [N, 32], F32, "ExternalInput")
    alog_h = p.dram("alog", [1, 32], F32, "ExternalInput")
    dvec_h = p.dram("dvec", [1, 32], F32, "ExternalInput")
    tri_h = p.dram("tri", [128, 128], F32, "ExternalInput")
    tris_h = p.dram("tris", [128, 128], F32, "ExternalInput")
    idf_h = p.dram("identf", [128, 128], F32, "ExternalInput")
    ys_h = p.dram("ys", [N, 2048], F32, "ExternalOutput")

    tri = p.sb("tri_sb", [128, 128], F32)
    tris = p.sb("tris_sb", [128, 128], F32)
    identf = p.sb("idf_sb", [128, 128], F32)
    alog = p.sb("alog_sb", [1, 32], F32)
    dvec = p.sb("dvec_sb", [1, 32], F32)
    onesr = p.sb("onesr", [1, 128], F32)
    arow = p.sb("arow", [1, 32], F32)
    A_rep = p.sb("A_rep", [128, 32], F32)
    D_rep = p.sb("D_rep", [128, 32], F32)
    S = p.sb("S_sb", [128, 2048], F32)
    Sbf = p.sb("Sbf", [128, 2048], BF16)
    xt = [p.sb(f"xt{i}", [128, 2048], BF16) for i in range(2)]
    bt = [p.sb(f"bt{i}", [128, 128], BF16) for i in range(2)]
    bf = [p.sb(f"bf{i}", [128, 128], BF16) for i in range(2)]
    cf = [p.sb(f"cf{i}", [128, 128], BF16) for i in range(2)]
    dtt = [p.sb(f"dtt{i}", [128, 32], F32) for i in range(2)]
    adt = p.sb("adt", [128, 32], F32)
    xtl = p.sb("xtl", [128, 2048], BF16)
    xD = p.sb("xD", [128, 2048], F32)
    cl = p.sb("cl", [128, 32], F32)
    clT = p.sb("clT", [32, 128], F32)
    llrow = p.sb("llrow", [1, 32], F32)
    Ll = p.sb("Ll", [128, 32], F32)
    Wd = p.sb("Wd", [128, 32], F32)
    expl = p.sb("expl", [128, 32], F32)
    decr = p.sb("decr", [128, 32], F32)
    cbt = p.sb("cbt_sb", [128, 128], BF16)
    rhs4 = [p.sb(f"rhs4{i}", [128, 512], F32) for i in range(2)]
    segx = [p.sb(f"segx{i}", [128, 512], BF16) for i in range(2)]
    P4 = [p.sb(f"P4{i}", [128, 512], BF16) for i in range(2)]
    Bw4 = [p.sb(f"Bw4{i}", [128, 512], BF16) for i in range(2)]
    e1 = p.sb("e1", [128, 256], F32)
    e2 = p.sb("e2", [128, 256], F32)
    ybuf = p.sb("ybuf", [128, 2048], F32)

    p.sems(["ld", "st", "pe", "dve", "act", "mx", "bank", "lns", "prep", "vd"])
    sld, sst = p.s("ld"), p.s("st")
    spe, sdve, sact, smx = p.s("pe"), p.s("dve"), p.s("act"), p.s("mx")
    lds = p.stream("ssd_ld", nparity=5)
    ysts = p.stream("ys_st")
    lb0 = lds.i

    ps_misc = p.psum("ps_misc", [128, 128], F32)
    ps_cbt = p.psum("ps_cbt", [128, 128], F32)
    ps_seg = [p.psum(f"ps_seg{i}", [128, 512], F32) for i in range(2)]
    ps_ab = [p.psum(f"ps_ab{i}", [128, 512], F32) for i in range(2)]
    ps_c = [p.psum(f"ps_c{i}", [128, 512], F32) for i in range(2)]

    NCH = (N // 128) if nch is None else nch

    def issue_loads(sync, c):
        r0 = c * 128
        if c >= 1:
            for j in range(5):
                sync.wait_ge(lds.sems[j], 16 * (lb0 // 5 + c))
        lds.issue(sync, dtt[c % 2][:], dts_h[r0:r0 + 128, :])
        lds.issue(sync, xt[c % 2][:], xpt_h[r0:r0 + 128, :])
        lds.issue(sync, bt[c % 2][:], bts_h[r0:r0 + 128, :])
        lds.issue(sync, bf[c % 2][:], bfs_h[:, r0:r0 + 128])
        lds.issue(sync, cf[c % 2][:], cfs_h[:, r0:r0 + 128])

    with nc.Block() as block:
        emit_param_loads(p, block, [(tri, tri_h), (tris, tris_h),
                                    (identf, idf_h), (alog, alog_h),
                                    (dvec, dvec_h)])

        @block.sync
        def _(sync):
            issue_loads(sync, 0)

        @block.vector
        def _(vector):
            vector.memset(S[:], 0.0)
            vector.memset(onesr[:], 1.0)

        @block.scalar
        def _(scalar):
            scalar.wait_ge(sld, p.val("ld"))
            scalar.activation(arow[:], alog[:], AF.Exp)

    with nc.Block() as block:
        pe0 = p.val("pe")

        @block.tensor
        def _(tensor):
            tensor.matmul(ps_misc[:, 0:32], onesr[:], arow[:],
                          start=True, stop=True)
            tensor.matmul(ps_misc[:, 32:64], onesr[:], dvec[:],
                          start=True, stop=True).then_inc(spe, 1)

        @block.vector
        def _(vector):
            vector.wait_ge(spe, pe0 + 1)
            vector.tensor_scalar(A_rep[:], ps_misc[:, 0:32], -1.0, None, OP.mult)
            vector.tensor_copy(D_rep[:], ps_misc[:, 32:64])
        p.bump("pe", 1)

    prev = {}
    with nc.Block() as block:
        bodies = {"sync": [], "vector": [], "tensor": [], "scalar": [],
                  "gpsimd": []}
        REC = {}
        for c in range(NCH):
            r0 = c * 128
            pe0, dve0, act0 = p.val("pe"), p.val("dve"), p.val("act")
            vd0 = p.val("vd")
            svd = p.s("vd")
            pv = dict(prev)

            def _sync(sync, c=c, dve0=dve0):
                if c + 1 < NCH:
                    sync.wait_ge(sdve, REC[("xtl", c)])
                    issue_loads(sync, c + 1)

            def _vector(vector, c=c, pe0=pe0, dve0=dve0, act0=act0, vd0=vd0,
                        pv=pv):
                nd = [dve0]

                def inc(instr, kind, g=None):
                    nd[0] += 1
                    REC[(kind, c, g) if g is not None else (kind, c)] = nd[0]
                    instr.then_inc(sdve, 1)

                for j in range(5):
                    lds.wait(vector, lds.tok(5 * c + j))
                if pv:
                    vector.wait_ge(sdve, pv["send"])
                    vector.wait_ge(sact, pv["act3"])
                vector.tensor_tensor(adt[:], dtt[c % 2][:], A_rep[:], OP.mult)
                dt_ap = bass.AP(dtt[c % 2], 0, [[32, 128], [1, 32], [0, 64]])
                inc(vector.tensor_tensor(xtl[:], xt[c % 2][:], dt_ap, OP.mult),
                    "xtl")
                d_ap = bass.AP(D_rep, 0, [[32, 128], [1, 32], [0, 64]])
                n0, n1 = (c + 1) // 2, c // 2
                if n0:
                    vector.wait_ge(ysts.sems[0], 16 * n0)
                if n1:
                    vector.wait_ge(ysts.sems[1], 16 * n1)
                vector.tensor_tensor(xD[:], xt[c % 2][:], d_ap, OP.mult)
                vector.tensor_copy(Sbf[:], S[:]).then_inc(svd, 1)
                vector.wait_ge(spe, pe0 + 1)
                inc(vector.tensor_copy(cl[:], ps_misc[:, 0:32]), "cl")
                vector.wait_ge(spe, pe0 + 2)
                inc(vector.tensor_tensor(cbt[:], ps_cbt[:], tri[:], OP.mult),
                    "cbt")
                vector.wait_ge(spe, pe0 + 3)
                inc(vector.tensor_copy(clT[:], ps_misc[0:32, 0:128]), "clT")
                vector.wait_ge(spe, pe0 + 4)
                inc(vector.tensor_copy(llrow[:], ps_misc[0:1, 0:32]), "llrow")
                vector.wait_ge(spe, pe0 + 5)
                vector.tensor_copy(Ll[:], ps_misc[:, 32:64]).then_inc(svd, 1)
                vector.wait_ge(svd, vd0 + 2)
                inc(vector.tensor_tensor(Wd[:], Ll[:], cl[:], OP.subtract), "Wd")
                vector.wait_ge(sact, act0 + 3)
                dec_ap = bass.AP(decr, 0, [[32, 128], [1, 32], [0, 64]])
                inc(vector.tensor_tensor(S[:], S[:], dec_ap, OP.mult), "dec")
                nvd = [vd0 + 2]
                for step in range(10):
                    if step < 8:
                        g = step
                        if g >= 2:
                            vector.wait_ge(spe, pe0 + 5 + 3 * (g - 1))
                        elif pv:
                            vector.wait_ge(spe, pv["pe_end"])
                        tri_ap = bass.AP(tri, 0, [[128, 128], [0, 4], [1, 128]])
                        adt_ap = bass.AP(adt, 4 * g,
                                         [[32, 128], [1, 4], [0, 128]])
                        inc(vector.tensor_tensor(rhs4[g % 2][:], tri_ap, adt_ap,
                                                 OP.mult), "A", g)
                    if 1 <= step <= 8:
                        g = step - 1
                        if g >= 2:
                            vector.wait_ge(spe, pe0 + 5 + 3 * (g - 2) + 3)
                        elif pv:
                            vector.wait_ge(spe, pv["pe_end"])
                        vector.wait_ge(sact, act0 + 3 + 2 * g + 2)
                        cbt_ap = bass.AP(cbt, 0, [[128, 128], [0, 4], [1, 128]])
                        inc(vector.tensor_tensor(P4[g % 2][:], segx[g % 2][:],
                                                 cbt_ap, OP.mult), "P", g)
                        bt_ap = bass.AP(bt[c % 2], 0,
                                        [[128, 128], [0, 4], [1, 128]])
                        wd_ap = bass.AP(Wd, 4 * g, [[32, 128], [1, 4], [0, 128]])
                        inc(vector.tensor_tensor(Bw4[g % 2][:], bt_ap, wd_ap,
                                                 OP.mult), "W", g)
                    if step >= 2:
                        g = step - 2
                        vector.wait_ge(spe, pe0 + 5 + 3 * g + 3)
                        eap = bass.AP(expl, 4 * g, [[32, 128], [1, 4], [0, 64]])
                        vector.tensor_tensor(e1[:], ps_ab[g % 2][:, 256:512],
                                             eap, OP.mult).then_inc(svd, 1)
                        nvd[0] += 1
                        vector.wait_ge(svd, nvd[0])
                        vector.tensor_tensor(e2[:], e1[:],
                                             ps_ab[g % 2][:, 0:256],
                                             OP.add).then_inc(svd, 1)
                        nvd[0] += 1
                        vector.wait_ge(svd, nvd[0])
                        inc(vector.tensor_tensor(
                            ybuf[:, g * 256:(g + 1) * 256], e2[:],
                            xD[:, g * 256:(g + 1) * 256], OP.add), "Y", g)
                        inc(vector.tensor_tensor(
                            S[:, g * 256:(g + 1) * 256],
                            S[:, g * 256:(g + 1) * 256],
                            ps_c[g % 2][:, 0:256], OP.add), "S", g)

            def _tensor(tensor, c=c, pe0=pe0, dve0=dve0, act0=act0, vd0=vd0,
                        pv=pv):
                tensor.wait_ge(svd, vd0 + 1)
                if pv:
                    tensor.wait_ge(svd, pv["vd_ll"])
                    tensor.wait_ge(sdve, pv["dve_cbt"])
                tensor.matmul(ps_misc[:, 0:32], tri[:], adt[:],
                              start=True, stop=True).then_inc(spe, 1)
                for j in range(3, 5):
                    lds.wait(tensor, lds.tok(5 * c + j))
                tensor.matmul(ps_cbt[:], bf[c % 2][:], cf[c % 2][:],
                              start=True, stop=True).then_inc(spe, 1)
                tensor.wait_ge(sdve, REC[("cl", c)])
                tensor.transpose(ps_misc[0:32, 0:128], cl[:],
                                 identf[:]).then_inc(spe, 1)
                tensor.wait_ge(sdve, REC[("clT", c)])
                tensor.transpose(ps_misc[0:1, 0:32], clT[:, 127:128],
                                 identf[0:32, 0:32]).then_inc(spe, 1)
                tensor.wait_ge(sdve, REC[("llrow", c)])
                tensor.matmul(ps_misc[:, 32:64], onesr[:], llrow[:],
                              start=True, stop=True).then_inc(spe, 1)
                for g in range(8):
                    tensor.wait_ge(sdve, REC[("A", c, g)])
                    if g >= 2:
                        tensor.wait_ge(sact, act0 + 3 + 2 * (g - 1))
                    elif pv:
                        tensor.wait_ge(sact, pv["act_end"])
                    mm = None
                    for hh_ in range(4):
                        mm = tensor.matmul(
                            ps_seg[g % 2][:, hh_ * 128:(hh_ + 1) * 128], tris[:],
                            rhs4[g % 2][:, hh_ * 128:(hh_ + 1) * 128],
                            start=True, stop=True)
                    mm.then_inc(spe, 1)
                    tensor.wait_ge(sdve, REC[("W", c, g)])
                    if g >= 2:
                        tensor.wait_ge(sdve, REC[("S", c, g - 2)])
                    elif pv:
                        tensor.wait_ge(sdve, pv["dve_end"])
                    mm = None
                    for hh_ in range(4):
                        h = 4 * g + hh_
                        mm = tensor.matmul(
                            ps_ab[g % 2][:, hh_ * 64:(hh_ + 1) * 64],
                            P4[g % 2][:, hh_ * 128:(hh_ + 1) * 128],
                            xtl[:, h * 64:(h + 1) * 64], start=True, stop=True)
                        mm = tensor.matmul(
                            ps_ab[g % 2][:, 256 + hh_ * 64:256 + (hh_ + 1) * 64],
                            cf[c % 2][:], Sbf[:, h * 64:(h + 1) * 64],
                            start=True, stop=True)
                    for hh_ in range(4):
                        h = 4 * g + hh_
                        mm = tensor.matmul(
                            ps_c[g % 2][:, hh_ * 64:(hh_ + 1) * 64],
                            Bw4[g % 2][:, hh_ * 128:(hh_ + 1) * 128],
                            xtl[:, h * 64:(h + 1) * 64], start=True, stop=True)
                    mm.then_inc(spe, 2)

            def _scalar(scalar, c=c, pe0=pe0, dve0=dve0, act0=act0, vd0=vd0,
                        pv=pv):
                scalar.wait_ge(sdve, REC[("cl", c)])
                scalar.activation(expl[:], cl[:], AF.Exp).then_inc(sact, 1)
                scalar.wait_ge(sdve, REC[("Wd", c)])
                scalar.activation(Wd[:], Wd[:], AF.Exp).then_inc(sact, 1)
                scalar.wait_ge(svd, vd0 + 2)
                scalar.activation(decr[:], Ll[:], AF.Exp).then_inc(sact, 1)
                for g in range(8):
                    scalar.wait_ge(spe, pe0 + 5 + 3 * g + 1)
                    if g >= 2:
                        scalar.wait_ge(sdve, REC[("P", c, g - 2)])
                    elif pv:
                        scalar.wait_ge(sdve, pv["dve_end"])
                    scalar.activation(segx[g % 2][:], ps_seg[g % 2][:],
                                      AF.Exp).then_inc(sact, 2)

            def _gpsimd(gpsimd, c=c, r0=r0):
                gpsimd.wait_ge(sdve, REC[("S", c, 7)])
                ysts.issue(gpsimd, ys_h[r0:r0 + 128, :], ybuf[:])

            bodies["sync"].append(_sync)
            bodies["vector"].append(_vector)
            bodies["tensor"].append(_tensor)
            bodies["scalar"].append(_scalar)
            bodies["gpsimd"].append(_gpsimd)

            prev = {
                "send": dve0 + 7 + 5 * 8,
                "pe_end": pe0 + 5 + 3 * 8,
                "act_end": act0 + 3 + 16,
                "act3": act0 + 3,
                "dve_end": dve0 + 7 + 5 * 8,
                "vd_ll": vd0 + 2,
                "dve_cbt": dve0 + 3,
            }
            p.bump("vd", 2 + 16)
            p.bump("dve", 7 + 5 * 8)
            p.bump("pe", 5 + 3 * 8)
            p.bump("act", 3 + 16)

        @block.vector
        def _(vector):
            for f in bodies["vector"]:
                f(vector)

        @block.sync
        def _(sync):
            for f in bodies["sync"]:
                f(sync)

        @block.tensor
        def _(tensor):
            for f in bodies["tensor"]:
                f(tensor)

        @block.scalar
        def _(scalar):
            for f in bodies["scalar"]:
                f(scalar)

        @block.gpsimd
        def _(gpsimd):
            for f in bodies["gpsimd"]:
                f(gpsimd)

    with nc.Block() as block:
        @block.sync
        def _(sync):
            for pr in range(ysts.np):
                if ysts.counts[pr]:
                    sync.wait_ge(ysts.sems[pr], ysts.counts[pr])

    p.pop_scope()
    if standalone:
        return p.close()


def build_lf(p=None, H=None):
    standalone = p is None
    if standalone:
        p = Prog()
        p.strip_bufs()
    p.H = H or {}
    p.push_scope()
    nc = p.nc
    ys_h = p.dram("ys", [N, 2048], F32, "ExternalInput")
    zs_h = p.dram("zs", [2048, N], BF16, "ExternalInput")
    x2_h = p.dram("x2", [N, D], F32, "ExternalInput")
    nw_h = p.dram("nw", [1, 2048], F32, "ExternalInput")
    opw_h = p.dram("opw", [2048, D], F32, "ExternalInput")
    id_h = p.dram("ident", [128, 128], BF16, "ExternalInput")
    idf_h = p.dram("identf", [128, 128], F32, "ExternalInput")
    x3_h = p.dram("x3", [N, D], F32, "ExternalOutput")
    xln_h = p.dram("xln", [N, D], F32, "ExternalOutput")
    ml_h = p.dram("mloc", [1, 1], F32, "ExternalOutput")

    stats = p.sb("stats", [128, 16 * NSUB], F32)
    rms = p.sb("rms", [128, 12], F32)
    eps = p.sb("eps", [128, 1], F32)
    ident = p.sb("id_sb", [128, 128], BF16)
    identf = p.sb("idf_sb", [128, 128], F32)
    onesr = p.sb("onesr", [1, 128], F32)
    nwrow = p.sb("nwrow", [1, 2048], F32)
    nwrep = p.sb("nwrep", [128, 2048], F32)
    opw = p.sb("opw_sb", [128, 16 * D], BF16)
    mrun = p.sb("mrun", [128, 2], F32)

    p.sems(["ld", "st", "prep", "pe", "dve", "act", "lns", "bank", "mx", "vd"])
    sld, sst = p.s("ld"), p.s("st")
    spe, sdve, sact = p.s("pe"), p.s("dve"), p.s("act")

    with nc.Block() as block:
        emit_param_loads(p, block, [(ident, id_h), (identf, idf_h),
                                    (nwrow, nw_h)])

        @block.vector
        def _(vector):
            vector.memset(eps[:], 1e-5)
            vector.memset(onesr[:], 1.0)
            vector.memset(mrun[:], 0.0)

    ps = [p.psum(f"ps{i}", [128, 512], F32) for i in range(4)]
    ps_t = [p.psum(f"ps_t{i}", [128, 1024], BF16) for i in range(2)]
    ps_b = p.psum("ps_b", [128, 512], F32)

    with nc.Block() as block:
        pe0 = p.val("pe")

        dve0f = p.val("dve")

        @block.tensor
        def _(tensor):
            for j in range(4):
                if j >= 1:
                    tensor.wait_ge(sdve, dve0f + j)
                tensor.matmul(ps_b[:], onesr[:],
                              nwrow[0:1, j * 512:(j + 1) * 512],
                              start=True, stop=True).then_inc(spe, 1)

        @block.vector
        def _(vector):
            for j in range(4):
                vector.wait_ge(spe, pe0 + j + 1)
                vector.tensor_copy(nwrep[:, j * 512:(j + 1) * 512],
                                   ps_b[:]).then_inc(sdve, 1)
        p.bump("pe", 4)
        p.bump("dve", 4)

    with p.scope() as prep_scope:
        with nc.Block() as block:
            emit_weight_prep(p, block, [{"src": opw_h, "dst_sb": opw}],
                             prep_scope)

    ybuf = p.sb("ybuf", [128, NSUB * 2048], F32)
    zbuf = p.sb("zbuf", [128, 16 * TT], BF16)
    szt = p.sb("szt", [128, NSUB * 2048], BF16)
    gbf = szt
    g_fm = p.sb("g_fm", [128, 16 * TT], BF16)
    x2b = p.sb("x2b", [128, NSUB * D], F32)
    x3b = p.sb("x3b", [128, NSUB * D], F32)
    xlnb = p.sb("xlnb", [128, NSUB * D], F32)
    y2 = p.sb("y2", [128, 8 * TT], BF16)
    sqd = p.sb("sqd", [128, 2048], BF16)
    gscr = p.sb("gscr", [128, 2048], F32)

    for i in range(NTILES):
        r0 = i * TT
        with nc.Block() as block:
            st0 = p.val("st")

            @block.sync
            def _(sync):
                sync.wait_ge(sst, st0)
                sync.dma_start(out=tok_sb(ybuf[:], NSUB, 2048),
                               in_=ys_h[r0:r0 + TT, :].rearrange(
                                   "(s p) f -> p s f", p=128)).then_inc(sld, 16)
                sync.dma_start(
                    out=zbuf[:].rearrange("p (c t) -> p c t", c=16),
                    in_=zs_h[:, r0:r0 + TT].rearrange("(c p) t -> p c t", p=128),
                ).then_inc(sld, 16)
                sync.dma_start(out=tok_sb(x2b[:], NSUB, D),
                               in_=tok_in(x2_h, r0)).then_inc(sld, 16)
            p.bump("ld", 48)

        with nc.Block() as block:
            pe0, dve0 = p.val("pe"), p.val("dve")

            @block.tensor
            def _(tensor):
                tensor.wait_ge(sld, p.val("ld"))
                t = 0
                for s in range(NSUB):
                    for gg in range(2):
                        if t >= 2:
                            tensor.wait_ge(sdve, dve0 + t - 1)
                        mm = None
                        for cc in range(8):
                            c = gg * 8 + cc
                            mm = tensor.transpose(
                                ps_t[t % 2][:, cc * 128:(cc + 1) * 128],
                                zbuf[:, c * TT + s * 128:c * TT + s * 128 + 128],
                                ident[:])
                        mm.then_inc(spe, 1)
                        t += 1

            @block.vector
            def _(vector):
                t = 0
                for s in range(NSUB):
                    for gg in range(2):
                        vector.wait_ge(spe, pe0 + t + 1)
                        vector.tensor_copy(
                            szt[:, s * 2048 + gg * 1024:s * 2048 + (gg + 1) * 1024],
                            ps_t[t % 2][:]).then_inc(sdve, 1)
                        t += 1
            p.bump("pe", 8)
            p.bump("dve", 8)

        with nc.Block() as block:
            dve0, act0 = p.val("dve"), p.val("act")

            @block.vector
            def _(vector):
                for s in range(NSUB):
                    vector.tensor_tensor(ybuf[:, s * 2048:(s + 1) * 2048],
                                         ybuf[:, s * 2048:(s + 1) * 2048],
                                         szt[:, s * 2048:(s + 1) * 2048],
                                         OP.mult).then_inc(sdve, 1)
                    vector.drain()
                vector.wait_ge(sact, act0 + NSUB)
                vector.tensor_scalar(rms[:, 4:8], rms[:, 0:4], 1.0 / 2048.0,
                                     None, OP.mult).then_inc(sdve, 1)
                vector.wait_ge(sact, act0 + NSUB + 1)
                vector.reciprocal(rms[:, 8:12], rms[:, 4:8])
                vector.drain()
                for s in range(NSUB):
                    vector.tensor_scalar(gscr[:], ybuf[:, s * 2048:(s + 1) * 2048],
                                         rms[:, 8 + s:9 + s], None, OP.mult)
                    vector.drain()
                    vector.tensor_tensor(gbf[:, s * 2048:(s + 1) * 2048],
                                         gscr[:], nwrep[:], OP.mult)
                    vector.drain()

            @block.scalar
            def _(scalar):
                for s in range(NSUB):
                    scalar.wait_ge(sdve, dve0 + s + 1)
                    scalar.activation(sqd[:], ybuf[:, s * 2048:(s + 1) * 2048],
                                      AF.Square, accum_out=rms[:, s:s + 1])
                    scalar.drain()
                scalar.nop().then_inc(sact, NSUB)
                scalar.wait_ge(sdve, dve0 + NSUB + 1)
                scalar.activation(rms[:, 4:8], rms[:, 4:8], AF.Sqrt,
                                  bias=eps[:, 0:1]).then_inc(sact, 1)
            p.bump("dve", NSUB + 1)
            p.bump("act", NSUB + 1)

        with nc.Block() as block:
            emit_tok2fm(p, block, gbf, g_fm, ident, ps_t, nchunks=16)

        with nc.Block() as block:
            pe0, dve0 = p.val("pe"), p.val("dve")

            @block.tensor
            def _(tensor):
                for g2 in range(2):
                    if g2 == 1:
                        tensor.wait_ge(sdve, dve0 + 4)
                    mm = None
                    for k in range(16):
                        for mi in range(4):
                            m = g2 * 4 + mi
                            mm = tensor.matmul(
                                ps[mi][:],
                                opw[:, k * D + m * 128:k * D + (m + 1) * 128],
                                g_fm[:, k * TT:(k + 1) * TT],
                                start=(k == 0), stop=(k == 15))
                    mm.then_inc(spe, 1)
                for s in range(NSUB):
                    tensor.wait_ge(sdve, dve0 + 8 + max(0, s - 1))
                    mm = None
                    for c in range(8):
                        mm = tensor.transpose(
                            ps_t[s % 2][:, c * 128:(c + 1) * 128],
                            y2[:, c * TT + s * 128:c * TT + (s + 1) * 128],
                            ident[:])
                    mm.then_inc(spe, 1)

            @block.vector
            def _(vector):
                for g2 in range(2):
                    vector.wait_ge(spe, pe0 + g2 + 1)
                    for mi in range(4):
                        m = g2 * 4 + mi
                        vector.tensor_copy(y2[:, m * TT:(m + 1) * TT],
                                           ps[mi][:]).then_inc(sdve, 1)
                for s in range(NSUB):
                    vector.wait_ge(spe, pe0 + 2 + s + 1)
                    vector.tensor_tensor(x3b[:, s * D:(s + 1) * D],
                                         x2b[:, s * D:(s + 1) * D],
                                         ps_t[s % 2][:, 0:D],
                                         OP.add).then_inc(sdve, 1)
            p.bump("pe", 2 + NSUB)
            p.bump("dve", 8 + NSUB)

        with nc.Block() as block:
            emit_ln(p, block, x3b, xlnb, stats, eps, done_inc=(sdve, 1))
            p.bump("dve")
            dve0 = p.val("dve")

            @block.vector
            def _(vector):
                vector.wait_ge(sdve, dve0)
                for s in range(NSUB):
                    vector.reduce_max(mrun[:, 1:2], xlnb[:, s * D:(s + 1) * D],
                                      axis=AX.X, apply_absolute_value=True)
                    vector.drain()
                    vector.tensor_tensor(mrun[:, 0:1], mrun[:, 0:1],
                                         mrun[:, 1:2], OP.max)
                    vector.drain()
                vector.nop().then_inc(sdve, 1)
            p.bump("dve")

            @block.sync
            def _(sync):
                sync.wait_ge(sdve, p.val("dve"))
                sync.dma_start(out=tok_in(x3_h, r0),
                               in_=tok_sb(x3b[:], NSUB, D)).then_inc(sst, 16)
                sync.dma_start(out=tok_in(xln_h, r0),
                               in_=tok_sb(xlnb[:], NSUB, D)).then_inc(sst, 16)
            p.bump("st", 32)

    with nc.Block() as block:
        pe0, dve0 = p.val("pe"), p.val("dve")

        @block.tensor
        def _(tensor):
            tensor.transpose(ps_b[0:1, 0:128], mrun[:, 0:1],
                             identf[:]).then_inc(spe, 1)

        @block.vector
        def _(vector):
            vector.wait_ge(spe, pe0 + 1)
            vector.reduce_max(mrun[0:1, 1:2], ps_b[0:1, 0:128],
                              axis=AX.X).then_inc(sdve, 1)
        p.bump("pe", 1)
        p.bump("dve", 1)

    with nc.Block() as block:
        @block.sync
        def _(sync):
            sync.dma_start(out=ml_h, in_=mrun[0:1, 1:2]).then_inc(sst, 16)
            p.bump("st", 16)
            sync.wait_ge(sst, p.val("st"))

    p.pop_scope()
    if standalone:
        return p.close()


def ld_inmaps(x2_full, ins):
    ipw = np.asarray(ins["L1_attn_in_proj_w"])
    com = {
        "ipw": np.ascontiguousarray(ipw[:, 0:4352]),
        "ipwdt": np.ascontiguousarray(ipw[:, 4352:4384]),
        "dtb": np.ascontiguousarray(np.asarray(ins["L1_attn_dt_bias"]).reshape(32, 1)),
        "ident": eye_bf16(),
        "identf": np.eye(128, dtype=np.float32),
    }
    return [dict(com, x2=np.ascontiguousarray(x2_full[c])) for c in range(NCORES)]


def ld2_inmaps(xbcs_list, ins):
    cwr = np.asarray(ins["L1_attn_conv_w"])[::-1]
    cw = np.ascontiguousarray(cwr.T.reshape(18, 128, 4).transpose(1, 0, 2)
                              .reshape(128, 72))
    com = {
        "convw": cw,
        "convb": vec_pm(ins["L1_attn_conv_b"]),
        "ident": eye_bf16(),
    }
    return [dict(com, xbcs=np.ascontiguousarray(xbcs_list[c]))
            for c in range(NCORES)]


def le_inmaps(xpt_l, bts_l, bfs_l, cfs_l, dts_l, ins):
    r = np.arange(128)
    tri = (r[:, None] <= r[None, :]).astype(np.float32)
    tris = (r[:, None] > r[None, :]).astype(np.float32)
    com = {
        "alog": np.ascontiguousarray(np.asarray(ins["L1_attn_A_log"]).reshape(1, 32)),
        "dvec": np.ascontiguousarray(np.asarray(ins["L1_attn_D"]).reshape(1, 32)),
        "tri": tri, "tris": tris,
        "identf": np.eye(128, dtype=np.float32),
    }
    return [dict(com, xpt=np.ascontiguousarray(xpt_l[c]),
                 bts=np.ascontiguousarray(bts_l[c]),
                 bfs=np.ascontiguousarray(bfs_l[c]),
                 cfs=np.ascontiguousarray(cfs_l[c]),
                 dts=np.ascontiguousarray(dts_l[c])) for c in range(NCORES)]


def lf_inmaps(ys_l, zs_l, x2_full, ins):
    com = {
        "nw": np.ascontiguousarray(np.asarray(ins["L1_attn_norm_w"]).reshape(1, 2048)),
        "opw": np.ascontiguousarray(ins["L1_attn_out_proj_w"]),
        "ident": eye_bf16(),
        "identf": np.eye(128, dtype=np.float32),
    }
    return [dict(com, ys=np.ascontiguousarray(ys_l[c]),
                 zs=np.ascontiguousarray(zs_l[c]),
                 x2=np.ascontiguousarray(x2_full[c])) for c in range(NCORES)]


def emit_wquant_rows(p, block_seq, w_h, wT_s, nch, ident, identf, name, C=1024):
    nc = p.nc
    spe, sdve = p.s("pe"), p.s("dve")
    with p.scope() as scope:
        wtmp = [p.sb_in(scope, f"{name}wt{i}", [128, C], F32) for i in range(2)]
        wq = p.sb_in(scope, f"{name}wq", [128, C], F32)
        wqb = p.sb_in(scope, f"{name}wqb", [128, C], BF16)
        pcs = p.sb_in(scope, f"{name}pc", [128, C], BF16)
        sca = p.sb_in(scope, f"{name}sca", [128, 4], F32)
        ps_tw = [scope.enter_context(nc.psum_tensor(p._nm(f"{name}pst{i}"),
                                                    [128, 1024], BF16))
                 for i in range(2)]
        lds = p.stream("wqrld")
        sts = p.stream("wqrst", nparity=8)
        lb0, sb0 = lds.i, sts.i

        with nc.Block() as block:
            dve0, pe0 = p.val("dve"), p.val("pe")
            npieces = C // 128
            NGR = (npieces + 7) // 8
            PERK = 2 + NGR

            @block.sync
            def _(sync):
                for k in range(nch):
                    if k >= 2:
                        sync.wait_ge(sdve, dve0 + (k - 2) * PERK + 1)
                    lds.issue(sync, wtmp[k % 2][:], w_h[k * 128:(k + 1) * 128, :])

            @block.vector
            def _(vector):
                for k in range(nch):
                    lds.wait(vector, lds.tok(lb0 + k))
                    vector.reduce_max(sca[:, 0:1], wtmp[k % 2][:], axis=AX.X,
                                      apply_absolute_value=True)
                    vector.drain()
                    vector.reciprocal(sca[:, 1:2], sca[:, 0:1])
                    vector.drain()
                    vector.tensor_scalar(sca[:, 1:2], sca[:, 1:2], 127.0, None,
                                         OP.mult)
                    vector.tensor_scalar(sca[:, 2:3], sca[:, 0:1], 1.0 / 127.0,
                                         None, OP.mult)
                    vector.drain()
                    vector.tensor_scalar(wq[:], wtmp[k % 2][:], sca[:, 1:2],
                                         MAGIC, OP.mult, OP.add).then_inc(sdve, 1)
                    vector.drain()
                    vector.tensor_scalar(wq[:], wq[:], MAGIC, -128.0,
                                         OP.subtract, OP.max)
                    vector.drain()
                    vector.tensor_scalar(wqb[:], wq[:], 127.0, sca[:, 2:3],
                                         OP.min, OP.mult).then_inc(sdve, 1)
                    vector.drain()
                    ngr = (npieces + 7) // 8
                    for g in range(ngr):
                        gw = min(8, npieces - g * 8)
                        gi = k * ngr + g
                        vector.wait_ge(spe, pe0 + gi + 1)
                        if k >= 1:
                            sts.prefix_wait(vector,
                                            sb0 + (k - 1) * npieces + g * 8 + gw)
                        vector.tensor_copy(
                            pcs[:, g * 1024:g * 1024 + gw * 128],
                            ps_tw[gi % 2][:, 0:gw * 128]).then_inc(sdve, 1)

            @block.tensor
            def _(tensor):
                ngr = (npieces + 7) // 8
                for k in range(nch):
                    tensor.wait_ge(sdve, dve0 + k * PERK + 2)
                    for g in range(ngr):
                        gw = min(8, npieces - g * 8)
                        gi = k * ngr + g
                        if gi >= 2:
                            gp = gi - 2
                            kp, gp2 = divmod(gp, ngr)
                            tensor.wait_ge(sdve, dve0 + kp * PERK + 2 + gp2 + 1)
                        mm = None
                        for jj in range(gw):
                            j = g * 8 + jj
                            mm = tensor.transpose(
                                ps_tw[gi % 2][:, jj * 128:(jj + 1) * 128],
                                wqb[:, j * 128:(j + 1) * 128],
                                ident[:])
                        mm.then_inc(spe, 1)

            @block.gpsimd
            def _(gpsimd):
                for k in range(nch):
                    for j in range(npieces):
                        t = sb0 + k * npieces + j
                        if t >= 8:
                            gpsimd.wait_ge(sts.sems[t % 8], 16 * (t // 8))
                        gpsimd.wait_ge(sdve, dve0 + k * PERK + 2 + (j // 8) + 1)
                        sts.issue(gpsimd,
                                  wT_s[j * 128:(j + 1) * 128, k * 128:(k + 1) * 128],
                                  pcs[:, j * 128:(j + 1) * 128])

            for k in range(nch):
                p.bump("dve", PERK)
                p.bump("pe", NGR)

        with nc.Block() as block:
            @block.sync
            def _(sync):
                for pr in range(sts.np):
                    if sts.counts[pr]:
                        sync.wait_ge(sts.sems[pr], sts.counts[pr])


def build_lh(ntiles=NTILES, p=None, H=None, mall_w=8):
    standalone = p is None
    if standalone:
        p = Prog()
        p.strip_bufs()
    p.H = H or {}
    p.push_scope()
    nc = p.nc
    I2 = 2816
    xln_h = p.dram("xln", [N, D], F32, "ExternalInput")
    mall_h = p.dram("mall", [1, mall_w], F32, "ExternalInput")
    clip_h = p.dram("clipx", [1, 1], F32, "ExternalInput")
    wg_h = p.dram("wg", [I2, D], F32, "ExternalInput")
    wu_h = p.dram("wu", [I2, D], F32, "ExternalInput")
    id_h = p.dram("ident", [128, 128], BF16, "ExternalInput")
    idf_h = p.dram("identf", [128, 128], F32, "ExternalInput")
    hs_h = p.dram("hs2", [I2, N], BF16, "ExternalOutput")
    m2_h = p.dram("m2loc", [1, 1], F32, "ExternalOutput")
    wgT_s = p.dram("wgT", [D, I2], BF16)
    wuT_s = p.dram("wuT", [D, I2], BF16)

    ident = p.sb("id_sb", [128, 128], BF16)
    identf = p.sb("idf_sb", [128, 128], F32)
    onesr = p.sb("onesr", [1, 128], F32)
    mall = p.sb("mall_sb", [1, mall_w], F32)
    clipx = p.sb("clip_sb", [1, 1], F32)
    gsc = p.sb("gsc", [1, 8], F32)
    gscb = p.sb("gscb", [128, 2], F32)
    mrun = p.sb("mrun", [128, 2], F32)
    qst = p.sb("qst", [128, 4 * NSUB], F32)

    p.sems(["ld", "st", "prep", "pe", "dve", "act", "lns", "bank", "mx", "vd"])
    sld, sst = p.s("ld"), p.s("st")
    spe, sdve, sact = p.s("pe"), p.s("dve"), p.s("act")

    with nc.Block() as block:
        emit_param_loads(p, block, [(ident, id_h), (identf, idf_h),
                                    (mall, mall_h), (clipx, clip_h)])

        @block.vector
        def _(vector):
            vector.memset(onesr[:], 1.0)
            vector.memset(mrun[:], 0.0)

    emit_wquant_rows(p, None, wg_h, wgT_s, 22, ident, identf, "wg")
    emit_wquant_rows(p, None, wu_h, wuT_s, 22, ident, identf, "wu")

    ps = [p.psum(f"ps{i}", [128, 512], F32) for i in range(4)]
    ps_t = [p.psum(f"ps_t{i}", [128, 1024], BF16) for i in range(2)]
    ps_b = p.psum("ps_b", [128, 512], F32)

    with nc.Block() as block:
        pe0, dve0, act0 = p.val("pe"), p.val("dve"), p.val("act")

        @block.vector
        def _(vector):
            vector.reduce_max(gsc[0:1, 0:1], mall[0:1, :],
                              axis=AX.X).then_inc(sdve, 1)
            vector.drain()
            vector.wait_ge(sact, act0 + 1)
            vector.tensor_tensor(gsc[0:1, 2:3], gsc[0:1, 0:1], gsc[0:1, 1:2],
                                 OP.mult)
            vector.drain()
            vector.tensor_scalar(gsc[0:1, 2:3], gsc[0:1, 2:3], 1.0 / 127.0,
                                 None, OP.mult)
            vector.drain()
            vector.reciprocal(gsc[0:1, 3:4], gsc[0:1, 2:3]).then_inc(sdve, 1)
            vector.drain()

        @block.scalar
        def _(scalar):
            scalar.wait_ge(sdve, dve0 + 1)
            scalar.activation(gsc[0:1, 1:2], clipx[0:1, 0:1],
                              AF.Sigmoid).then_inc(sact, 1)

        @block.tensor
        def _(tensor):
            tensor.wait_ge(sdve, dve0 + 2)
            tensor.matmul(ps_b[:, 0:2], onesr[:], gsc[0:1, 2:4],
                          start=True, stop=True).then_inc(spe, 1)

        @block.vector
        def _(vector):
            vector.wait_ge(spe, pe0 + 1)
            vector.tensor_copy(gscb[:, 0:1], ps_b[:, 1:2])
            vector.tensor_copy(gscb[:, 1:2], ps_b[:, 0:1])
        p.bump("pe", 1)
        p.bump("dve", 2)
        p.bump("act", 1)


    xbuf = p.sb("xbuf", [128, NSUB * D], F32)
    xq = p.sb("xq", [128, NSUB * D], BF16)
    xq_fm = p.sb("xq_fm", [128, 8 * TT], BF16)
    f32s = p.sb("f32s", [128, D], F32)
    sg = p.sb("sg", [128, 22 * TT], BF16)
    su = p.sb("su", [128, 22 * TT], BF16)
    strip = p.strip_bufs()
    strips = p.stream("strips", 12)
    hsts = p.stream("aux_st")

    for i in range(ntiles):
        r0 = i * TT
        with nc.Block() as block:
            @block.sync
            def _(sync):
                sync.dma_start(out=tok_sb(xbuf[:], NSUB, D),
                               in_=tok_in(xln_h, r0)).then_inc(sld, 16)
            p.bump("ld", 16)

            @block.vector
            def _(vector):
                vector.wait_ge(sld, p.val("ld"))
                for s in range(NSUB):
                    vector.tensor_scalar(f32s[:], xbuf[:, s * D:(s + 1) * D],
                                         gscb[:, 0:1], MAGIC, OP.mult, OP.add)
                    vector.drain()
                    vector.tensor_scalar(f32s[:], f32s[:], MAGIC, -128.0,
                                         OP.subtract, OP.max)
                    vector.drain()
                    vector.tensor_scalar(xq[:, s * D:(s + 1) * D], f32s[:],
                                         127.0, gscb[:, 1:2], OP.min, OP.mult)
                    vector.drain()

        with nc.Block() as block:
            emit_tok2fm(p, block, xq, xq_fm, ident, ps_t, nchunks=8)

        if i >= 1:
            with nc.Block() as block:
                @block.scalar
                def _(scalar):
                    for pr in range(hsts.np):
                        if hsts.counts[pr]:
                            scalar.wait_ge(hsts.sems[pr], hsts.counts[pr])

                @block.vector
                def _(vector):
                    for pr in range(hsts.np):
                        if hsts.counts[pr]:
                            vector.wait_ge(hsts.sems[pr], hsts.counts[pr])

        with nc.Block() as block:
            evacs = []
            for m in range(22):
                def f(eng, pst, m=m):
                    return eng.activation(sg[:, m * TT:(m + 1) * TT], pst, AF.Silu)
                evacs.append(f)
            emit_mm_stream(p, block, wgT_s, xq_fm, 8, 22, strip, strips, ps,
                           "scalar", evacs)
            evacs = []
            for m in range(22):
                def f(eng, pst, m=m):
                    return eng.tensor_copy(su[:, m * TT:(m + 1) * TT], pst)
                evacs.append(f)
            emit_mm_stream(p, block, wuT_s, xq_fm, 8, 22, strip, strips, ps,
                           "vector", evacs)

        with nc.Block() as block:
            dve0 = p.val("dve")

            @block.vector
            def _(vector):
                vector.tensor_tensor(sg[:], sg[:], su[:], OP.mult)
                vector.drain()
                vector.reduce_max(mrun[:, 1:2], sg[:], axis=AX.X,
                                  apply_absolute_value=True)
                vector.drain()
                vector.tensor_tensor(mrun[:, 0:1], mrun[:, 0:1], mrun[:, 1:2],
                                     OP.max).then_inc(sdve, 1)
                vector.drain()
            p.bump("dve", 1)

            @block.gpsimd
            def _(gpsimd):
                gpsimd.wait_ge(sdve, p.val("dve"))
                for c in range(22):
                    hsts.issue(gpsimd, hs_h[c * 128:(c + 1) * 128, r0:r0 + TT],
                               sg[:, c * TT:(c + 1) * TT])

    with nc.Block() as block:
        pe0, dve0 = p.val("pe"), p.val("dve")

        @block.tensor
        def _(tensor):
            tensor.transpose(ps_b[0:1, 0:128], mrun[:, 0:1],
                             identf[:]).then_inc(spe, 1)

        @block.vector
        def _(vector):
            vector.wait_ge(spe, pe0 + 1)
            vector.reduce_max(mrun[0:1, 1:2], ps_b[0:1, 0:128],
                              axis=AX.X).then_inc(sdve, 1)
        p.bump("pe", 1)
        p.bump("dve", 1)

    with nc.Block() as block:
        @block.sync
        def _(sync):
            sync.wait_ge(sdve, p.val("dve"))
            sync.dma_start(out=m2_h, in_=mrun[0:1, 1:2]).then_inc(sst, 16)
            p.bump("st", 16)
            sync.wait_ge(sst, p.val("st"))
            for pr in range(hsts.np):
                if hsts.counts[pr]:
                    sync.wait_ge(hsts.sems[pr], hsts.counts[pr])

    p.pop_scope()
    if standalone:
        return p.close()


def build_li(p=None, H=None, mall_w=8):
    standalone = p is None
    if standalone:
        p = Prog()
        p.strip_bufs()
    p.H = H or {}
    p.push_scope()
    nc = p.nc
    I2 = 2816
    hs_h = p.dram("hs2", [I2, N], BF16, "ExternalInput")
    m2a_h = p.dram("m2all", [1, mall_w], F32, "ExternalInput")
    clip_h = p.dram("clipd", [1, 1], F32, "ExternalInput")
    wd_h = p.dram("wd", [D, I2], F32, "ExternalInput")
    x3_h = p.dram("x3", [N, D], F32, "ExternalInput")
    id_h = p.dram("ident", [128, 128], BF16, "ExternalInput")
    idf_h = p.dram("identf", [128, 128], F32, "ExternalInput")
    out_h = p.dram("out", [N, D], F32, "ExternalOutput")
    wdT_s = p.dram("wdT", [I2, D], BF16)

    ident = p.sb("id_sb", [128, 128], BF16)
    identf = p.sb("idf_sb", [128, 128], F32)
    onesr = p.sb("onesr", [1, 128], F32)
    m2a = p.sb("m2a_sb", [1, mall_w], F32)
    clipd = p.sb("clip_sb", [1, 1], F32)
    gsc = p.sb("gsc", [1, 8], F32)
    gscb = p.sb("gscb", [128, 2], F32)

    p.sems(["ld", "st", "prep", "pe", "dve", "act", "lns", "bank", "mx", "vd"])
    sld, sst = p.s("ld"), p.s("st")
    spe, sdve, sact = p.s("pe"), p.s("dve"), p.s("act")

    with nc.Block() as block:
        emit_param_loads(p, block, [(ident, id_h), (identf, idf_h),
                                    (m2a, m2a_h), (clipd, clip_h)])

        @block.vector
        def _(vector):
            vector.memset(onesr[:], 1.0)

    emit_wquant_rows(p, None, wd_h, wdT_s, 8, ident, identf, "wd", C=I2)

    ps = [p.psum(f"ps{i}", [128, 512], F32) for i in range(4)]
    ps_t = [p.psum(f"ps_t{i}", [128, 1024], BF16) for i in range(2)]
    ps_b = p.psum("ps_b", [128, 512], F32)

    with nc.Block() as block:
        pe0, dve0, act0 = p.val("pe"), p.val("dve"), p.val("act")

        @block.vector
        def _(vector):
            vector.reduce_max(gsc[0:1, 0:1], m2a[0:1, :],
                              axis=AX.X).then_inc(sdve, 1)
            vector.drain()
            vector.wait_ge(sact, act0 + 1)
            vector.tensor_tensor(gsc[0:1, 2:3], gsc[0:1, 0:1], gsc[0:1, 1:2],
                                 OP.mult)
            vector.drain()
            vector.tensor_scalar(gsc[0:1, 2:3], gsc[0:1, 2:3], 1.0 / 127.0,
                                 None, OP.mult)
            vector.drain()
            vector.reciprocal(gsc[0:1, 3:4], gsc[0:1, 2:3]).then_inc(sdve, 1)
            vector.drain()

        @block.scalar
        def _(scalar):
            scalar.wait_ge(sdve, dve0 + 1)
            scalar.activation(gsc[0:1, 1:2], clipd[0:1, 0:1],
                              AF.Sigmoid).then_inc(sact, 1)

        @block.tensor
        def _(tensor):
            tensor.wait_ge(sdve, dve0 + 2)
            tensor.matmul(ps_b[:, 0:2], onesr[:], gsc[0:1, 2:4],
                          start=True, stop=True).then_inc(spe, 1)

        @block.vector
        def _(vector):
            vector.wait_ge(spe, pe0 + 1)
            vector.tensor_copy(gscb[:, 0:1], ps_b[:, 1:2])
            vector.tensor_copy(gscb[:, 1:2], ps_b[:, 0:1])
        p.bump("pe", 1)
        p.bump("dve", 2)
        p.bump("act", 1)


    hbuf = p.sb("hbuf", [128, 22 * TT], BF16)
    f32s = p.sb("f32s", [128, TT], F32)
    x3b = p.sb("x3b", [128, NSUB * D], F32)
    outb = p.sb("outb", [128, NSUB * D], F32)
    y_fm = p.sb("y_fm", [128, 8 * TT], BF16)
    strip = p.strip_bufs()
    strips = p.stream("strips", 12)
    hlds = p.stream("aux_ld")

    for i in range(NTILES):
        r0 = i * TT
        with nc.Block() as block:
            st0 = p.val("st")

            @block.sync
            def _(sync):
                sync.wait_ge(sst, st0)
                for c in range(22):
                    hlds.issue(sync, hbuf[:, c * TT:(c + 1) * TT],
                               hs_h[c * 128:(c + 1) * 128, r0:r0 + TT])
                sync.dma_start(out=tok_sb(x3b[:], NSUB, D),
                               in_=tok_in(x3_h, r0)).then_inc(sld, 16)
            p.bump("ld", 16)

            @block.vector
            def _(vector):
                for pr in range(hlds.np):
                    vector.wait_ge(hlds.sems[pr], hlds.counts[pr])
                vector.wait_ge(sld, p.val("ld"))
                for c in range(22):
                    vector.tensor_scalar(f32s[:], hbuf[:, c * TT:(c + 1) * TT],
                                         gscb[:, 0:1], MAGIC, OP.mult, OP.add)
                    vector.drain()
                    vector.tensor_scalar(f32s[:], f32s[:], MAGIC, -128.0,
                                         OP.subtract, OP.max)
                    vector.drain()
                    vector.tensor_scalar(hbuf[:, c * TT:(c + 1) * TT], f32s[:],
                                         127.0, gscb[:, 1:2], OP.min, OP.mult)
                    vector.drain()

        with nc.Block() as block:
            evacs = []
            for m in range(8):
                def f(eng, pst, m=m):
                    return eng.tensor_copy(y_fm[:, m * TT:(m + 1) * TT], pst)
                evacs.append(f)
            emit_mm_stream(p, block, wdT_s, hbuf, 22, 8, strip, strips, ps,
                           "vector", evacs)

        with nc.Block() as block:
            pe0, dve0 = p.val("pe"), p.val("dve")

            @block.tensor
            def _(tensor):
                for s in range(NSUB):
                    if s >= 2:
                        tensor.wait_ge(sdve, dve0 + s - 1)
                    mm = None
                    for c in range(8):
                        mm = tensor.transpose(
                            ps_t[s % 2][:, c * 128:(c + 1) * 128],
                            y_fm[:, c * TT + s * 128:c * TT + (s + 1) * 128],
                            ident[:])
                    mm.then_inc(spe, 1)

            @block.vector
            def _(vector):
                for s in range(NSUB):
                    vector.wait_ge(spe, pe0 + s + 1)
                    vector.tensor_tensor(outb[:, s * D:(s + 1) * D],
                                         x3b[:, s * D:(s + 1) * D],
                                         ps_t[s % 2][:, 0:D],
                                         OP.add).then_inc(sdve, 1)

            @block.sync
            def _(sync):
                sync.wait_ge(sdve, dve0 + NSUB)
                sync.dma_start(out=tok_in(out_h, r0),
                               in_=tok_sb(outb[:], NSUB, D)).then_inc(sst, 16)
            p.bump("pe", NSUB)
            p.bump("dve", NSUB)
            p.bump("st", 16)

    with nc.Block() as block:
        @block.sync
        def _(sync):
            sync.wait_ge(sst, p.val("st"))

    p.pop_scope()
    if standalone:
        return p.close()


def lh_inmaps(xln_l, mall, ins):
    com = {
        "mall": np.ascontiguousarray(np.asarray(mall, np.float32).reshape(1, 8)),
        "clipx": np.ascontiguousarray(np.asarray(ins["L1_ffn_clip_x"],
                                                 np.float32).reshape(1, 1)),
        "wg": np.ascontiguousarray(ins["L1_ffn_w_gate"]),
        "wu": np.ascontiguousarray(ins["L1_ffn_w_up"]),
        "ident": eye_bf16(),
        "identf": np.eye(128, dtype=np.float32),
    }
    return [dict(com, xln=np.ascontiguousarray(xln_l[c])) for c in range(NCORES)]


def li_inmaps(hs_l, m2all, x3_l, ins):
    com = {
        "m2all": np.ascontiguousarray(np.asarray(m2all, np.float32).reshape(1, 8)),
        "clipd": np.ascontiguousarray(np.asarray(ins["L1_ffn_clip_down"],
                                                 np.float32).reshape(1, 1)),
        "wd": np.ascontiguousarray(ins["L1_ffn_w_down"]),
        "ident": eye_bf16(),
        "identf": np.eye(128, dtype=np.float32),
    }
    return [dict(com, hs2=np.ascontiguousarray(hs_l[c]),
                 x3=np.ascontiguousarray(x3_l[c])) for c in range(NCORES)]


def build_la(ntiles=NTILES, p=None, H=None):
    standalone = p is None
    if standalone:
        p = Prog()
        p.strip_bufs()
    p.H = H or {}
    p.push_scope()
    nc = p.nc
    x_h = p.dram("x", [N, D], F32, "ExternalInput")
    w_h = {}
    for w in ["wq", "wk", "wv", "wo"]:
        w_h[w] = p.dram(w, [D, D], F32, "ExternalInput")
    b_h = {}
    for b in ["bq", "bk", "bv", "bo", "ln1w", "ln1b"]:
        b_h[b] = p.dram(b, [128, 8], F32, "ExternalInput")
    id_h = p.dram("ident", [128, 128], BF16, "ExternalInput")
    bo_h = p.dram("bones", [128, 128], BF16, "ExternalInput")
    xa_h = p.dram("xa", [N, D], F32, "ExternalOutput")
    wo_s = p.dram("wo_bf", [D, D], BF16)

    stats = p.sb("stats", [128, 16 * NSUB], F32)
    eps = p.sb("eps", [128, 1], F32)
    ident = p.sb("id_sb", [128, 128], BF16)
    bones = p.sb("bones_sb", [128, 128], BF16)
    par = {}
    for b in b_h:
        par[b] = p.sb(b + "_sb", [128, 8], F32)
    wsb = {}
    for w in ["wq", "wk", "wv"]:
        wsb[w] = p.sb(w + "_sb", [128, 8 * D], BF16)

    p.sems(["ld", "st", "prep", "pe", "dve", "act", "lns", "bank", "mx", "vd"])
    sld, sst = p.s("ld"), p.s("st")
    spe, sdve, sact = p.s("pe"), p.s("dve"), p.s("act")
    sbank, smx = p.s("bank"), p.s("mx")

    with nc.Block() as block:
        emit_param_loads(p, block, [(ident, id_h), (bones, bo_h)] +
                         [(par[b], b_h[b]) for b in b_h])

        @block.vector
        def _(vector):
            vector.memset(eps[:], 1e-5)

    with p.scope() as prep_scope:
        with nc.Block() as block:
            sts, _nh = emit_weight_prep(p, block, [
                {"src": w_h["wq"], "dst_sb": wsb["wq"]},
                {"src": w_h["wk"], "dst_sb": wsb["wk"]},
                {"src": w_h["wv"], "dst_sb": wsb["wv"]},
                {"src": w_h["wo"], "dst_hbm": wo_s},
            ], prep_scope)
        with nc.Block() as block:
            @block.sync
            def _(sync):
                for pr in range(sts.np):
                    if sts.counts[pr]:
                        sync.wait_ge(sts.sems[pr], sts.counts[pr])

    xbuf = p.sb("xbuf", [128, NSUB * D], F32)
    qx = p.sb("qx", [128, NSUB * D], BF16)
    xn2 = p.sb("xn2", [128, NSUB * D], BF16)
    xn_fm = p.sb("xn_fm", [128, 8 * TT], BF16)
    fq = p.sb("fq", [128, 8 * TT], BF16)
    fk = p.sb("fk", [128, 16 * TT], BF16)
    fkd = p.sb("fkd", [128, 15 * TT], BF16)
    v_fm = p.sb("v_fm", [128, 8 * TT], BF16)
    veh = p.sb("veh", [128, NSUB * 2048], BF16)
    tmin = p.sb("tmin", [128, TT], F32)
    texp = p.sb("texp", [128, TT], BF16)
    Pb = [p.sb(f"P{i}", [128, 8 * TT], BF16) for i in range(2)]
    g_sb = p.sb("g_sb", [16, 16 * TT], BF16)
    G_t = p.sb("G_t", [128, NSUB * 256], BF16)
    qk = p.sb("qk", [128, NSUB * 16], F32)
    rcp = p.sb("rcp", [128, NSUB * 16], F32)
    Gn = p.sb("Gn", [128, NSUB * 256], BF16)
    qkv = p.sb("qkv", [128, NSUB * 1024], BF16)
    term = v_fm
    qkvhe = fk
    y_fm = fq
    strip = p.strip_bufs()
    strips = p.stream("strips", 12)

    ps = [p.psum(f"ps{i}", [128, 512], F32) for i in range(4)]
    ps_g = [p.psum(f"psg{i}", [16, 512], F32) for i in range(2)]
    ps_t = [p.psum(f"ps_t{i}", [128, 1024], BF16) for i in range(2)]

    for i in range(ntiles):
        r0 = i * TT
        with nc.Block() as block:
            st0 = p.val("st")

            @block.sync
            def _(sync):
                sync.wait_ge(sst, st0)
                sync.dma_start(out=tok_sb(xbuf[:], NSUB, D),
                               in_=tok_in(x_h, r0)).then_inc(sld, 16)

            p.bump("ld", 16)
            emit_ln(p, block, xbuf, qx, stats, eps,
                    wait_sem=sld, wait_val=p.val("ld"), done_inc=(sdve, 1))
            qx_done = p.bump("dve")
            emit_ln(p, block, qx, xn2, stats, eps,
                    wait_sem=sdve, wait_val=qx_done)

        with nc.Block() as block:
            emit_tok2fm(p, block, xn2, xn_fm, ident, ps_t, nchunks=8,
                        affine=(par["ln1w"], par["ln1b"], 0))

        with nc.Block() as block:
            pe0 = p.val("pe")
            bank0 = p.val("bank")
            act0 = p.val("act")
            dve0 = p.val("dve")
            mx0 = p.val("mx")

            @block.tensor
            def _(tensor):
                g = 0
                for wname in ["wq", "wk", "wv"]:
                    wt = wsb[wname]
                    for m in range(8):
                        if g >= 4:
                            tensor.wait_ge(sbank, bank0 + g - 3)
                        mm = None
                        for k in range(8):
                            mm = tensor.matmul(
                                ps[g % 4][:],
                                wt[:, k * D + m * 128:k * D + (m + 1) * 128],
                                xn_fm[:, k * TT:(k + 1) * TT],
                                start=(k == 0), stop=(k == 7))
                        mm.then_inc(spe, 1)
                        g += 1

            @block.vector
            def _(vector):
                g = 0
                nact = 0
                nmx = 0
                for pi, (wname, dst) in enumerate([("bq", fq), ("bk", fk)]):
                    for m in range(8):
                        vector.wait_ge(spe, pe0 + g + 1)
                        if g >= 1:
                            vector.wait_ge(sact, act0 + nact)
                        vector.tensor_scalar(tmin[:, 0:TT],
                                             ps[g % 4][:], par[wname][:, m:m + 1],
                                             0.0, OP.add, OP.min).then_inc(sdve, 1)
                        vector.tensor_scalar(dst[:, m * TT:(m + 1) * TT],
                                             ps[g % 4][:], par[wname][:, m:m + 1],
                                             0.0, OP.add, OP.max).then_inc(sbank, 1)
                        vector.wait_ge(sact, act0 + nact + 1)
                        vector.wait_ge(sbank, bank0 + g + 1)
                        vector.tensor_tensor(dst[:, m * TT:(m + 1) * TT],
                                             dst[:, m * TT:(m + 1) * TT],
                                             texp[:, 0:TT],
                                             OP.add).then_inc(smx, 1)
                        nact += 1
                        nmx += 1
                        g += 1
                for m in range(8):
                    vector.wait_ge(spe, pe0 + g + 1)
                    vector.tensor_scalar(v_fm[:, m * TT:(m + 1) * TT], ps[g % 4][:],
                                         par["bv"][:, m:m + 1], None,
                                         OP.add).then_inc(sbank, 1)
                    g += 1

            @block.scalar
            def _(scalar):
                for g in range(16):
                    scalar.wait_ge(sdve, dve0 + g + 1)
                    if g >= 1:
                        scalar.wait_ge(smx, mx0 + g)
                    scalar.activation(texp[:, 0:TT], tmin[:, 0:TT],
                                      AF.Exp).then_inc(sact, 1)

            p.bump("pe", 24)
            p.bump("bank", 24)
            p.bump("dve", 16)
            p.bump("act", 16)
            p.bump("mx", 16)

        with nc.Block() as block:
            pe0, dve0 = p.val("pe"), p.val("dve")

            @block.tensor
            def _(tensor):
                t = 0
                for m in range(8):
                    for s in range(NSUB):
                        if t >= 2:
                            tensor.wait_ge(sdve, dve0 + 2 * (t - 1))
                        tensor.transpose(ps_t[t % 2][:, 0:128],
                                         v_fm[:, m * TT + s * 128:m * TT + (s + 1) * 128],
                                         ident[:]).then_inc(spe, 1)
                        t += 1

            @block.scalar
            def _(scalar):
                t = 0
                for m in range(8):
                    for s in range(NSUB):
                        scalar.wait_ge(spe, pe0 + t + 1)
                        src = bass.AP(ps_t[t % 2], 0, [[1024, 128], [64, 2], [1, 64]])
                        for dup in range(2):
                            dst = bass.AP(veh, s * 2048 + 2 * m + 16 * dup,
                                          [[NSUB * 2048, 128], [1, 2], [32, 64]])
                            scalar.copy(dst, src).then_inc(sdve, 1)
                        t += 1

            p.bump("pe", 32)
            p.bump("dve", 64)

        with nc.Block() as block:
            pe0, dve0, act0 = p.val("pe"), p.val("dve"), p.val("act")

            @block.vector
            def _(vector):
                vector.tensor_copy(fk[:, 8 * TT:16 * TT], fk[:, 0:8 * TT])
                for c in range(8):
                    c3 = (c + 1) % 8
                    vector.tensor_copy(fkd[0:64, c * TT:(c + 1) * TT],
                                       fk[64:128, c * TT:(c + 1) * TT])
                    vector.tensor_copy(fkd[64:128, c * TT:(c + 1) * TT],
                                       fk[0:64, c3 * TT:(c3 + 1) * TT])
                vector.drain()
                vector.tensor_copy(fkd[:, 8 * TT:15 * TT], fkd[:, 0:7 * TT])
                vector.drain()
                for o in range(16):
                    if o >= 2:
                        vector.wait_ge(spe, pe0 + o - 1)
                    src = fk if o % 2 == 0 else fkd
                    off = (o // 2) * TT
                    vector.tensor_tensor(Pb[o % 2][:], fq[:, 0:8 * TT],
                                         src[:, off:off + 8 * TT],
                                         OP.mult).then_inc(sdve, 1)

            @block.tensor
            def _(tensor):
                for o in range(16):
                    tensor.wait_ge(sdve, dve0 + o + 1)
                    if o >= 2:
                        tensor.wait_ge(sact, act0 + o - 1)
                    mm = None
                    for c in range(8):
                        mm = tensor.matmul(ps_g[o % 2][:],
                                           bones[:, c * 16:(c + 1) * 16],
                                           Pb[o % 2][:, c * TT:(c + 1) * TT],
                                           start=(c == 0), stop=(c == 7))
                    mm.then_inc(spe, 1)

            @block.scalar
            def _(scalar):
                for o in range(16):
                    scalar.wait_ge(spe, pe0 + o + 1)
                    scalar.copy(g_sb[:, o * TT:(o + 1) * TT],
                                ps_g[o % 2][:]).then_inc(sact, 1)

            p.bump("dve", 16)
            p.bump("pe", 16)
            p.bump("act", 16)

        with nc.Block() as block:
            pe0 = p.val("pe")

            @block.tensor
            def _(tensor):
                mm = None
                for o in range(16):
                    for s in range(NSUB):
                        mm = tensor.transpose(
                            ps_t[0][:, s * 256 + o * 16:s * 256 + (o + 1) * 16],
                            g_sb[:, o * TT + s * 128:o * TT + (s + 1) * 128],
                            ident[0:16, 0:16])
                mm.then_inc(spe, 1)

            @block.vector
            def _(vector):
                vector.wait_ge(spe, pe0 + 1)
                vector.tensor_copy(G_t[:], ps_t[0][:])
                vector.drain()
                for s in range(NSUB):
                    red_in = bass.AP(G_t, s * 256,
                                     [[NSUB * 256, 128], [1, 16], [16, 16]])
                    vector.tensor_reduce(qk[:, s * 16:(s + 1) * 16], red_in,
                                         axis=AX.X, op=OP.add)
                vector.drain()
                vector.tensor_scalar(qk[:], qk[:], 1e-6, None, OP.add)
                vector.drain()
                vector.reciprocal(rcp[:], qk[:])
                vector.drain()
                rcp_ap = bass.AP(rcp, 0, [[NSUB * 16, 128], [16, NSUB],
                                          [0, 16], [1, 16]])
                vector.tensor_tensor(Gn[:], G_t[:], rcp_ap, OP.mult)

            p.bump("pe", 1)

        with nc.Block() as block:
            @block.vector
            def _(vector):
                svd = p.s("vd")
                for o in range(16):
                    g_ap = bass.AP(Gn, o * 16, [[NSUB * 256, 128], [256, NSUB],
                                                [0, 64], [1, 16]])
                    v_ap = bass.AP(veh, o, [[NSUB * 2048, 128], [2048, NSUB],
                                            [32, 64], [1, 16]])
                    if o == 0:
                        vector.tensor_tensor(qkv[:], g_ap, v_ap,
                                             OP.mult).then_inc(svd, 1)
                        p.bump("vd")
                    else:
                        vector.wait_ge(svd, p.val("vd"))
                        vector.tensor_tensor(term[:], g_ap, v_ap,
                                             OP.mult).then_inc(svd, 1)
                        p.bump("vd")
                        vector.wait_ge(svd, p.val("vd"))
                        if o < 15:
                            vector.tensor_tensor(qkv[:], qkv[:], term[:],
                                                 OP.add).then_inc(svd, 1)
                        else:
                            out_ap = bass.AP(qkvhe, 0,
                                             [[qkvhe.shape[1], 128], [1024, NSUB],
                                              [1, 64], [64, 16]])
                            vector.tensor_tensor(out_ap, qkv[:], term[:],
                                                 OP.add).then_inc(svd, 1)
                        p.bump("vd")

        with nc.Block() as block:
            emit_tok2fm(p, block, qkvhe, xn_fm, ident, ps_t, nchunks=8)

        with nc.Block() as block:
            evacs = []
            for m in range(8):
                def f(eng, pst, m=m):
                    return eng.tensor_scalar(y_fm[:, m * TT:(m + 1) * TT], pst,
                                             par["bo"][:, m:m + 1], None, OP.add)
                evacs.append(f)
            emit_mm_stream(p, block, wo_s, xn_fm, 8, 8, strip, strips, ps,
                           "vector", evacs)

        with nc.Block() as block:
            pe0, dve0, st0 = p.val("pe"), p.val("dve"), p.val("st")

            @block.tensor
            def _(tensor):
                for s in range(NSUB):
                    if s >= 2:
                        tensor.wait_ge(sdve, dve0 + s - 1)
                    mm = None
                    for c in range(8):
                        mm = tensor.transpose(
                            ps_t[s % 2][:, c * 128:(c + 1) * 128],
                            y_fm[:, c * TT + s * 128:c * TT + (s + 1) * 128],
                            ident[:])
                    mm.then_inc(spe, 1)

            @block.vector
            def _(vector):
                vector.wait_ge(sst, st0)
                for s in range(NSUB):
                    vector.wait_ge(spe, pe0 + s + 1)
                    vector.tensor_tensor(xbuf[:, s * D:(s + 1) * D],
                                         qx[:, s * D:(s + 1) * D],
                                         ps_t[s % 2][:, 0:D],
                                         OP.add).then_inc(sdve, 1)

            @block.sync
            def _(sync):
                sync.wait_ge(sdve, dve0 + NSUB)
                sync.dma_start(out=tok_in(xa_h, r0),
                               in_=tok_sb(xbuf[:], NSUB, D)).then_inc(sst, 16)

            p.bump("pe", NSUB)
            p.bump("dve", NSUB)
            p.bump("st", 16)

    with nc.Block() as block:
        @block.sync
        def _(sync):
            sync.wait_ge(sst, p.val("st"))

    p.pop_scope()
    if standalone:
        return p.close()


def la_inmaps(x_full, ins):
    bones = np.zeros((128, 128), dtype=BF)
    for c in range(8):
        for r in range(128):
            bones[r, c * 16 + 2 * c + r // 64] = 1
    com = {
        "wq": np.ascontiguousarray(ins["L0_attn_wq"]),
        "wk": np.ascontiguousarray(ins["L0_attn_wk"]),
        "wv": np.ascontiguousarray(ins["L0_attn_wv"]),
        "wo": np.ascontiguousarray(ins["L0_attn_wo"]),
        "bq": vec_pm(ins["L0_attn_bq"]),
        "bk": vec_pm(ins["L0_attn_bk"]),
        "bv": vec_pm(ins["L0_attn_bv"]),
        "bo": vec_pm(ins["L0_attn_bo"]),
        "ln1w": vec_pm(ins["L0_attn_ln1_w"]),
        "ln1b": vec_pm(ins["L0_attn_ln1_b"]),
        "ident": eye_bf16(),
        "bones": bones,
    }
    return [dict(com, x=np.ascontiguousarray(x_full[c])) for c in range(NCORES)]


def build_fused():
    p = Prog()
    nc = p.nc
    p.strip_bufs()
    E = lambda n, sh, dt: p.dram(n, sh, dt, "ExternalInput")
    I = lambda n, sh, dt: p.dram(n, sh, dt)

    x = E("x", [N, D], F32)
    out = p.dram("out", [N, D], F32, "ExternalOutput")
    ident = E("ident", [128, 128], BF16)
    identf = E("identf", [128, 128], F32)

    ew = {}
    for n, sh in [("wq", [D, D]), ("wk", [D, D]), ("wv", [D, D]), ("wo", [D, D]),
                  ("bq", [128, 8]), ("bk", [128, 8]), ("bv", [128, 8]),
                  ("bo", [128, 8]), ("ln1w", [128, 8]), ("ln1b", [128, 8]),
                  ("fcw", [D, 4096]), ("fcb", [128, 32]), ("pjw", [4096, D]),
                  ("pjb", [128, 8]), ("lnw", [128, 8]), ("lnb", [128, 8]),
                  ("w1t", [D, 4096]), ("w2t", [4096, D]),
                  ("convw", [128, 96]), ("convb", [128, 32]),
                  ("ipw", [D, 4352]), ("ipwdt", [D, 32]), ("dtb", [32, 1]),
                  ("convw2", [128, 72]), ("convb2", [128, 18]),
                  ("alog", [1, 32]), ("dvec", [1, 32]),
                  ("tri", [128, 128]), ("tris", [128, 128]),
                  ("nw", [1, 2048]), ("opw", [2048, D]),
                  ("clipx", [1, 1]), ("wg", [2816, D]), ("wu", [2816, D]),
                  ("clipd", [1, 1]), ("wd", [D, 2816])]:
        ew[n] = E(n, sh, F32)
    bones = E("bones", [128, 128], BF16)

    xa = I("i_xa", [N, D], F32)
    x1 = I("i_x1", [N, D], F32)
    hs = I("i_hs", [4096, N], BF16)
    x2 = I("i_x2", [N, D], F32)
    zs = I("i_zs", [2048, N], BF16)
    xbcs = I("i_xbcs", [2304, N], BF16)
    dts = I("i_dts", [N, 32], F32)
    xpt = I("i_xpt", [N, 2048], BF16)
    bts = I("i_bts", [N, 128], BF16)
    bfs = I("i_bfs", [128, N], BF16)
    cfs = I("i_cfs", [128, N], BF16)
    ys = I("i_ys", [N, 2048], F32)
    x3 = I("i_x3", [N, D], F32)
    xln = I("i_xln", [N, D], F32)
    mloc = I("i_mloc", [1, 1], F32)
    mlg = I("i_mlg", [1, 1], F32)
    hs2 = I("i_hs2", [2816, N], BF16)
    m2loc = I("i_m2loc", [1, 1], F32)
    m2g = I("i_m2g", [1, 1], F32)

    com = {"ident": ident, "identf": identf}

    build_la(p=p, H=dict(com, x=x, xa=xa, bones=bones,
                         **{k: ew[k] for k in ["wq", "wk", "wv", "wo", "bq",
                                               "bk", "bv", "bo", "ln1w", "ln1b"]}))
    build_lb(p=p, H=dict(com, xa=xa, xo=x, y=x1,
                         **{k: ew[k] for k in ["fcw", "fcb", "pjw", "pjb",
                                               "lnw", "lnb"]}))
    build_lc1(p=p, H=dict(com, x1=x1, hs=hs, w1t=ew["w1t"]))
    build_lc2(p=p, H=dict(com, hs=hs, x1=x1, x2=x2, w2t=ew["w2t"],
                          convw=ew["convw"], convb=ew["convb"]))
    build_ld(p=p, H=dict(com, x2=x2, zs=zs, xbcs=xbcs, dts=dts,
                         ipw=ew["ipw"], ipwdt=ew["ipwdt"], dtb=ew["dtb"]))
    build_ld2(p=p, H=dict(com, xbcs=xbcs, xpt=xpt, bfs=bfs, cfs=cfs, bts=bts,
                          convw=ew["convw2"], convb=ew["convb2"]))
    build_le(p=p, H=dict(com, xpt=xpt, bts=bts, bfs=bfs, cfs=cfs, dts=dts,
                         ys=ys, alog=ew["alog"], dvec=ew["dvec"],
                         tri=ew["tri"], tris=ew["tris"]))
    build_lf(p=p, H=dict(com, ys=ys, zs=zs, x2=x2, x3=x3, xln=xln, mloc=mloc,
                         nw=ew["nw"], opw=ew["opw"]))

    ccsem = p.sem("cc")
    with nc.Block() as block:
        @block.gpsimd
        def _(gpsimd):
            gpsimd.collective_compute(
                "AllReduce", OP.max, replica_groups=[list(range(NCORES))],
                ins=[mloc.opt()], outs=[mlg.opt()]).then_inc(ccsem, 1)
            gpsimd.wait_ge(ccsem, p.bump("cc"))

    build_lh(p=p, H=dict(com, xln=xln, mall=mlg, hs2=hs2, m2loc=m2loc,
                         clipx=ew["clipx"], wg=ew["wg"], wu=ew["wu"]),
             mall_w=1)

    with nc.Block() as block:
        @block.gpsimd
        def _(gpsimd):
            gpsimd.collective_compute(
                "AllReduce", OP.max, replica_groups=[list(range(NCORES))],
                ins=[m2loc.opt()], outs=[m2g.opt()]).then_inc(ccsem, 1)
            gpsimd.wait_ge(ccsem, p.bump("cc"))

    build_li(p=p, H=dict(com, hs2=hs2, m2all=m2g, x3=x3, out=out,
                         clipd=ew["clipd"], wd=ew["wd"]),
             mall_w=1)

    return p.close()


def fused_inmaps(ins):
    x = np.ascontiguousarray(np.asarray(ins["L0_attn_q_x"], np.float32))
    bones = np.zeros((128, 128), dtype=BF)
    for c in range(8):
        for r in range(128):
            bones[r, c * 16 + 2 * c + r // 64] = 1
    r = np.arange(128)
    ipw = np.asarray(ins["L1_attn_in_proj_w"])
    cwr = np.asarray(ins["L1_attn_conv_w"])[::-1]
    com = {
        "ident": eye_bf16(), "identf": np.eye(128, dtype=np.float32),
        "bones": bones,
        "wq": np.ascontiguousarray(ins["L0_attn_wq"]),
        "wk": np.ascontiguousarray(ins["L0_attn_wk"]),
        "wv": np.ascontiguousarray(ins["L0_attn_wv"]),
        "wo": np.ascontiguousarray(ins["L0_attn_wo"]),
        "bq": vec_pm(ins["L0_attn_bq"]), "bk": vec_pm(ins["L0_attn_bk"]),
        "bv": vec_pm(ins["L0_attn_bv"]), "bo": vec_pm(ins["L0_attn_bo"]),
        "ln1w": vec_pm(ins["L0_attn_ln1_w"]), "ln1b": vec_pm(ins["L0_attn_ln1_b"]),
        "fcw": np.ascontiguousarray(ins["L0_attn_fc_w"]),
        "fcb": vec_pm(ins["L0_attn_fc_b"]),
        "pjw": np.ascontiguousarray(ins["L0_attn_proj_w"]),
        "pjb": vec_pm(ins["L0_attn_proj_b"]),
        "lnw": vec_pm(ins["L0_attn_ln2_w"]), "lnb": vec_pm(ins["L0_attn_ln2_b"]),
        "w1t": np.ascontiguousarray(ins["L0_ffn_w1"].T),
        "w2t": np.ascontiguousarray(ins["L0_ffn_w2"].T),
        "convw": np.ascontiguousarray(
            np.asarray(ins["L0_ffn_conv_w"])[:, 0, :].reshape(32, 128, 3)
            .transpose(1, 0, 2).reshape(128, 96)),
        "convb": vec_pm(ins["L0_ffn_conv_b"]),
        "ipw": np.ascontiguousarray(ipw[:, 0:4352]),
        "ipwdt": np.ascontiguousarray(ipw[:, 4352:4384]),
        "dtb": np.ascontiguousarray(np.asarray(ins["L1_attn_dt_bias"]).reshape(32, 1)),
        "convw2": np.ascontiguousarray(cwr.T.reshape(18, 128, 4)
                                       .transpose(1, 0, 2).reshape(128, 72)),
        "convb2": vec_pm(ins["L1_attn_conv_b"]),
        "alog": np.ascontiguousarray(np.asarray(ins["L1_attn_A_log"]).reshape(1, 32)),
        "dvec": np.ascontiguousarray(np.asarray(ins["L1_attn_D"]).reshape(1, 32)),
        "tri": (r[:, None] <= r[None, :]).astype(np.float32),
        "tris": (r[:, None] > r[None, :]).astype(np.float32),
        "nw": np.ascontiguousarray(np.asarray(ins["L1_attn_norm_w"]).reshape(1, 2048)),
        "opw": np.ascontiguousarray(ins["L1_attn_out_proj_w"]),
        "clipx": np.ascontiguousarray(np.asarray(ins["L1_ffn_clip_x"],
                                                 np.float32).reshape(1, 1)),
        "wg": np.ascontiguousarray(ins["L1_ffn_w_gate"]),
        "wu": np.ascontiguousarray(ins["L1_ffn_w_up"]),
        "clipd": np.ascontiguousarray(np.asarray(ins["L1_ffn_clip_down"],
                                                 np.float32).reshape(1, 1)),
        "wd": np.ascontiguousarray(ins["L1_ffn_w_down"]),
    }
    return [dict(com, x=np.ascontiguousarray(x[c])) for c in range(NCORES)]


def _run(nc, in_maps):
    res = run_bass_kernel_spmd(nc, in_maps, core_ids=list(range(NCORES)))
    return res.results


def kernel(**ins):
    nc = build_fused()
    res = _run(nc, fused_inmaps(ins))
    return np.stack([np.asarray(c["out"], np.float32) for c in res], axis=0)


def kernel_unfused(**ins):
    x = np.ascontiguousarray(np.asarray(ins["L0_attn_q_x"], np.float32))

    r = _run(build_la(), la_inmaps(x, ins))
    xa = [c["xa"] for c in r]

    r = _run(build_lb(), lb_inmaps(xa, x, ins))
    x1 = [c["y"] for c in r]

    r = _run(build_lc1(), lc1_inmaps(x1, ins))
    hs = [c["hs"] for c in r]

    r = _run(build_lc2(), lc2_inmaps(hs, x1, ins))
    x2 = [c["x2"] for c in r]

    r = _run(build_ld(), ld_inmaps(x2, ins))
    zs = [c["zs"] for c in r]
    xbcs = [c["xbcs"] for c in r]
    dts = [c["dts"] for c in r]

    r = _run(build_ld2(), ld2_inmaps(xbcs, ins))
    xpt = [c["xpt"] for c in r]
    bts = [c["bts"] for c in r]
    bfs = [c["bfs"] for c in r]
    cfs = [c["cfs"] for c in r]

    r = _run(build_le(), le_inmaps(xpt, bts, bfs, cfs, dts, ins))
    ys = [c["ys"] for c in r]

    r = _run(build_lf(), lf_inmaps(ys, zs, x2, ins))
    x3 = [c["x3"] for c in r]
    xln = [c["xln"] for c in r]
    mall = [np.asarray(c["mloc"]).ravel()[0] for c in r]

    r = _run(build_lh(), lh_inmaps(xln, mall, ins))
    hs2 = [c["hs2"] for c in r]
    m2all = [np.asarray(c["m2loc"]).ravel()[0] for c in r]

    r = _run(build_li(), li_inmaps(hs2, m2all, x3, ins))
    return np.stack([np.asarray(c["out"], np.float32) for c in r], axis=0)
